# version 01
# baseline (speedup 1.0000x reference)
import numpy as np
import ml_dtypes
import concourse.bass as bass
import concourse.mybir as mybir
from concourse.bass_utils import run_bass_kernel_spmd
from contextlib import ExitStack

F32 = mybir.dt.float32
BF16 = mybir.dt.bfloat16
AF = mybir.ActivationFunctionType
ALU = mybir.AluOpType
BFNP = ml_dtypes.bfloat16

NCORES = 8
L = 2048
D = 1024
HALO = 2049
NQ = HALO + L
NK = 8192
CAUS = NK - NQ
QT_A = [(0, 1)] + [(1 + 512 * i, 512) for i in range(8)]
TB = 3074
KVB = 4096
RING = 1536
HID = 2730
NBLK = 21
TAIL = HID - NBLK * 128
DEN_EPS = 1e-30


def _bf(a):
    return np.ascontiguousarray(np.asarray(a)).astype(BFNP)


def _f32(a):
    return np.ascontiguousarray(np.asarray(a), dtype=np.float32)


def _slice_pad(xt, lo, hi):
    n = hi - lo
    out = np.zeros((xt.shape[0], n), dtype=np.float32)
    s = max(lo, 0)
    if hi > s:
        out[:, s - lo:] = xt[:, s:hi]
    return np.ascontiguousarray(out)


def _perm_w1_cols(w1):
    a, g = w1[:, :HID], w1[:, HID:]
    blocks = []
    for k in range(NBLK):
        blocks += [a[:, k * 128:(k + 1) * 128], g[:, k * 128:(k + 1) * 128]]
    blocks += [a[:, NBLK * 128:], g[:, NBLK * 128:]]
    return np.concatenate(blocks, axis=1)


def _perm_b1(b1):
    a, g = b1[:HID], b1[HID:]
    parts = []
    for k in range(NBLK):
        parts += [a[k * 128:(k + 1) * 128], g[k * 128:(k + 1) * 128]]
    parts += [a[NBLK * 128:], g[NBLK * 128:]]
    return np.concatenate(parts)


def _rot_tables():
    d = np.arange(64)
    half = (d % 32) * 2.0
    inv_freq = 1.0 / 10000.0 ** (half / 64.0)
    t = np.arange(NK, dtype=np.float64)
    fr = t[None, :] * inv_freq[:, None]
    base = (half + 0.4 * 64.0) / (1.4 * 64.0)
    power = (t - NK // 2) / 256.0
    scale = base[:, None] ** power[None, :]
    cos, sin = np.cos(fr), np.sin(fr)
    return [_f32(np.tile(m, (2, 1))) for m in
            (cos * scale, sin * scale, cos / scale, sin / scale)]


def host_prepare(inputs):
    x0t = np.ascontiguousarray(
        np.asarray(inputs["L0_attn_x"], dtype=np.float32).transpose(0, 2, 1))
    wq = np.asarray(inputs["L0_attn_wq"], dtype=np.float32)
    wk = np.asarray(inputs["L0_attn_wk"], dtype=np.float32)
    wv = np.asarray(inputs["L0_attn_wv"], dtype=np.float32)
    wo = np.asarray(inputs["L0_attn_wo"], dtype=np.float32)

    shared = {
        "wqT": _bf(wq.transpose(2, 0, 1).reshape(1024, 1024)),
        "wkT": _bf(wk.transpose(2, 0, 1).reshape(1024, 1024)),
        "wvT": _bf(wv.transpose(2, 0, 1).reshape(1024, 1024)),
        "woT": _bf(wo.T),
        "IDTF": _f32(np.eye(128)),
        "ONE1": _bf(np.ones((1, 128))),
        "ONE1F": _f32(np.ones((1, 128))),
    }
    P = np.zeros((128, 128), dtype=np.float32)
    for dd in range(128):
        sig = dd + 32 if (dd % 64) < 32 else dd - 32
        P[sig, dd] = 1.0
    shared["PERMT"] = _bf(P)
    S = np.zeros((8, 128, 16), dtype=np.float32)
    for c in range(8):
        S[c, 0:64, 2 * c] = 1.0
        S[c, 64:128, 2 * c + 1] = 1.0
    shared["SELPA"] = _bf(S)
    S2 = np.zeros((4, 128, 8), dtype=np.float32)
    for c in range(4):
        S2[c, 0:64, 2 * c] = 1.0
        S2[c, 64:128, 2 * c + 1] = 1.0
    shared["SELB"] = _bf(S2)
    S3 = np.zeros((4, 8, 128), dtype=np.float32)
    for c in range(4):
        S3[c, 2 * c, 0:64] = 1.0
        S3[c, 2 * c + 1, 64:128] = 1.0
    shared["SELBR"] = _f32(S3)

    tq_c, tq_s, tk_c, tk_s = _rot_tables()

    for l in range(2):
        w1 = np.asarray(inputs["L0_ffn_w1"][l], np.float32)
        w2 = np.asarray(inputs["L0_ffn_w2"][l], np.float32)
        shared[f"wqkv{l}"] = _bf(inputs["L0_ffn_wqkv"][l])
        shared[f"woB{l}"] = _bf(inputs["L0_ffn_wo"][l])
        shared[f"w1p{l}"] = _bf(_perm_w1_cols(w1))
        shared[f"b1p{l}"] = _f32(_perm_b1(np.asarray(inputs["L0_ffn_b1"][l], np.float32)))
        w2pad = np.concatenate(
            [w2[k * 128:(k + 1) * 128] for k in range(NBLK)]
            + [w2[NBLK * 128:], np.zeros((128 - TAIL, 1024), np.float32)], axis=0)
        shared[f"w2p{l}"] = _bf(w2pad)
        shared[f"b2_{l}"] = _f32(inputs["L0_ffn_b2"][l])
        shared[f"lg{l}"] = _f32(inputs["L0_ffn_ln_g"][l])
        shared[f"lb{l}"] = _f32(inputs["L0_ffn_ln_b"][l])
        shared[f"qs{l}"] = _f32(inputs["L0_ffn_q_scale"][l])
        shared[f"ks{l}"] = _f32(inputs["L0_ffn_k_scale"][l])

    per_core = []
    for c in range(NCORES):
        b, j = c // 4, c % 4
        S0, E0 = j * L, (j + 1) * L
        m = dict(shared)
        m["xq"] = _slice_pad(x0t[b], E0 - NQ, E0)
        m["xk"] = _slice_pad(x0t[b], E0 - NK, E0)
        oa = np.zeros((NK, 8), dtype=np.float32)
        oa[max(NK - E0, 0):, :] = 1.0
        m["onesA"] = _bf(oa)
        ob = np.zeros((KVB, 8), dtype=np.float32)
        ob[max(2048 - S0, 0):, :] = 1.0
        m["onesB"] = _bf(ob)
        vm = np.zeros((1, TB), dtype=np.float32)
        vm[0, max(1026 - S0, 0):] = 1.0
        m["vmaskB"] = _bf(vm)

        def tsl(tab):
            lo2 = S0 - 2048
            o = np.zeros((128, KVB), dtype=np.float32)
            s = max(lo2, 0)
            o[:, s - lo2:] = tab[:, s:E0]
            return _f32(o)
        m["TQC"], m["TQS"], m["TKC"], m["TKS"] = (tsl(t) for t in (tq_c, tq_s, tk_c, tk_s))
        per_core.append(m)
    return per_core


class Dev:
    ENGS = ("te", "act", "dve", "pool", "sync")
    DEF_CTR = {"te": "PE", "act": "ACT", "dve": "DVE", "pool": "POOL", "sync": "LS"}
    CTRS = ("LD", "LS", "PE", "ACT", "DVE", "POOL", "SP")

    def __init__(self):
        self.q = {e: [] for e in self.ENGS}
        self.cnt = {k: 0 for k in self.CTRS}
        self.val = {}
        self.paranoid = False

    def op(self, eng, fn, waits=(), inc=None, ctr=None):
        if self.paranoid:
            waits = list(waits) + self.barrier()
        if ctr is None:
            ctr = self.DEF_CTR[eng]
        amt = 16 if ctr in ("LD", "LS", "SP") else 1
        self.cnt[ctr] += amt
        if inc is not None:
            assert inc not in self.val, f"dup label {inc}"
            self.val[inc] = (ctr, self.cnt[ctr])
        self.q[eng].append({"fn": fn, "waits": list(waits), "ctr": ctr, "amt": amt})

    def mark(self, label, ctr):
        assert label not in self.val
        self.val[label] = (ctr, self.cnt[ctr])

    def barrier(self):
        return [(k, self.cnt[k]) for k in self.CTRS]

    def phase_barrier(self):
        b = self.barrier()
        for e in self.ENGS:
            self.q[e].append({"fn": None, "waits": b})

    def replay(self, blk, sems):
        def resolve(w):
            return w if isinstance(w, tuple) else self.val[w]

        def run(e, name):
            for rec in self.q[name]:
                for w in rec["waits"]:
                    ctr, v = resolve(w)
                    if v > 0:
                        e.wait_ge(sems[ctr], v)
                if rec["fn"] is not None:
                    rec["fn"](e).then_inc(sems[rec["ctr"]], rec["amt"])
        blk.tensor(lambda e: run(e, "te"))
        blk.scalar(lambda e: run(e, "act"))
        blk.vector(lambda e: run(e, "dve"))
        blk.gpsimd(lambda e: run(e, "pool"))
        blk.sync(lambda e: run(e, "sync"))


class Arena:
    def __init__(self, nc, ctx, name, dtype, cols):
        self.t = ctx.enter_context(nc.sbuf_tensor(name, [128, cols], dtype))
        self.cols = cols
        self.off = 0
        self.stack = []
        self.hi = 0

    def push(self):
        self.stack.append(self.off)

    def pop(self):
        self.off = self.stack.pop()

    def alloc(self, shape):
        p = shape[0]
        n = 1
        for s in shape[1:]:
            n *= s
        off = (self.off + 15) & ~15
        self.off = off + n
        self.hi = max(self.hi, self.off)
        assert self.off <= self.cols, f"arena overflow {self.off} > {self.cols}"
        v = self.t[0:p, off:off + n]
        if len(shape) == 3:
            v = v.rearrange("p (a b) -> p a b", a=shape[1])
        elif len(shape) == 4:
            v = v.rearrange("p (a b c) -> p a b c", a=shape[1], b=shape[2])
        elif len(shape) == 5:
            v = v.rearrange("p (a b c dd) -> p a b c dd",
                            a=shape[1], b=shape[2], c=shape[3])
        else:
            assert len(shape) == 2
        return v


class Builder:
    def __init__(self, upto="p7", debug=()):
        self.upto = upto
        self.debug = set(debug)
        self.nc = bass.Bass("TRN2", target_bir_lowering=False, debug=False)
        self.d = Dev()

    def _decl(self):
        nc = self.nc

        def di(name, shape, dt):
            return nc.dram_tensor(name, shape, dt, kind="ExternalInput").ap()
        self.xq = di("xq", [D, NQ], F32)
        self.xk = di("xk", [D, NK], F32)
        self.wqT = di("wqT", [D, D], BF16)
        self.wkT = di("wkT", [D, D], BF16)
        self.wvT = di("wvT", [D, D], BF16)
        self.woT = di("woT", [D, D], BF16)
        self.onesA = di("onesA", [NK, 8], BF16)
        self.onesB = di("onesB", [KVB, 8], BF16)
        self.vmaskB = di("vmaskB", [1, TB], BF16)
        self.IDTF = di("IDTF", [128, 128], F32)
        self.ONE1 = di("ONE1", [1, 128], BF16)
        self.ONE1F = di("ONE1F", [1, 128], F32)
        self.PERMT = di("PERMT", [128, 128], BF16)
        self.SELPA = di("SELPA", [8, 128, 16], BF16)
        self.SELB = di("SELB", [4, 128, 8], BF16)
        self.SELBR = di("SELBR", [4, 8, 128], F32)
        self.TQC = di("TQC", [128, KVB], F32)
        self.TQS = di("TQS", [128, KVB], F32)
        self.TKC = di("TKC", [128, KVB], F32)
        self.TKS = di("TKS", [128, KVB], F32)
        self.W = {}
        for l in range(2):
            self.W[l] = dict(
                wqkv=di(f"wqkv{l}", [D, 1536], BF16),
                woB=di(f"woB{l}", [512, D], BF16),
                w1p=di(f"w1p{l}", [D, 2 * HID], BF16),
                b1p=di(f"b1p{l}", [2 * HID], F32),
                w2p=di(f"w2p{l}", [NBLK * 128 + 128, D], BF16),
                b2=di(f"b2_{l}", [D], F32),
                lg=di(f"lg{l}", [D], F32),
                lb=di(f"lb{l}", [D], F32),
                qs=di(f"qs{l}", [64], F32),
                ks=di(f"ks{l}", [64], F32),
            )
        self.out = nc.dram_tensor("out", [D, L], F32, kind="ExternalOutput").ap()

        def scr(name, shape, dt):
            kind = "ExternalOutput" if name in self.debug else "Internal"
            return nc.dram_tensor(name, shape, dt, kind=kind).ap()
        self.kTsp = scr("kTsp", [D, NK], BF16)
        self.Vsp = scr("Vsp", [NK, D], BF16)
        self.qTsp = scr("qTsp", [D, NQ], BF16)
        self.ysp = scr("ysp", [D, NQ], F32)
        self.uA0sp = scr("uA0sp", [D, TB], F32)
        self.u1sp = scr("u1sp", [D, TB], F32)
        self.uA1sp = scr("uA1sp", [D, TB], F32)
        self.u2sp = scr("u2sp", [D, TB], F32)
        self.bounceA = scr("bounceA", [16, 9], BF16)
        self.bounceB = scr("bounceB", [2, 48, 8], BF16)

    def ld(self, fn, waits=(), inc=None):
        self.d.op("pool", fn, waits, inc, ctr="LD")

    def ls(self, fn, waits=(), inc=None):
        self.d.op("sync", fn, waits, inc, ctr="LS")

    def sp(self, fn, waits=(), inc=None):
        self.d.op("sync", fn, waits, inc, ctr="SP")

    def build(self):
        self._decl()
        nc, d = self.nc, self.d
        with ExitStack() as top:
            E = top.enter_context
            self.abf = Arena(nc, top, "abf", BF16, 69 * 1024)
            self.af = Arena(nc, top, "af", F32, 16 * 1024)
            self.ps = E(nc.psum_tensor("ps", [128, 4096], F32))

            self.idtf = E(nc.sbuf_tensor("idtf", [128, 128], F32))
            self.one1 = E(nc.sbuf_tensor("one1", [1, 128], BF16))
            self.one1f = E(nc.sbuf_tensor("one1f", [1, 128], F32))
            self.permt = E(nc.sbuf_tensor("permt", [128, 128], BF16))
            self.selpa = E(nc.sbuf_tensor("selpa", [128, 8, 16], BF16))
            self.selb = E(nc.sbuf_tensor("selb", [128, 4, 8], BF16))
            self.selbr = E(nc.sbuf_tensor("selbr", [8, 4, 128], F32))
            self.biasq = E(nc.sbuf_tensor("biasq", [128, 16 * 9], F32))
            self.o128f = E(nc.sbuf_tensor("o128f", [128, 1], F32))
            self.o128b = E(nc.sbuf_tensor("o128b", [128, 1], BF16))

            self._p0()
            stop = {"p1": 1, "p2": 2, "p34": 3, "b0": 4, "b0f": 5,
                    "b1": 6, "b1f": 7, "p7": 8}[self.upto]
            import os
            paraset = set(os.environ.get("KPARANOID", "").split(","))

            def para(name, f):
                def g():
                    self.d.paranoid = name in paraset
                    f()
                    self.d.paranoid = False
                return g
            steps = [para("p1", self._p1), para("p2", self._p2),
                     para("p34", self._p34),
                     para("b0", lambda: self._bpass1(0)),
                     para("b0f", lambda: self._bpass2(0)),
                     para("b1", lambda: self._bpass1(1)),
                     para("b1f", lambda: self._bpass2(1)),
                     para("p7", self._p7)]
            for i, f in enumerate(steps[:stop]):
                self.abf.push()
                self.af.push()
                f()
                self.abf.pop()
                self.af.pop()
                d.phase_barrier()

            sems = {k: E(nc.semaphore("sem" + k)) for k in Dev.CTRS}
            blk = E(nc.Block())
            with nc.allow_non_contiguous_dma(reason="unit-width edge tiles"):
                d.replay(blk, sems)
            fin = dict(d.cnt)
            blk.sync(lambda e: (e.wait_ge(sems["SP"], fin["SP"]),
                                e.wait_ge(sems["LS"], fin["LS"]),
                                e.wait_ge(sems["LD"], fin["LD"]),
                                e.wait_ge(sems["DVE"], fin["DVE"]),
                                e.wait_ge(sems["ACT"], fin["ACT"]),
                                e.wait_ge(sems["PE"], fin["PE"]),
                                e.wait_ge(sems["POOL"], fin["POOL"])))
        return nc

    def _p0(self):
        for dst, src in ((self.idtf[:, :], self.IDTF),
                         (self.one1[:, :], self.ONE1), (self.one1f[:, :], self.ONE1F),
                         (self.permt[:, :], self.PERMT),
                         (self.selpa[:, :, :], self.SELPA.rearrange("c p s -> p c s")),
                         (self.selb[:, :, :], self.SELB.rearrange("c p s -> p c s")),
                         (self.selbr[:, :, :], self.SELBR.rearrange("c p s -> p c s")),
                         (self.o128f[:, :], self.ONE1F.rearrange("o p -> p o")),
                         (self.o128b[:, :], self.ONE1.rearrange("o p -> p o"))):
            self.ls((lambda dst=dst, src=src: (lambda e: e.dma_start(out=dst, in_=src)))())
        self.d.mark("const", "LS")

    def _p1(self):
        d, abf, af = self.d, self.abf, self.af
        wq_sb = abf.alloc([128, 8, 1024])
        xqb = abf.alloc([128, 2, 8, 512])
        sqv = abf.alloc([128, 2, 512])
        qtb = abf.alloc([128, 2, 8, 512])
        maxq2 = af.alloc([16, 9])
        negb = abf.alloc([16, 9])
        negbf = abf.alloc([1, 144])
        ps = self.ps

        for ic in range(8):
            self.ld((lambda ic=ic: (lambda e: e.dma_start(
                out=wq_sb[:, ic, :], in_=self.wqT[ic * 128:(ic + 1) * 128, :])))())
        d.mark("p1wq", "LD")

        for t, (q0, nq) in enumerate(QT_A):
            w = [f"p1selz_{t-2}"] if t >= 2 else []
            for ic in range(8):
                self.ld((lambda t=t, ic=ic, q0=q0, nq=nq: (lambda e: e.dma_start(
                    out=xqb[:, t % 2, ic, 0:nq],
                    in_=self.xq[ic * 128:(ic + 1) * 128, q0:q0 + nq])))(),
                    waits=w if ic == 0 else ())
            d.mark(f"p1x_{t}", "LD")
            for oc in range(8):
                w = [f"p1x_{t}", "const", "p1wq"]
                k = t * 8 + oc
                if k >= 2:
                    pt, po = divmod(k - 2, 8)
                    w.append(f"p1cp_{pt}_{po}")
                if oc == 0 and t >= 2:
                    w.append(f"p1rm_{t-2}")
                for ic in range(8):
                    d.op("te", (lambda t=t, oc=oc, ic=ic, nq=nq: (lambda e: e.matmul(
                        ps[:, (oc % 2) * 512:(oc % 2) * 512 + nq],
                        wq_sb[:, ic, oc * 128:(oc + 1) * 128],
                        xqb[:, t % 2, ic, 0:nq],
                        start=(ic == 0), stop=(ic == 7))))(),
                        waits=w if ic == 0 else (),
                        inc=f"p1mm_{t}_{oc}" if ic == 7 else None)
                w2 = [f"p1mm_{t}_{oc}"]
                if k >= 2:
                    pt, po = divmod(k - 2, 8)
                    w2.append(f"p1sel_{pt}_{po}")
                if oc == 0 and t >= 2:
                    w2.append(f"p1spq_{t-2}")
                d.op("act", (lambda t=t, oc=oc, nq=nq: (lambda e: e.activation(
                    out=sqv[:, oc % 2, 0:nq],
                    in_=ps[:, (oc % 2) * 512:(oc % 2) * 512 + nq],
                    func=AF.Square)))(), waits=w2, inc=f"p1sq_{t}_{oc}")
                d.op("act", (lambda t=t, oc=oc, nq=nq: (lambda e: e.copy(
                    out=qtb[:, t % 2, oc, 0:nq],
                    in_=ps[:, (oc % 2) * 512:(oc % 2) * 512 + nq])))(),
                    inc=f"p1cp_{t}_{oc}")
                d.op("te", (lambda t=t, oc=oc, nq=nq: (lambda e: e.matmul(
                    ps[:, 1024 + (t % 2) * 512:1024 + (t % 2) * 512 + nq][0:16, :],
                    self.selpa[:, oc, :], sqv[:, oc % 2, 0:nq],
                    start=(oc == 0), stop=(oc == 7))))(),
                    waits=[f"p1sq_{t}_{oc}"], inc=f"p1sel_{t}_{oc}")
            d.mark(f"p1selz_{t}", "PE")
            d.op("dve", (lambda t=t, nq=nq: (lambda e: e.reduce_max(
                maxq2[:, t:t + 1],
                self.ps[:, 1024 + (t % 2) * 512:1024 + (t % 2) * 512 + nq][0:16, :],
                mybir.AxisListType.X)))(),
                waits=[f"p1sel_{t}_7"], inc=f"p1rm_{t}")
            self.sp((lambda t=t, q0=q0, nq=nq: (lambda e: e.dma_start(
                out=self.qTsp.rearrange("(c p) t -> p c t", p=128)[:, :, q0:q0 + nq],
                in_=qtb[:, t % 2, :, 0:nq])))(),
                waits=[f"p1cp_{t}_7"], inc=f"p1spq_{t}")
        nt = len(QT_A)
        d.op("act", lambda e: e.sqrt(out=maxq2[:, :], in_=maxq2[:, :]),
             waits=[f"p1rm_{nt-1}"])
        d.op("act", lambda e: e.mul(out=maxq2[:, :], in_=maxq2[:, :], mul=-8.0))
        d.op("act", lambda e: e.copy(out=negb[:, :], in_=maxq2[:, :]), inc="p1negb")
        self.sp(lambda e: e.dma_start(out=self.bounceA, in_=negb[:, :]),
                waits=["p1negb"], inc="p1bnc")
        self.ls(lambda e: e.dma_start(
            out=negbf[:, :],
            in_=self.bounceA.rearrange("h q -> (h q)").rearrange("(o x) -> o x", o=1)),
            waits=["p1bnc"], inc="p1bncr")
        d.op("te", lambda e: e.matmul(self.ps[:, 0:144], self.one1[:, :],
                                      negbf[:, 0:144], start=True, stop=True),
             waits=["p1bncr", f"p1selz_{nt-1}", f"p1cp_{nt-1}_7"], inc="p1bmm")
        d.op("dve", lambda e: e.tensor_copy(self.biasq[:, :], self.ps[:, 0:144]),
             waits=["p1bmm"], inc="p1bias")

    def _p2(self):
        d, abf = self.d, self.abf
        wk_sb = abf.alloc([128, 8, 1024])
        wv_sb = abf.alloc([128, 8, 1024])
        xkb = abf.alloc([128, 2, 8, 512])
        ktb = abf.alloc([128, 2, 8, 512])
        vtb = abf.alloc([128, 2, 4, 1024])
        ps = self.ps

        for ic in range(8):
            self.ld((lambda ic=ic: (lambda e: e.dma_start(
                out=wk_sb[:, ic, :], in_=self.wkT[ic * 128:(ic + 1) * 128, :])))())
            self.ld((lambda ic=ic: (lambda e: e.dma_start(
                out=wv_sb[:, ic, :], in_=self.wvT[ic * 128:(ic + 1) * 128, :])))())
        d.mark("p2w", "LD")

        for kt in range(NK // 512):
            w = [f"p2pe_{kt-2}"] if kt >= 2 else []
            for ic in range(8):
                self.ld((lambda kt=kt, ic=ic: (lambda e: e.dma_start(
                    out=xkb[:, kt % 2, ic, :],
                    in_=self.xk[ic * 128:(ic + 1) * 128, kt * 512:(kt + 1) * 512])))(),
                    waits=w if ic == 0 else ())
            d.mark(f"p2x_{kt}", "LD")
            for oc in range(8):
                w2 = [f"p2x_{kt}", "p2w"]
                k = kt * 8 + oc
                if k >= 2:
                    pt, po = divmod(k - 2, 8)
                    w2.append(f"p2kcp_{pt}_{po}")
                for ic in range(8):
                    d.op("te", (lambda kt=kt, oc=oc, ic=ic: (lambda e: e.matmul(
                        ps[:, (oc % 2) * 512:(oc % 2 + 1) * 512],
                        wk_sb[:, ic, oc * 128:(oc + 1) * 128],
                        xkb[:, kt % 2, ic, :],
                        start=(ic == 0), stop=(ic == 7))))(),
                        waits=w2 if ic == 0 else (),
                        inc=f"p2kmm_{kt}_{oc}" if ic == 7 else None)
                w3 = [f"p2kmm_{kt}_{oc}"]
                if oc == 0 and kt >= 2:
                    w3.append(f"p2spk_{kt-2}")
                d.op("act", (lambda kt=kt, oc=oc: (lambda e: e.copy(
                    out=ktb[:, kt % 2, oc, :],
                    in_=ps[:, (oc % 2) * 512:(oc % 2 + 1) * 512])))(),
                    waits=w3, inc=f"p2kcp_{kt}_{oc}")
            self.sp((lambda kt=kt: (lambda e: e.dma_start(
                out=self.kTsp.rearrange("(c p) t -> p c t", p=128)[:, :, kt * 512:(kt + 1) * 512],
                in_=ktb[:, kt % 2, :, :])))(),
                waits=[f"p2kcp_{kt}_7"], inc=f"p2spk_{kt}")
            for st in range(4):
                for hf in range(2):
                    j = st * 2 + hf
                    w2 = []
                    if kt * 8 + j >= 2:
                        pt, pj = divmod(kt * 8 + j - 2, 8)
                        w2 = [f"p2vcp_{pt}_{pj}"]
                    for ic in range(8):
                        d.op("te", (lambda kt=kt, st=st, hf=hf, ic=ic: (lambda e: e.matmul(
                            ps[:, 1024 + hf * 512:1024 + (hf + 1) * 512],
                            xkb[:, kt % 2, ic, st * 128:(st + 1) * 128],
                            wv_sb[:, ic, hf * 512:(hf + 1) * 512],
                            start=(ic == 0), stop=(ic == 7))))(),
                            waits=w2 if ic == 0 else (),
                            inc=f"p2vmm_{kt}_{j}" if ic == 7 else None)
                    w3 = [f"p2vmm_{kt}_{j}"]
                    if j == 0 and kt >= 2:
                        w3.append(f"p2spv_{kt-2}_3")
                    d.op("act", (lambda kt=kt, st=st, hf=hf: (lambda e: e.copy(
                        out=vtb[:, kt % 2, st, hf * 512:(hf + 1) * 512],
                        in_=ps[:, 1024 + hf * 512:1024 + (hf + 1) * 512])))(),
                        waits=w3, inc=f"p2vcp_{kt}_{j}")
                self.sp((lambda kt=kt, st=st: (lambda e: e.dma_start(
                    out=self.Vsp[kt * 512 + st * 128: kt * 512 + (st + 1) * 128, :],
                    in_=vtb[:, kt % 2, st, :])))(),
                    waits=[f"p2vcp_{kt}_{st*2+1}"], inc=f"p2spv_{kt}_{st}")
            d.val[f"p2pe_{kt}"] = d.val[f"p2vmm_{kt}_7"]

    def _p34(self):
        d, abf, af = self.d, self.abf, self.af
        ps = self.ps
        self.oT = abf.alloc([128, 8, NQ])
        rcp1 = af.alloc([1, 512])
        rcpb = af.alloc([64, 512])

        def psS(hh, n):
            return ps[:, hh * 512:hh * 512 + n]

        def psSb(hh, n):
            return ps[0:64, hh * 512:hh * 512 + n]

        def psA(hh, n):
            return ps[0:65, 2048 + hh * 512:2048 + hh * 512 + n]

        def psAd(hh, n):
            return ps[64:65, 2048 + hh * 512:2048 + hh * 512 + n]

        def psAn(hh, n):
            return ps[0:64, 2048 + hh * 512:2048 + hh * 512 + n]

        abf.push()
        Kb0 = abf.alloc([128, 2, 4096])
        Vb0 = abf.alloc([128, 32, 4, 65])
        Qb0 = abf.alloc([128, 2, 1])
        Pb_s = abf.alloc([128, 4, 32])
        for g in range(4):
            lw = [f"p3so_{g-1}_3"] if g >= 1 else ["p2spk_15", "p2spv_15_3"]
            for ch in range(2):
                self.ls((lambda g=g, ch=ch: (lambda e: e.dma_start(
                    out=Kb0[:, ch, :],
                    in_=self.kTsp[g * 256 + ch * 128: g * 256 + (ch + 1) * 128, 0:4096])))(),
                    waits=lw if ch == 0 else ())
                self.ls((lambda g=g, ch=ch: (lambda e: e.dma_start(
                    out=Qb0[:, ch, :],
                    in_=self.qTsp[g * 256 + ch * 128: g * 256 + (ch + 1) * 128, 0:1])))(),
                    waits=["p1spq_0"] if (g == 0 and ch == 0) else ())
            for st in range(32):
                self.ls((lambda g=g, st=st: (lambda e: e.dma_start(
                    out=Vb0[:, st, :, 0:64],
                    in_=self.Vsp[st * 128:(st + 1) * 128,
                                 g * 256:(g + 1) * 256].rearrange("p (h e) -> p h e", h=4))))())
                self.ls((lambda st=st: (lambda e: e.dma_start(
                    out=Vb0[:, st, :, 64:65],
                    in_=self.onesA[st * 128:(st + 1) * 128, 0:4].rearrange(
                        "p (h o) -> p h o", o=1))))())
            d.mark(f"p3sld_{g}", "LS")
            for hh in range(4):
                h = g * 4 + hh
                rh = (hh % 2) * 64
                ck = hh // 2
                for kc in range(32):
                    w = []
                    if kc == 0:
                        w = [f"p3sld_{g}", "p1bias"]
                        if g >= 1:
                            w += [f"p3se_{g-1}_{hh}", f"p3so_{g-1}_{hh}"]
                    d.op("te", (lambda hh=hh, rh=rh, ck=ck, kc=kc: (lambda e: e.matmul(
                        ps[:, hh * 512 + kc: hh * 512 + kc + 1],
                        Kb0[rh:rh + 64, ck, kc * 128:(kc + 1) * 128],
                        Qb0[rh:rh + 64, ck, :],
                        start=True, stop=True, tile_position=(rh, 0))))(),
                        waits=w, inc=f"p3sm_{g}_{hh}" if kc == 31 else None)
                d.op("act", (lambda g=g, hh=hh, h=h: (lambda e: e.activation(
                    out=Pb_s[:, hh, 0:32], in_=ps[:, hh * 512:hh * 512 + 32],
                    func=AF.Exp, bias=self.biasq[:, h * 9:h * 9 + 1], scale=0.125)))(),
                    waits=[f"p3sm_{g}_{hh}"] + ([f"p3sav_{g-1}_{hh}"] if g >= 1 else []),
                    inc=f"p3se_{g}_{hh}")
                for kc in range(32):
                    d.op("te", (lambda hh=hh, kc=kc: (lambda e: e.matmul(
                        psA(hh, 1),
                        Vb0[:, kc, hh, :], Pb_s[:, hh, kc:kc + 1],
                        start=(kc == 0), stop=(kc == 31))))(),
                        waits=[f"p3se_{g}_{hh}"] if kc == 0 else (),
                        inc=f"p3sav_{g}_{hh}" if kc == 31 else None)
                d.op("dve", (lambda hh=hh: (lambda e: e.tensor_scalar_max(
                    rcp1[:, 0:1], psAd(hh, 1), DEN_EPS)))(),
                    waits=[f"p3sav_{g}_{hh}"])
                d.op("dve", lambda e: e.reciprocal(rcp1[:, 0:1], rcp1[:, 0:1]),
                     inc=f"p3src_{g}_{hh}")
                d.op("te", (lambda hh=hh: (lambda e: e.matmul(
                    psSb(hh, 1), self.one1f[:, 0:64], rcp1[:, 0:1],
                    start=True, stop=True)))(),
                    waits=[f"p3src_{g}_{hh}", f"p3se_{g}_{hh}"], inc=f"p3sbc_{g}_{hh}")
                d.op("dve", (lambda hh=hh: (lambda e: e.tensor_copy(
                    rcpb[:, 0:1], psSb(hh, 1))))(),
                    waits=[f"p3sbc_{g}_{hh}"])
                d.op("dve", (lambda hh=hh, h=h, rh=rh: (lambda e: e.tensor_mul(
                    self.oT[rh:rh + 64, h // 2, 0:1],
                    psAn(hh, 1), rcpb[:, 0:1])))(),
                    inc=f"p3so_{g}_{hh}")
        abf.pop()

        Kb = abf.alloc([128, 2, 2, 512])
        Vb = abf.alloc([128, 2, 4, 4, 65])
        Qb = abf.alloc([128, 2, 512])
        Pb = abf.alloc([128, 2, 4, 512])
        for ti in range(1, 9):
            q0, nq = QT_A[ti]
            ext = q0 + nq + CAUS
            assert ext % 512 == 0
            nks = ext // 512
            for g in range(4):
                prevg = (f"p3o_{ti}_{g-1}_3" if g >= 1 else
                         (f"p3o_{ti-1}_3_3" if ti >= 2 else "p3so_3_3"))
                prevm = (f"p3mm_{ti}_{g-1}" if g >= 1 else
                         (f"p3mm_{ti-1}_3" if ti >= 2 else "p3sm_3_3"))
                for ch in range(2):
                    self.ls((lambda g=g, ch=ch, q0=q0, nq=nq: (lambda e: e.dma_start(
                        out=Qb[:, ch, 0:nq],
                        in_=self.qTsp[g * 256 + ch * 128: g * 256 + (ch + 1) * 128,
                                      q0:q0 + nq])))(),
                        waits=[prevm, f"p1spq_{ti}"] if ch == 0 else ())
                for ks in range(nks):
                    lw = [f"p3pe_{ti}_{g}_{ks-2}"] if ks >= 2 else []
                    for ch in range(2):
                        self.ls((lambda g=g, ks=ks, ch=ch: (lambda e: e.dma_start(
                            out=Kb[:, ks % 2, ch, :],
                            in_=self.kTsp[g * 256 + ch * 128: g * 256 + (ch + 1) * 128,
                                          ks * 512:(ks + 1) * 512])))(),
                            waits=lw if ch == 0 else ())
                    for st in range(4):
                        self.ls((lambda g=g, ks=ks, st=st: (lambda e: e.dma_start(
                            out=Vb[:, ks % 2, st, :, 0:64],
                            in_=self.Vsp[ks * 512 + st * 128: ks * 512 + (st + 1) * 128,
                                         g * 256:(g + 1) * 256].rearrange(
                                "p (h e) -> p h e", h=4))))())
                        self.ls((lambda ks=ks, st=st: (lambda e: e.dma_start(
                            out=Vb[:, ks % 2, st, :, 64:65],
                            in_=self.onesA[ks * 512 + st * 128: ks * 512 + (st + 1) * 128,
                                           0:4].rearrange("p (h o) -> p h o", o=1))))())
                    d.mark(f"p3kv_{ti}_{g}_{ks}", "LS")
                    for kc in range(4):
                        k0 = ks * 512 + kc * 128
                        base = q0 + CAUS - k0
                        masked = base < 127
                        for hh in range(4):
                            h = g * 4 + hh
                            rh = (hh % 2) * 64
                            ck = hh // 2
                            sw = []
                            if kc == 0 and ks == 0:
                                sw = [f"p3kv_{ti}_{g}_{ks}", "p1bias", prevg]
                            elif kc == 0:
                                sw = [f"p3kv_{ti}_{g}_{ks}"]
                            prev_e = (f"p3e_{ti}_{g}_{ks}_{kc-1}_{hh}" if kc >= 1 else
                                      (f"p3e_{ti}_{g}_{ks-1}_3_{hh}" if ks >= 1 else None))
                            if prev_e:
                                sw.append(prev_e)
                            d.op("te", (lambda hh=hh, rh=rh, ck=ck, ks=ks, kc=kc, nq=nq:
                                        (lambda e: e.matmul(
                                            psS(hh, nq),
                                            Kb[rh:rh + 64, ks % 2, ck, kc * 128:(kc + 1) * 128],
                                            Qb[rh:rh + 64, ck, 0:nq],
                                            start=True, stop=True, tile_position=(rh, 0))))(),
                                waits=sw,
                                inc=f"p3m_{ti}_{g}_{ks}_{kc}_{hh}")
                            ew = [f"p3m_{ti}_{g}_{ks}_{kc}_{hh}"]
                            prev_av = (f"p3av_{ti}_{g}_{ks}_{kc-2}_{hh}" if kc >= 2 else
                                       (f"p3av_{ti}_{g}_{ks-1}_{kc+2}_{hh}" if ks >= 1 else None))
                            if prev_av:
                                ew.append(prev_av)
                            d.op("act", (lambda ti=ti, hh=hh, h=h, kc=kc, nq=nq:
                                         (lambda e: e.activation(
                                             out=Pb[:, kc % 2, hh, 0:nq], in_=psS(hh, nq),
                                             func=AF.Exp,
                                             bias=self.biasq[:, h * 9 + ti:h * 9 + ti + 1],
                                             scale=0.125)))(),
                                waits=ew, inc=f"p3e_{ti}_{g}_{ks}_{kc}_{hh}")
                            avw = [f"p3e_{ti}_{g}_{ks}_{kc}_{hh}"]
                            if masked:
                                d.op("pool", (lambda hh=hh, kc=kc, base=base, nq=nq:
                                              (lambda e: e.affine_select(
                                                  out=Pb[:, kc % 2, hh, 0:nq],
                                                  in_=Pb[:, kc % 2, hh, 0:nq],
                                                  compare_op=ALU.is_ge, fill=0.0,
                                                  base=base, pattern=[[1, nq]],
                                                  channel_multiplier=-1)))(),
                                    waits=[f"p3e_{ti}_{g}_{ks}_{kc}_{hh}"],
                                    inc=f"p3k_{ti}_{g}_{ks}_{kc}_{hh}")
                                avw.append(f"p3k_{ti}_{g}_{ks}_{kc}_{hh}")
                            d.op("te", (lambda hh=hh, ks=ks, kc=kc, nq=nq,
                                        first=(ks == 0 and kc == 0),
                                        last=(ks == nks - 1 and kc == 3):
                                        (lambda e: e.matmul(
                                            psA(hh, nq),
                                            Vb[:, ks % 2, kc, hh, :],
                                            Pb[:, kc % 2, hh, 0:nq],
                                            start=first, stop=last)))(),
                                waits=avw, inc=f"p3av_{ti}_{g}_{ks}_{kc}_{hh}")
                    d.mark(f"p3pe_{ti}_{g}_{ks}", "PE")
                d.mark(f"p3mm_{ti}_{g}", "PE")
                for hh in range(4):
                    h = g * 4 + hh
                    rh = (hh % 2) * 64
                    d.op("dve", (lambda hh=hh, nq=nq, ti=ti, g=g, nks=nks:
                                 (lambda e: e.tensor_scalar_max(
                                     rcp1[:, 0:nq], psAd(hh, nq), DEN_EPS)))(),
                        waits=[f"p3av_{ti}_{g}_{nks-1}_3_{hh}"])
                    d.op("dve", (lambda nq=nq: (lambda e: e.reciprocal(
                        rcp1[:, 0:nq], rcp1[:, 0:nq])))(), inc=f"p3rc_{ti}_{g}_{hh}")
                    d.op("te", (lambda hh=hh, nq=nq: (lambda e: e.matmul(
                        psSb(hh, nq), self.one1f[:, 0:64], rcp1[:, 0:nq],
                        start=True, stop=True)))(),
                        waits=[f"p3rc_{ti}_{g}_{hh}", f"p3e_{ti}_{g}_{nks-1}_3_{hh}"],
                        inc=f"p3bc_{ti}_{g}_{hh}")
                    d.op("dve", (lambda hh=hh, nq=nq: (lambda e: e.tensor_copy(
                        rcpb[:, 0:nq], psSb(hh, nq))))(),
                        waits=[f"p3bc_{ti}_{g}_{hh}"])
                    d.op("dve", (lambda hh=hh, h=h, rh=rh, q0=q0, nq=nq:
                                 (lambda e: e.tensor_mul(
                                     self.oT[rh:rh + 64, h // 2, q0:q0 + nq],
                                     psAn(hh, nq), rcpb[:, 0:nq])))(),
                        inc=f"p3o_{ti}_{g}_{hh}" if hh == 3 else None)
        d.phase_barrier()

        woT_sb = abf.alloc([128, 8, 1024])
        xa = af.alloc([128, 8, 512])
        xabf = abf.alloc([128, 2, 512])
        sqb = abf.alloc([128, 2, 512])
        row = af.alloc([1, 6, 512])
        for ic in range(8):
            self.ls((lambda ic=ic: (lambda e: e.dma_start(
                out=woT_sb[:, ic, :], in_=self.woT[ic * 128:(ic + 1) * 128, :])))())
        d.mark("p4w", "LS")
        for t, (q0, nq) in enumerate(QT_A):
            for oc in range(8):
                self.ls((lambda t=t, oc=oc, q0=q0, nq=nq: (lambda e: e.dma_start(
                    out=xa[0:128, oc, 0:nq].bitcast(F32) if False else xa[:, oc, 0:nq],
                    in_=self.xq[oc * 128:(oc + 1) * 128, q0:q0 + nq])))(),
                    waits=[f"p4res_{t-1}"] if (oc == 0 and t >= 1) else ())
            d.mark(f"p4x_{t}", "LS")
            for oc in range(8):
                w = ["p4w", f"p4x_{t}"]
                if t * 8 + oc >= 2:
                    pt, po = divmod(t * 8 + oc - 2, 8)
                    w.append(f"p4add_{pt}_{po}")
                for ic in range(8):
                    d.op("te", (lambda t=t, oc=oc, ic=ic, q0=q0, nq=nq: (lambda e: e.matmul(
                        ps[:, (oc % 2) * 512:(oc % 2) * 512 + nq],
                        woT_sb[:, ic, oc * 128:(oc + 1) * 128],
                        self.oT[:, ic, q0:q0 + nq],
                        start=(ic == 0), stop=(ic == 7))))(),
                        waits=w if ic == 0 else (),
                        inc=f"p4mm_{t}_{oc}" if ic == 7 else None)
                d.op("dve", (lambda t=t, oc=oc, nq=nq: (lambda e: e.tensor_add(
                    xa[:, oc, 0:nq], ps[:, (oc % 2) * 512:(oc % 2) * 512 + nq],
                    xa[:, oc, 0:nq])))(),
                    waits=[f"p4mm_{t}_{oc}"], inc=f"p4add_{t}_{oc}")
                d.op("act", (lambda t=t, oc=oc, nq=nq: (lambda e: e.copy(
                    out=xabf[:, oc % 2, 0:nq], in_=xa[:, oc, 0:nq])))(),
                    waits=[f"p4add_{t}_{oc}"] +
                          ([f"p4st_{t}_{oc-2}"] if oc >= 2 else
                           ([f"p4st_{t-1}_{oc+6}"] if t >= 1 else [])),
                    inc=f"p4cb_{t}_{oc}")
                d.op("act", (lambda t=t, oc=oc, nq=nq: (lambda e: e.activation(
                    out=sqb[:, oc % 2, 0:nq], in_=xa[:, oc, 0:nq], func=AF.Square)))(),
                    inc=f"p4sb_{t}_{oc}")
                d.op("te", (lambda t=t, oc=oc, nq=nq: (lambda e: e.matmul(
                    ps[0:1, 1024:1024 + nq], self.o128b[:, :], xabf[:, oc % 2, 0:nq],
                    start=(oc == 0), stop=(oc == 7))))(),
                    waits=[f"p4cb_{t}_{oc}"] +
                          ([f"p4ms_{t-1}"] if (oc == 0 and t >= 1) else []))
                d.op("te", (lambda t=t, oc=oc, nq=nq: (lambda e: e.matmul(
                    ps[0:1, 1536:1536 + nq], self.o128b[:, :], sqb[:, oc % 2, 0:nq],
                    start=(oc == 0), stop=(oc == 7))))(),
                    waits=[f"p4sb_{t}_{oc}"],
                    inc=f"p4st_{t}_{oc}")
            d.op("act", (lambda nq=nq: (lambda e: e.mul(
                out=row[:, 0, 0:nq], in_=ps[0:1, 1024:1024 + nq], mul=1.0 / 1024)))(),
                waits=[f"p4st_{t}_7"] + ([f"p4ab_{t-1}"] if t >= 1 else []),
                inc=f"p4mu_{t}")
            d.op("act", (lambda nq=nq: (lambda e: e.mul(
                out=row[:, 1, 0:nq], in_=ps[0:1, 1536:1536 + nq], mul=1.0 / 1024)))())
            d.op("act", (lambda nq=nq: (lambda e: e.activation(
                out=row[:, 2, 0:nq], in_=row[:, 0, 0:nq], func=AF.Square)))(),
                inc=f"p4ms_{t}")
            d.op("dve", (lambda nq=nq: (lambda e: e.tensor_sub(
                row[:, 3, 0:nq], row[:, 1, 0:nq], row[:, 2, 0:nq])))(),
                waits=[f"p4ms_{t}"], inc=f"p4var_{t}")
            d.op("dve", (lambda nq=nq: (lambda e: e.tensor_scalar_add(
                row[:, 3, 0:nq], row[:, 3, 0:nq], 1e-5)))(),
                waits=[f"p4var_{t}"], inc=f"p4vare_{t}")
            d.op("act", (lambda nq=nq: (lambda e: e.sqrt(
                out=row[:, 3, 0:nq], in_=row[:, 3, 0:nq])))(),
                waits=[f"p4vare_{t}"], inc=f"p4sd_{t}")
            d.op("dve", (lambda nq=nq: (lambda e: e.reciprocal(
                row[:, 4, 0:nq], row[:, 3, 0:nq])))(),
                waits=[f"p4sd_{t}"])
            d.op("dve", (lambda nq=nq: (lambda e: e.tensor_mul(
                row[:, 5, 0:nq], row[:, 0, 0:nq], row[:, 4, 0:nq])))(),
                inc=f"p4ab_{t}")
            d.op("te", (lambda nq=nq: (lambda e: e.matmul(
                ps[:, 2048:2048 + nq], self.one1f[:, :], row[:, 4, 0:nq],
                start=True, stop=True)))(),
                waits=[f"p4ab_{t}", f"p4y_{t-1}" if t >= 1 else ("DVE", 0)],
                inc=f"p4bca_{t}")
            d.op("te", (lambda nq=nq: (lambda e: e.matmul(
                ps[:, 2560:2560 + nq], self.one1f[:, :], row[:, 5, 0:nq],
                start=True, stop=True)))(), inc=f"p4bcb_{t}")
            for oc in range(8):
                d.op("dve", (lambda oc=oc, nq=nq: (lambda e: e.tensor_mul(
                    xa[:, oc, 0:nq], xa[:, oc, 0:nq], ps[:, 2048:2048 + nq])))(),
                    waits=[f"p4bca_{t}"] if oc == 0 else ())
                d.op("dve", (lambda oc=oc, nq=nq: (lambda e: e.tensor_sub(
                    xa[:, oc, 0:nq], xa[:, oc, 0:nq], ps[:, 2560:2560 + nq])))(),
                    waits=[f"p4bcb_{t}"] if oc == 0 else (),
                    inc=f"p4yc_{t}_{oc}")
            d.mark(f"p4y_{t}", "DVE")
            self.sp((lambda t=t, q0=q0, nq=nq: (lambda e: e.dma_start(
                out=self.ysp.rearrange("(c p) t -> p c t", p=128)[:, :, q0:q0 + nq],
                in_=xa[:, :, 0:nq])))(),
                waits=[f"p4yc_{t}_7"], inc=f"p4sp_{t}")
            d.mark(f"p4res_{t}", "SP")

    def _bpass1(self, l):
        d, abf, af = self.d, self.abf, self.af
        ps = self.ps
        W = self.W[l]
        P = f"b{l}"
        kvofs = 0 if l == 0 else 1024
        ntt = 8 if l == 0 else 6
        nsp = 2 if l == 0 else 1
        SRC = self.ysp if l == 0 else self.u1sp
        srcofs = 1 if l == 0 else 2
        DST = self.uA0sp if l == 0 else self.uA1sp
        uofs = -1022 if l == 0 else 2

        wqkv_sb = abf.alloc([128, 8, 1536])
        woB_sb = abf.alloc([128, 4, 1024])
        kBr = abf.alloc([128, 4, RING])
        VBr = abf.alloc([128, 12, 8, 65])
        rkT = af.alloc([128, 12, 8])
        qs_st = af.alloc([128, 2, 1])
        ks_st = af.alloc([128, 2, 1])
        selwq = abf.alloc([128, 4, 8])
        qs2b = abf.alloc([128, 1])
        ksrow = af.alloc([1, 64])
        maxks8 = af.alloc([8, 1])
        ibf = abf.alloc([128, 2, 8, 513])
        rawb = abf.alloc([128, 2, 8, 512])
        sqv = abf.alloc([128, 2, 512])
        tab = af.alloc([128, 2, 4, 512])
        qq = af.alloc([128, 2, 4, 512])
        tmpA = af.alloc([128, 512])
        tmpB = af.alloc([128, 512])
        qtb = abf.alloc([128, 2, 4, 512])
        r8 = af.alloc([8, 6, 512])
        maxqc = af.alloc([8, 4])
        negc = af.alloc([8, 4])
        negTb = abf.alloc([4, 8])
        negfl = abf.alloc([1, 32])
        biasB = af.alloc([128, 2, 32])
        PbB = abf.alloc([128, 2, 4, 128])
        otb = abf.alloc([128, 2, 4, 512])
        otbs = abf.alloc([128, 4, 2])
        basef = af.alloc([128, 2, 2, 512])
        rcp1 = af.alloc([1, 128])
        rcpb = af.alloc([64, 128])

        for ic in range(8):
            self.ls((lambda ic=ic, W=W: (lambda e: e.dma_start(
                out=wqkv_sb[:, ic, :], in_=W["wqkv"][ic * 128:(ic + 1) * 128, :])))())
        for ic in range(4):
            self.ls((lambda ic=ic, W=W: (lambda e: e.dma_start(
                out=woB_sb[:, ic, :], in_=W["woB"][ic * 128:(ic + 1) * 128, :])))())
        for rep in range(2):
            self.ls((lambda rep=rep, W=W: (lambda e: e.dma_start(
                out=qs_st[rep * 64:(rep + 1) * 64, 0, :],
                in_=W["qs"].rearrange("(k o) -> k o", o=1))))())
            self.ls((lambda rep=rep, W=W: (lambda e: e.dma_start(
                out=ks_st[rep * 64:(rep + 1) * 64, 0, :],
                in_=W["ks"].rearrange("(k o) -> k o", o=1))))())
            self.ls((lambda rep=rep, W=W: (lambda e: e.dma_start(
                out=qs_st[rep * 64:rep * 64 + 32, 1, :],
                in_=W["qs"].rearrange("(k o) -> k o", o=1)[32:64])))())
            self.ls((lambda rep=rep, W=W: (lambda e: e.dma_start(
                out=qs_st[rep * 64 + 32:rep * 64 + 64, 1, :],
                in_=W["qs"].rearrange("(k o) -> k o", o=1)[0:32])))())
            self.ls((lambda rep=rep, W=W: (lambda e: e.dma_start(
                out=ks_st[rep * 64:rep * 64 + 32, 1, :],
                in_=W["ks"].rearrange("(k o) -> k o", o=1)[32:64])))())
            self.ls((lambda rep=rep, W=W: (lambda e: e.dma_start(
                out=ks_st[rep * 64 + 32:rep * 64 + 64, 1, :],
                in_=W["ks"].rearrange("(k o) -> k o", o=1)[0:32])))())
        self.ls((lambda W=W: (lambda e: e.dma_start(
            out=ksrow[:, :], in_=W["ks"].rearrange("(o k) -> o k", o=1))))())
        d.mark(P + "sld", "LS")
        for b0 in (0, 64):
            d.op("act", (lambda b0=b0: (lambda e: e.mul(
                out=qs_st[b0:b0 + 32, 1, :], in_=qs_st[b0:b0 + 32, 1, :], mul=-1.0)))(),
                waits=[P + "sld"] if b0 == 0 else ())
            d.op("act", (lambda b0=b0: (lambda e: e.mul(
                out=ks_st[b0:b0 + 32, 1, :], in_=ks_st[b0:b0 + 32, 1, :], mul=-1.0)))())
        d.op("act", lambda e: e.activation(out=qs2b[:, :], in_=qs_st[:, 0, :],
                                           func=AF.Square), inc=P + "qs2")
        d.op("dve", lambda e: e.memset(selwq[:, :, :], 0.0), waits=[P + "qs2"],
             inc=P + "selz")
        for c in range(4):
            d.op("act", (lambda c=c: (lambda e: e.copy(
                out=selwq[0:64, c, 2 * c:2 * c + 1], in_=qs2b[0:64, :])))(),
                waits=[P + "selz"] if c == 0 else ())
            d.op("act", (lambda c=c: (lambda e: e.copy(
                out=selwq[64:128, c, 2 * c + 1:2 * c + 2], in_=qs2b[64:128, :])))())
        d.mark(P + "selwq_a", "ACT")
        d.op("dve", lambda e: e.reduce_max(maxks8[0:1, :], ksrow[:, :],
                                           mybir.AxisListType.X,
                                           apply_absolute_value=True),
             waits=[P + "sld"], inc=P + "mk1")
        d.op("te", lambda e: e.matmul(ps[0:8, 0:1], self.one1f[:, 0:8],
                                      maxks8[0:1, 0:1], start=True, stop=True),
             waits=[P + "mk1"], inc=P + "mk2")
        d.op("dve", lambda e: e.tensor_copy(maxks8[:, :], ps[0:8, 0:1]),
             waits=[P + "mk2"], inc=P + "mk")
        d.mark(P + "setup_dve", "DVE")

        marks = {}

        def attend(lblp, qslot, nqv, qtbpar, qcol0, biaslbl, biaspar, biascol0,
                   kvrdy, prevo, o_pre, o_par):
            kbase0 = (qslot - 512) // 128 * 128
            kcs = [kbase0 + 128 * k for k in range(5)]
            for grp in range(2):
                for kci, k0 in enumerate(kcs):
                    rc = (k0 // 128) % 12
                    ring0 = k0 % RING
                    for hh in range(4):
                        h = grp * 4 + hh
                        rh = (h % 2) * 64
                        ck = h // 2
                        w = []
                        if kci == 0:
                            w = list(kvrdy) + [biaslbl]
                            if grp >= 1:
                                w.append(f"{lblp}_o_0_{hh}")
                            elif prevo is not None:
                                w.append(prevo + f"_o_1_{hh}")
                        else:
                            w.append(f"{lblp}_e_{grp}_{kci-1}_{hh}")
                        d.op("te", (lambda hh=hh, rh=rh, ck=ck, ring0=ring0,
                                    qtbpar=qtbpar, qcol0=qcol0, nqv=nqv:
                                    (lambda e: e.matmul(
                                        ps[:, hh * 512:hh * 512 + nqv],
                                        kBr[rh:rh + 64, ck, ring0:ring0 + 128],
                                        qtb[rh:rh + 64, qtbpar, ck, qcol0:qcol0 + nqv],
                                        start=True, stop=True, tile_position=(rh, 0))))(),
                            waits=w, inc=f"{lblp}_m_{grp}_{kci}_{hh}")
                        ew = [f"{lblp}_m_{grp}_{kci}_{hh}"]
                        if kci >= 2:
                            ew.append(f"{lblp}_av_{grp}_{kci-2}_{hh}")
                        d.op("act", (lambda hh=hh, h=h, kci=kci, rc=rc, nqv=nqv,
                                     biaspar=biaspar, biascol0=biascol0:
                                     (lambda e: e.activation(
                                         out=PbB[:, kci % 2, hh, 0:nqv],
                                         in_=ps[:, hh * 512:hh * 512 + nqv],
                                         func=AF.Exp,
                                         bias=biasB[:, biaspar,
                                                    biascol0 + h:biascol0 + h + 1],
                                         scale=rkT[:, rc, h:h + 1])))(),
                            waits=ew, inc=f"{lblp}_e_{grp}_{kci}_{hh}")
                        avw = [f"{lblp}_e_{grp}_{kci}_{hh}"]
                        if kci == 0 or kci == 4:
                            cm = 1 if kci == 0 else -1
                            base = (k0 - qslot + 512) if kci == 0 else (qslot - k0)
                            d.op("pool", (lambda hh=hh, kci=kci, cm=cm, base=base,
                                          nqv=nqv: (lambda e: e.affine_select(
                                              out=PbB[:, kci % 2, hh, 0:nqv],
                                              in_=PbB[:, kci % 2, hh, 0:nqv],
                                              compare_op=ALU.is_ge, fill=0.0,
                                              base=base, pattern=[[-cm, nqv]],
                                              channel_multiplier=cm)))(),
                                waits=[f"{lblp}_e_{grp}_{kci}_{hh}"],
                                inc=f"{lblp}_k_{grp}_{kci}_{hh}")
                            avw.append(f"{lblp}_k_{grp}_{kci}_{hh}")
                        if kci == 0:
                            if grp >= 1:
                                avw.append(f"{lblp}_oav_0_{hh}")
                            elif prevo is not None:
                                avw.append(prevo + f"_oav_1_{hh}")
                        d.op("te", (lambda hh=hh, kci=kci, rc=rc, nqv=nqv, h=h,
                                    first=(kci == 0), last=(kci == 4):
                                    (lambda e: e.matmul(
                                        ps[0:65, 2048 + hh * 512:2048 + hh * 512 + nqv],
                                        VBr[:, rc, h, :], PbB[:, kci % 2, hh, 0:nqv],
                                        start=first, stop=last)))(),
                            waits=avw, inc=f"{lblp}_av_{grp}_{kci}_{hh}")
                for hh in range(4):
                    h = grp * 4 + hh
                    rh = (h % 2) * 64
                    d.op("dve", (lambda hh=hh, nqv=nqv: (lambda e: e.tensor_scalar_max(
                        rcp1[:, 0:nqv],
                        ps[64:65, 2048 + hh * 512:2048 + hh * 512 + nqv], DEN_EPS)))(),
                        waits=[f"{lblp}_av_{grp}_4_{hh}"])
                    d.op("dve", (lambda nqv=nqv: (lambda e: e.reciprocal(
                        rcp1[:, 0:nqv], rcp1[:, 0:nqv])))(), inc=f"{lblp}_rc_{grp}_{hh}")
                    d.op("te", (lambda hh=hh, nqv=nqv: (lambda e: e.matmul(
                        ps[0:64, hh * 512:hh * 512 + nqv], self.one1f[:, 0:64],
                        rcp1[:, 0:nqv], start=True, stop=True)))(),
                        waits=[f"{lblp}_rc_{grp}_{hh}", f"{lblp}_e_{grp}_4_{hh}"],
                        inc=f"{lblp}_bc_{grp}_{hh}")
                    d.op("dve", (lambda hh=hh, nqv=nqv: (lambda e: e.tensor_copy(
                        rcpb[:, 0:nqv], ps[0:64, hh * 512:hh * 512 + nqv])))(),
                        waits=[f"{lblp}_bc_{grp}_{hh}"])
                    if o_par is not None:
                        d.op("dve", (lambda hh=hh, h=h, rh=rh, nqv=nqv:
                                     (lambda e: e.tensor_mul(
                                         otb[rh:rh + 64, o_par, h // 2,
                                             o_pre:o_pre + nqv],
                                         ps[0:64, 2048 + hh * 512:2048 + hh * 512 + nqv],
                                         rcpb[:, 0:nqv])))(),
                            inc=f"{lblp}_o_{grp}_{hh}")
                    else:
                        d.op("dve", (lambda hh=hh, h=h, rh=rh, nqv=nqv:
                                     (lambda e: e.tensor_mul(
                                         otbs[rh:rh + 64, h // 2, 0:nqv],
                                         ps[0:64, 2048 + hh * 512:2048 + hh * 512 + nqv],
                                         rcpb[:, 0:nqv])))(),
                            inc=f"{lblp}_o_{grp}_{hh}")
                    d.mark(f"{lblp}_oav_{grp}_{hh}", "DVE")

        def wo_res(pre, src_c0, dst_c0, nqv, opar, use_otbs, owaits, bpar):
            for oc in range(8):
                self.ls((lambda oc=oc, src_c0=src_c0, nqv=nqv, bpar=bpar:
                         (lambda e: e.dma_start(
                             out=basef[:, bpar, oc % 2, 0:nqv],
                             in_=SRC[oc * 128:(oc + 1) * 128, src_c0:src_c0 + nqv])))(),
                    inc=f"{pre}_base_{oc}")
                w = list(owaits) if oc == 0 else []
                if oc >= 2:
                    w.append(f"{pre}_res_{oc-2}")
                for ic in range(4):
                    d.op("te", (lambda oc=oc, ic=ic, opar=opar, nqv=nqv,
                                use_otbs=use_otbs: (lambda e: e.matmul(
                                    ps[:, (oc % 2) * 512:(oc % 2) * 512 + nqv],
                                    woB_sb[:, ic, oc * 128:(oc + 1) * 128],
                                    otbs[:, ic, 0:nqv] if use_otbs
                                    else otb[:, opar, ic, 0:nqv],
                                    start=(ic == 0), stop=(ic == 3))))(),
                        waits=w if ic == 0 else (),
                        inc=f"{pre}_wmm_{oc}" if ic == 3 else None)
                d.op("dve", (lambda oc=oc, nqv=nqv, bpar=bpar: (lambda e: e.tensor_add(
                    basef[:, bpar, oc % 2, 0:nqv],
                    ps[:, (oc % 2) * 512:(oc % 2) * 512 + nqv],
                    basef[:, bpar, oc % 2, 0:nqv])))(),
                    waits=[f"{pre}_wmm_{oc}", f"{pre}_base_{oc}"], inc=f"{pre}_res_{oc}")
                self.sp((lambda oc=oc, dst_c0=dst_c0, nqv=nqv, bpar=bpar:
                         (lambda e: e.dma_start(
                             out=DST[oc * 128:(oc + 1) * 128, dst_c0:dst_c0 + nqv],
                             in_=basef[:, bpar, oc % 2, 0:nqv])))(),
                    waits=[f"{pre}_res_{oc}"],
                    inc=f"{pre}_sp_{oc}" if oc == 7 else None)

        def tt_body(ti):
            pre = f"{P}t{ti}"
            s0 = 512 * ti
            qact = ti >= 1
            attn = ti >= 2
            par = ti % 2
            if ti >= 2:
                snap = marks[ti - 2]["snap"]
                for eng in Dev.ENGS:
                    d.q[eng].append({"fn": None, "waits": snap})
            mprev = marks.get(ti - 1)
            pw = [("PE", mprev["pe"])] if mprev else []
            for ic in range(8):
                self.ld((lambda ic=ic, s0=s0: (lambda e: e.dma_start(
                    out=ibf[:, ti % 2, ic, :],
                    in_=SRC[ic * 128:(ic + 1) * 128,
                            s0 + srcofs - 1: s0 + srcofs + 512])))(),
                    waits=pw if ic == 0 else ())
            for j, T in enumerate((self.TQC, self.TQS, self.TKC, self.TKS)):
                if j < 2 and not qact:
                    continue
                self.ls((lambda j=j, T=T, s0=s0: (lambda e: e.dma_start(
                    out=tab[:, ti % 2, j, :],
                    in_=T[:, s0 + kvofs: s0 + kvofs + 512])))())
            d.mark(pre + "_ld", "LD")
            d.mark(pre + "_ls", "LS")
            ocs = (list(range(4)) if qact else []) + list(range(4, 8))
            first_pe = True
            first_act = True
            for oc in ocs:
                w = [pre + "_ld", P + "sld"]
                if first_pe:
                    if mprev:
                        w += [("ACT", mprev["act"]), ("DVE", mprev["dve"])]
                    first_pe = False
                prevs = [o for o in ocs if o < oc]
                if len(prevs) >= 2:
                    w.append(f"{pre}_qkcp_{prevs[-2]}")
                for ic in range(8):
                    ofs = 1 if ic < 4 else 0
                    d.op("te", (lambda oc=oc, ic=ic, ofs=ofs, par=par:
                                (lambda e: e.matmul(
                                    ps[:, (oc % 2) * 512:(oc % 2 + 1) * 512],
                                    wqkv_sb[:, ic, oc * 128:(oc + 1) * 128],
                                    ibf[:, par, ic, ofs:ofs + 512],
                                    start=(ic == 0), stop=(ic == 7))))(),
                        waits=w if ic == 0 else (),
                        inc=f"{pre}_qkmm_{oc}" if ic == 7 else None)
                w2 = [f"{pre}_qkmm_{oc}"]
                if first_act:
                    if mprev:
                        w2.append(("PE", mprev["pe"]))
                    first_act = False
                if len(prevs) >= 2:
                    w2.append(f"{pre}_nrm_{prevs[-2]}")
                d.op("act", (lambda oc=oc: (lambda e: e.activation(
                    out=sqv[:, oc % 2, :],
                    in_=ps[:, (oc % 2) * 512:(oc % 2 + 1) * 512],
                    func=AF.Square)))(), waits=w2, inc=f"{pre}_sq_{oc}")
                d.op("act", (lambda oc=oc, par=par: (lambda e: e.copy(
                    out=rawb[:, par, oc, :],
                    in_=ps[:, (oc % 2) * 512:(oc % 2 + 1) * 512])))(),
                    inc=f"{pre}_qkcp_{oc}")
                if oc < 4:
                    d.op("te", (lambda oc=oc: (lambda e: e.matmul(
                        ps[0:8, 1024:1536], self.selb[:, oc, :], sqv[:, oc % 2, :],
                        start=(oc == 0), stop=(oc == 3))))(),
                        waits=[f"{pre}_sq_{oc}", P + "mk2", P + "selwq_a"])
                    d.op("te", (lambda oc=oc: (lambda e: e.matmul(
                        ps[0:8, 1536:2048], selwq[:, oc, :], sqv[:, oc % 2, :],
                        start=(oc == 0), stop=(oc == 3))))(),
                        inc=f"{pre}_nrm_{oc}")
                else:
                    d.op("te", (lambda oc=oc: (lambda e: e.matmul(
                        ps[0:8, 2048:2560], self.selb[:, oc - 4, :], sqv[:, oc % 2, :],
                        start=(oc == 4), stop=(oc == 7))))(),
                        waits=[f"{pre}_sq_{oc}", P + "mk2"], inc=f"{pre}_nrm_{oc}")
            ftw = [pre + "_ls", P + "setup_dve"]
            if mprev:
                ftw.append(("PE", mprev["pe"]))
            jlist = ([0, 1] if qact else []) + [2, 3]
            for jj, j in enumerate(jlist):
                st = qs_st if j < 2 else ks_st
                d.op("dve", (lambda j=j, st=st, par=par: (lambda e: e.tensor_scalar(
                    tab[:, par, j, :], tab[:, par, j, :], st[:, j % 2, :],
                    None, ALU.mult)))(),
                    waits=ftw if jj == 0 else (),
                    inc=pre + "_ftab" if jj == len(jlist) - 1 else None)
            for i_oc, oc in enumerate(ocs):
                c = oc % 4
                isq = oc < 4
                tj = 0 if isq else 2
                d.op("te", (lambda oc=oc, par=par: (lambda e: e.matmul(
                    ps[:, 2560:3072], self.permt[:, :], rawb[:, par, oc, :],
                    start=True, stop=True)))(),
                    waits=[f"{pre}_qkcp_{oc}"] +
                          ([f"{pre}_rotB_{ocs[i_oc-1]}"] if i_oc >= 1 else []),
                    inc=f"{pre}_perm_{oc}")
                d.op("dve", (lambda oc=oc, tj=tj, par=par: (lambda e: e.tensor_mul(
                    tmpA[:, :], rawb[:, par, oc, :], tab[:, par, tj, :])))(),
                    waits=[f"{pre}_qkcp_{oc}", pre + "_ftab"])
                d.op("dve", (lambda tj=tj, par=par: (lambda e: e.tensor_mul(
                    tmpB[:, :], ps[:, 2560:3072], tab[:, par, tj + 1, :])))(),
                    waits=[f"{pre}_perm_{oc}"], inc=f"{pre}_rotB_{oc}")
                if isq:
                    d.op("dve", (lambda c=c, par=par: (lambda e: e.tensor_add(
                        qq[:, par, c, :], tmpA[:, :], tmpB[:, :])))(),
                        inc=f"{pre}_qq_{c}")
                else:
                    d.op("dve", (lambda c=c, s0=s0: (lambda e: e.tensor_add(
                        kBr[:, c, (s0 % RING):(s0 % RING) + 512],
                        tmpA[:, :], tmpB[:, :])))(),
                        inc=f"{pre}_kw_{c}")
            d.op("act", lambda e: e.sqrt(out=r8[:, 4, :], in_=ps[0:8, 2048:2560]),
                 waits=[f"{pre}_nrm_7"], inc=pre + "_ks1")
            d.op("dve", lambda e: e.tensor_scalar_max(r8[:, 4, :], r8[:, 4, :], 1e-12),
                 waits=[pre + "_ks1"])
            d.op("dve", lambda e: e.reciprocal(r8[:, 4, :], r8[:, 4, :]),
                 inc=pre + "_rk0")
            d.op("act", lambda e: e.mul(out=r8[:, 4, :], in_=r8[:, 4, :], mul=8.0),
                 waits=[pre + "_rk0"], inc=pre + "_rk")
            for sub in range(4):
                rc = ((s0 + sub * 128) // 128) % 12
                d.op("te", (lambda sub=sub: (lambda e: e.transpose(
                    ps[:, 3584:3592], r8[:, 4, sub * 128:(sub + 1) * 128],
                    self.idtf[0:8, 0:8])))(),
                    waits=[pre + "_rk"] +
                          ([f"{pre}_rkT_{sub-1}"] if sub >= 1 else []),
                    inc=f"{pre}_rkTm_{sub}")
                d.op("act", (lambda rc=rc: (lambda e: e.copy(
                    out=rkT[:, rc, :], in_=ps[:, 3584:3592])))(),
                    waits=[f"{pre}_rkTm_{sub}"], inc=f"{pre}_rkT_{sub}")
            if qact:
                d.op("act", lambda e: e.sqrt(out=r8[:, 0, :], in_=ps[0:8, 1024:1536]),
                     waits=[f"{pre}_nrm_3"], inc=pre + "_qs1")
                d.op("act", lambda e: e.sqrt(out=r8[:, 2, :], in_=ps[0:8, 1536:2048]),
                     inc=pre + "_qs2s")
                d.op("dve", lambda e: e.tensor_scalar_max(r8[:, 1, :], r8[:, 0, :], 1e-12),
                     waits=[pre + "_qs1"])
                d.op("dve", lambda e: e.reciprocal(r8[:, 1, :], r8[:, 1, :]),
                     inc=pre + "_rq")
                d.op("dve", lambda e: e.tensor_mul(r8[:, 3, :], r8[:, 2, :], r8[:, 1, :]),
                     waits=[pre + "_qs2s"])
                for sub in range(4):
                    d.op("dve", (lambda sub=sub: (lambda e: e.reduce_max(
                        maxqc[:, sub:sub + 1], r8[:, 3, sub * 128:(sub + 1) * 128],
                        mybir.AxisListType.X)))(),
                        inc=f"{pre}_mq_{sub}" if sub == 3 else None)
                d.op("dve", lambda e: e.tensor_scalar(
                    negc[:, :], maxqc[:, :], maxks8[:, 0:1], -8.0, ALU.mult, ALU.mult),
                    waits=[f"{pre}_mq_3", P + "mk"], inc=pre + "_negc")
                d.op("te", lambda e: e.transpose(
                    ps[0:4, 3592:3600], negc[:, :], self.idtf[0:8, 0:8]),
                    waits=[pre + "_negc", f"{pre}_rkT_3"], inc=pre + "_ngT")
                d.op("act", lambda e: e.copy(out=negTb[:, :], in_=ps[0:4, 3592:3600]),
                     waits=[pre + "_ngT"], inc=pre + "_ngTc")
                self.sp((lambda ti=ti: (lambda e: e.dma_start(
                    out=self.bounceB.rearrange("l t h -> (l t) h")[
                        l * 48 + ti * 4: l * 48 + (ti + 1) * 4, :],
                    in_=negTb[:, :])))(),
                    waits=[pre + "_ngTc"], inc=pre + "_bnc")
                self.ls((lambda ti=ti: (lambda e: e.dma_start(
                    out=negfl[:, :],
                    in_=self.bounceB.rearrange("l t h -> l (t h)")[
                        l:l + 1, ti * 32:(ti + 1) * 32])))(),
                    waits=[pre + "_bnc"], inc=pre + "_bncr")
                d.op("te", lambda e: e.matmul(
                    ps[:, 3616:3648], self.one1[:, :], negfl[:, 0:32],
                    start=True, stop=True),
                    waits=[pre + "_bncr"], inc=pre + "_bbm")
                d.op("dve", (lambda par=par: (lambda e: e.tensor_copy(
                    biasB[:, par, :], ps[:, 3616:3648])))(),
                    waits=[pre + "_bbm"], inc=pre + "_bias")
                for c in range(4):
                    d.op("te", (lambda c=c: (lambda e: e.matmul(
                        ps[:, 3072:3584], self.selbr[:, c, :], r8[:, 1, :],
                        start=True, stop=True)))(),
                        waits=[pre + "_rq", f"{pre}_qq_{c}"] +
                              ([f"{pre}_qf_{c-1}"] if c >= 1 else []),
                        inc=f"{pre}_rqb_{c}")
                    d.op("dve", (lambda c=c, par=par: (lambda e: e.tensor_mul(
                        qtb[:, par, c, :], qq[:, par, c, :], ps[:, 3072:3584])))(),
                        waits=[f"{pre}_rqb_{c}"], inc=f"{pre}_qf_{c}")
            for st in range(4):
                rc = ((s0 + st * 128) // 128) % 12
                w = [pre + "_ld", P + "sld"]
                if st >= 2:
                    w.append(f"{pre}_vcp_{st-2}")
                for ic in range(8):
                    ofs = 1 if ic < 4 else 0
                    d.op("te", (lambda st=st, ic=ic, ofs=ofs, par=par:
                                (lambda e: e.matmul(
                                    ps[:, (st % 2) * 512:(st % 2 + 1) * 512],
                                    ibf[:, par, ic, ofs + st * 128: ofs + (st + 1) * 128],
                                    wqkv_sb[:, ic, 1024:1536],
                                    start=(ic == 0), stop=(ic == 7))))(),
                        waits=w if ic == 0 else (),
                        inc=f"{pre}_vmm_{st}" if ic == 7 else None)
                for h in range(8):
                    d.op("act", (lambda st=st, h=h, rc=rc: (lambda e: e.copy(
                        out=VBr[:, rc, h, 0:64],
                        in_=ps[:, (st % 2) * 512 + h * 64:(st % 2) * 512 + (h + 1) * 64])))(),
                        waits=[f"{pre}_vmm_{st}"] if h == 0 else (),
                        inc=f"{pre}_vcp_{st}" if h == 7 else None)
                self.ls((lambda st=st, rc=rc, s0=s0: (lambda e: e.dma_start(
                    out=VBr[:, rc, :, 64:65],
                    in_=self.onesB[s0 + kvofs + st * 128: s0 + kvofs + (st + 1) * 128, :]
                    .rearrange("p (h o) -> p h o", o=1))))(),
                    inc=f"{pre}_vones_{st}")
            d.mark(pre + "_kvrdy_pe", "PE")
            d.mark(pre + "_kvrdy_act", "ACT")
            d.mark(pre + "_kvrdy_dve", "DVE")
            d.mark(pre + "_kvrdy_ls", "LS")
            kvrdy = [(pre + "_kvrdy_pe",), (pre + "_kvrdy_act",),
                     (pre + "_kvrdy_dve",), (pre + "_kvrdy_ls",)]
            kvrdy = [pre + "_kvrdy_pe", pre + "_kvrdy_act",
                     pre + "_kvrdy_dve", pre + "_kvrdy_ls"]
            if attn:
                prevo = marks[ti - 1].get("lasto")
                for qt in range(4):
                    lblp = f"{pre}_a{qt}"
                    attend(lblp, s0 + qt * 128, 128, par, qt * 128,
                           pre + "_bias", par, qt * 8, kvrdy,
                           prevo if qt == 0 else f"{pre}_a{qt-1}",
                           qt * 128, par)
                marks.setdefault(ti, {})["lasto"] = f"{pre}_a3"
                wo_res(pre, s0 + srcofs, s0 + uofs, 512, par, False,
                       [f"{pre}_a3_o_1_0", f"{pre}_a3_o_1_1",
                        f"{pre}_a3_o_1_2", f"{pre}_a3_o_1_3"], par)
            marks.setdefault(ti, {})
            marks[ti]["snap"] = d.barrier()
            marks[ti]["pe"] = d.cnt["PE"]
            marks[ti]["act"] = d.cnt["ACT"]
            marks[ti]["dve"] = d.cnt["DVE"]

        for ti in range(ntt):
            tt_body(ti)
            if ti == 1:
                pre = f"{P}tS"
                qslot = 1024 - nsp
                t1 = f"{P}t1"
                kvrdy1 = [t1 + "_kvrdy_pe", t1 + "_kvrdy_act",
                          t1 + "_kvrdy_dve", t1 + "_kvrdy_ls"]
                attend(pre + "_a", qslot, nsp, 1, qslot - 512,
                       t1 + "_bias", 1, 24, kvrdy1, None, 0, None)
                wo_res(pre, qslot + srcofs, qslot + uofs, nsp, None, True,
                       [f"{pre}_a_o_1_0", f"{pre}_a_o_1_1",
                        f"{pre}_a_o_1_2", f"{pre}_a_o_1_3"], 0)
                marks[1]["lasto"] = pre + "_a"
                marks[1]["snap"] = d.barrier()
                marks[1]["pe"] = d.cnt["PE"]
                marks[1]["act"] = d.cnt["ACT"]
                marks[1]["dve"] = d.cnt["DVE"]

    def _bpass2(self, l):
        d, abf, af = self.d, self.abf, self.af
        ps = self.ps
        W = self.W[l]
        P = f"c{l}"
        SRC2 = self.uA0sp if l == 0 else self.uA1sp
        DST2 = self.u1sp if l == 0 else self.u2sp
        subs = [(1, 1), (2, 1024), (1026, 1024), (2050, 1024)] if l == 0 else \
               [(1026, 1024), (2050, 1024)]

        b1pc = af.alloc([128, 44])
        lgc = af.alloc([128, 8])
        lbc = af.alloc([128, 8])
        b2c = af.alloc([128, 8])
        ubf2 = abf.alloc([128, 8, 1025])
        sq2 = abf.alloc([128, 2, 512])
        hn = abf.alloc([128, 8, 1024])
        h1 = abf.alloc([128, 22, 1024])
        gel = abf.alloc([128, 2, 512])
        w1k = abf.alloc([128, 2, 8, 256])
        w2res = abf.alloc([128, 22, 1024])
        vmb = abf.alloc([128, 512])
        vmrow = abf.alloc([1, 512])
        tmf = af.alloc([128, 2, 512])
        twf = af.alloc([128, 2, 512])
        basef = af.alloc([128, 2, 512])
        row = af.alloc([1, 6, 512])

        for j in range(44):
            off = j * 128 if j < 42 else (5376 if j == 42 else 5418)
            sz = 128 if j < 42 else 42
            self.ls((lambda j=j, off=off, sz=sz, W=W: (lambda e: e.dma_start(
                out=b1pc[0:sz, j:j + 1], in_=W["b1p"].rearrange("(k o) -> k o", o=1)[off:off + sz])))())
        for ic in range(8):
            self.ls((lambda ic=ic, W=W: (lambda e: e.dma_start(
                out=lgc[:, ic:ic + 1],
                in_=W["lg"].rearrange("(k o) -> k o", o=1)[ic * 128:(ic + 1) * 128])))())
            self.ls((lambda ic=ic, W=W: (lambda e: e.dma_start(
                out=lbc[:, ic:ic + 1],
                in_=W["lb"].rearrange("(k o) -> k o", o=1)[ic * 128:(ic + 1) * 128])))())
            self.ls((lambda ic=ic, W=W: (lambda e: e.dma_start(
                out=b2c[:, ic:ic + 1],
                in_=W["b2"].rearrange("(k o) -> k o", o=1)[ic * 128:(ic + 1) * 128])))())
        d.mark(P + "consts", "LS")

        import os
        p2tok = set(os.environ.get("KPARA2", "").split(","))
        for si, (us0, n) in enumerate(subs):
            pre = f"{P}s{si}"
            n2list = [(o, min(512, n - o)) for o in range(0, n, 512)]
            force_para = (n == 1)
            if force_para:
                d.paranoid = True
            for ic in range(8):
                self.ld((lambda ic=ic, us0=us0, n=n: (lambda e: e.dma_start(
                    out=ubf2[:, ic, 0:n + 1],
                    in_=SRC2[ic * 128:(ic + 1) * 128, us0 - 1:us0 + n])))())
            d.mark(pre + "_ld", "LD")
            d.paranoid = force_para or ("ln" in p2tok)
            for ni, (o2, n2) in enumerate(n2list):
                for ic in range(8):
                    ofs = 1 if ic < 4 else 0
                    d.op("act", (lambda ic=ic, ofs=ofs, o2=o2, n2=n2: (lambda e: e.activation(
                        out=sq2[:, ic % 2, 0:n2], in_=ubf2[:, ic, ofs + o2:ofs + o2 + n2],
                        func=AF.Square)))(),
                        waits=[pre + "_ld"] +
                              ([f"{pre}_st_{ni}_{ic-2}"] if ic >= 2 else
                               ([f"{pre}_stz_{ni-1}"] if ni >= 1 else [])),
                        inc=f"{pre}_sq_{ni}_{ic}")
                    d.op("te", (lambda ic=ic, ofs=ofs, o2=o2, n2=n2: (lambda e: e.matmul(
                        ps[0:1, 1024:1024 + n2], self.o128b[:, :],
                        ubf2[:, ic, ofs + o2:ofs + o2 + n2],
                        start=(ic == 0), stop=(ic == 7))))(),
                        waits=([pre + "_ld"] +
                               ([f"{pre}_mu_{ni-1}"] if ni >= 1 else [])) if ic == 0 else ())
                    d.op("te", (lambda ic=ic, n2=n2: (lambda e: e.matmul(
                        ps[0:1, 1536:1536 + n2], self.o128b[:, :], sq2[:, ic % 2, 0:n2],
                        start=(ic == 0), stop=(ic == 7))))(),
                        waits=[f"{pre}_sq_{ni}_{ic}"],
                        inc=f"{pre}_st_{ni}_{ic}" if ic < 7 else f"{pre}_stz_{ni}")
                d.val[f"{pre}_st_{ni}_6x"] = d.val[f"{pre}_stz_{ni}"]
                d.op("act", (lambda n2=n2: (lambda e: e.mul(
                    out=row[:, 0, 0:n2], in_=ps[0:1, 1024:1024 + n2], mul=1.0 / 1024)))(),
                    waits=[f"{pre}_stz_{ni}"] +
                          ([f"{pre}_ab_{ni-1}"] if ni >= 1 else []),
                    inc=f"{pre}_mu_{ni}")
                d.op("act", (lambda n2=n2: (lambda e: e.mul(
                    out=row[:, 1, 0:n2], in_=ps[0:1, 1536:1536 + n2], mul=1.0 / 1024)))())
                d.op("act", (lambda n2=n2: (lambda e: e.activation(
                    out=row[:, 2, 0:n2], in_=row[:, 0, 0:n2], func=AF.Square)))(),
                    inc=f"{pre}_ms_{ni}")
                d.op("dve", (lambda n2=n2: (lambda e: e.tensor_sub(
                    row[:, 3, 0:n2], row[:, 1, 0:n2], row[:, 2, 0:n2])))(),
                    waits=[f"{pre}_ms_{ni}"], inc=f"{pre}_var_{ni}")
                d.op("dve", (lambda n2=n2: (lambda e: e.tensor_scalar_add(
                    row[:, 3, 0:n2], row[:, 3, 0:n2], 1e-5)))(),
                    waits=[f"{pre}_var_{ni}"], inc=f"{pre}_vare_{ni}")
                d.op("act", (lambda n2=n2: (lambda e: e.sqrt(
                    out=row[:, 3, 0:n2], in_=row[:, 3, 0:n2])))(),
                    waits=[f"{pre}_vare_{ni}"], inc=f"{pre}_sd_{ni}")
                d.op("dve", (lambda n2=n2: (lambda e: e.reciprocal(
                    row[:, 4, 0:n2], row[:, 3, 0:n2])))(),
                    waits=[f"{pre}_sd_{ni}"])
                d.op("dve", (lambda n2=n2: (lambda e: e.tensor_mul(
                    row[:, 5, 0:n2], row[:, 0, 0:n2], row[:, 4, 0:n2])))(),
                    inc=f"{pre}_ab_{ni}")
                d.op("te", (lambda n2=n2: (lambda e: e.matmul(
                    ps[:, 2048:2048 + n2], self.one1f[:, :], row[:, 4, 0:n2],
                    start=True, stop=True)))(),
                    waits=[f"{pre}_ab_{ni}"] +
                          ([f"{pre}_hnz_{ni-1}"] if ni >= 1 else []),
                    inc=f"{pre}_bca_{ni}")
                d.op("te", (lambda n2=n2: (lambda e: e.matmul(
                    ps[:, 2560:2560 + n2], self.one1f[:, :], row[:, 5, 0:n2],
                    start=True, stop=True)))(), inc=f"{pre}_bcb_{ni}")
                for ic in range(8):
                    ofs = 1 if ic < 4 else 0
                    w0 = [f"{pre}_bca_{ni}"] if ic == 0 else []
                    if ic >= 2:
                        w0.append(f"{pre}_hn_{ni}_{ic-2}")
                    elif ni >= 1:
                        w0.append(f"{pre}_hn_{ni-1}_{6+ic}")
                    d.op("dve", (lambda ic=ic, ofs=ofs, o2=o2, n2=n2: (lambda e: e.tensor_mul(
                        tmf[:, ic % 2, 0:n2], ubf2[:, ic, ofs + o2:ofs + o2 + n2],
                        ps[:, 2048:2048 + n2])))(),
                        waits=w0)
                    d.op("dve", (lambda ic=ic, n2=n2: (lambda e: e.tensor_sub(
                        tmf[:, ic % 2, 0:n2], tmf[:, ic % 2, 0:n2],
                        ps[:, 2560:2560 + n2])))(),
                        waits=[f"{pre}_bcb_{ni}"] if ic == 0 else (),
                        inc=f"{pre}_t2_{ni}_{ic}")
                    d.op("act", (lambda ic=ic, o2=o2, n2=n2: (lambda e: e.activation(
                        out=hn[:, ic, o2:o2 + n2], in_=tmf[:, ic % 2, 0:n2],
                        func=AF.Identity,
                        bias=lbc[:, ic:ic + 1], scale=lgc[:, ic:ic + 1])))(),
                        waits=[f"{pre}_t2_{ni}_{ic}", P + "consts"],
                        inc=f"{pre}_hn_{ni}_{ic}")
                d.mark(f"{pre}_hnz_{ni}", "ACT")
            d.mark(pre + "_lnz", "ACT")
            d.paranoid = force_para or ("w1" in p2tok)
            d.op("dve", lambda e: e.memset(h1[:, 21, :], 0.0), inc=pre + "_h1z")
            for k in range(22):
                M = 128 if k < 21 else TAIL
                cw = 256 if k < 21 else 84
                self.ls((lambda k=k, cw=cw, W=W: (lambda e: e.dma_start(
                    out=w1k[:, k % 2, :, 0:cw].rearrange("p a b -> p (a b)")
                    .rearrange("p (a b) -> p a b", a=8) if False else
                    w1k[:, k % 2, :, 0:cw],
                    in_=W["w1p"].rearrange("(c p) n -> p c n", p=128)[
                        :, :, k * 256:k * 256 + cw])))(),
                    waits=[f"{pre}_w1mm_{k-2}"] if k >= 2 else (),
                    inc=f"{pre}_w1l_{k}")
                for ni, (o2, n2) in enumerate(n2list):
                    w = [f"{pre}_w1l_{k}", pre + "_lnz"]
                    if k >= 1 or ni >= 1:
                        pk, pn = (k, ni - 1) if ni >= 1 else (k - 1, len(n2list) - 1)
                        w.append(f"{pre}_g1_{pk}_{pn}")
                    for ic in range(8):
                        d.op("te", (lambda k=k, ic=ic, o2=o2, n2=n2, M=M:
                                    (lambda e: e.matmul(
                                        ps[0:M, 0:n2], w1k[:, k % 2, ic, 0:M],
                                        hn[:, ic, o2:o2 + n2],
                                        start=(ic == 0), stop=(ic == 7))))(),
                            waits=w if ic == 0 else ())
                    for ic in range(8):
                        d.op("te", (lambda k=k, ic=ic, o2=o2, n2=n2, M=M, cw=cw:
                                    (lambda e: e.matmul(
                                        ps[0:M, 512:512 + n2],
                                        w1k[:, k % 2, ic, cw // 2:cw // 2 + M],
                                        hn[:, ic, o2:o2 + n2],
                                        start=(ic == 0), stop=(ic == 7))))(),
                            inc=f"{pre}_gmm_{k}_{ni}" if ic == 7 else None)
                    d.val.setdefault(f"{pre}_w1mm_{k}", d.val[f"{pre}_gmm_{k}_{ni}"])
                    d.val[f"{pre}_w1mm_{k}"] = d.val[f"{pre}_gmm_{k}_{ni}"]
                    d.op("act", (lambda k=k, ni=ni, n2=n2, M=M: (lambda e: e.activation(
                        out=gel[0:M, ni % 2, 0:n2], in_=ps[0:M, 512:512 + n2],
                        func=AF.Gelu, bias=b1pc[0:M, 2 * k + 1:2 * k + 2])))(),
                        waits=[f"{pre}_gmm_{k}_{ni}", P + "consts"] +
                              ([f"{pre}_g1_{k}_{ni-1}" if ni >= 1 else
                                (f"{pre}_g1_{k-1}_{len(n2list)-1}" if k >= 1 else
                                 pre + "_h1z")]),
                        inc=f"{pre}_gg_{k}_{ni}")
                    d.op("dve", (lambda k=k, ni=ni, o2=o2, n2=n2, M=M:
                                 (lambda e: e.scalar_tensor_tensor(
                                     h1[0:M, k, o2:o2 + n2], ps[0:M, 0:n2],
                                     b1pc[0:M, 2 * k:2 * k + 1], gel[0:M, ni % 2, 0:n2],
                                     ALU.add, ALU.mult)))(),
                        waits=[f"{pre}_gg_{k}_{ni}", pre + "_h1z"],
                        inc=f"{pre}_g1_{k}_{ni}")
            d.paranoid = force_para or ("w2" in p2tok)
            for k in range(22):
                self.ls((lambda k=k, W=W: (lambda e: e.dma_start(
                    out=w2res[:, k, :],
                    in_=W["w2p"].rearrange("(c p) n -> p c n", p=128)[:, k, :])))(),
                    inc=f"{pre}_w2l_{k}" if k == 21 else None)
            for ni, (o2, n2) in enumerate(n2list):
                nvl = min(512, TB - (us0 + o2))
                self.ls((lambda us0=us0, o2=o2, nvl=nvl: (lambda e: e.dma_start(
                    out=vmrow[:, 0:nvl],
                    in_=self.vmaskB[:, us0 + o2:us0 + o2 + nvl])))(),
                    waits=[f"{pre}_vmm_{ni-1}"] if ni >= 1 else (),
                    inc=f"{pre}_vml_{ni}")
                d.op("te", (lambda n2=n2: (lambda e: e.matmul(
                    ps[:, 3072:3072 + n2], self.one1[:, :],
                    vmrow[:, 0:n2], start=True, stop=True)))(),
                    waits=[f"{pre}_g1_21_{len(n2list)-1}", f"{pre}_vml_{ni}"] +
                          ([f"{pre}_vmc_{ni-1}"] if ni >= 1 else []),
                    inc=f"{pre}_vmm_{ni}")
                d.op("dve", (lambda n2=n2: (lambda e: e.tensor_copy(
                    vmb[:, 0:n2], ps[:, 3072:3072 + n2])))(),
                    waits=[f"{pre}_vmm_{ni}"], inc=f"{pre}_vmc_{ni}")
                for oc in range(8):
                    self.ls((lambda oc=oc, us0=us0, o2=o2, n2=n2: (lambda e: e.dma_start(
                        out=basef[:, oc % 2, 0:n2],
                        in_=SRC2[oc * 128:(oc + 1) * 128,
                                 us0 + o2:us0 + o2 + n2])))(),
                        inc=f"{pre}_bs_{ni}_{oc}")
                    w = [f"{pre}_w2l_21", f"{pre}_vmc_{ni}"]
                    if oc >= 2 or ni >= 1:
                        po, pn = (oc - 2, ni) if oc >= 2 else (6 + oc, ni - 1)
                        w.append(f"{pre}_w2s_{pn}_{po}")
                    for k in range(22):
                        d.op("te", (lambda k=k, oc=oc, o2=o2, n2=n2: (lambda e: e.matmul(
                            ps[:, (oc % 2) * 512:(oc % 2) * 512 + n2],
                            w2res[:, k, oc * 128:(oc + 1) * 128],
                            h1[:, k, o2:o2 + n2],
                            start=(k == 0), stop=(k == 21))))(),
                            waits=w if k == 0 else (),
                            inc=f"{pre}_w2m_{ni}_{oc}" if k == 21 else None)
                    d.op("dve", (lambda oc=oc, ni=ni, n2=n2: (lambda e: e.scalar_tensor_tensor(
                        twf[:, oc % 2, 0:n2], ps[:, (oc % 2) * 512:(oc % 2) * 512 + n2],
                        b2c[:, oc:oc + 1], vmb[:, 0:n2],
                        ALU.add, ALU.mult)))(),
                        waits=[f"{pre}_w2m_{ni}_{oc}"], inc=f"{pre}_w2s_{ni}_{oc}")
                    d.op("dve", (lambda oc=oc, ni=ni, n2=n2: (lambda e: e.tensor_add(
                        basef[:, oc % 2, 0:n2], twf[:, oc % 2, 0:n2],
                        basef[:, oc % 2, 0:n2])))(),
                        waits=[f"{pre}_bs_{ni}_{oc}"], inc=f"{pre}_uadd_{ni}_{oc}")
                    self.sp((lambda oc=oc, ni=ni, us0=us0, o2=o2, n2=n2:
                             (lambda e: e.dma_start(
                                 out=DST2[oc * 128:(oc + 1) * 128,
                                          us0 + o2:us0 + o2 + n2],
                                 in_=basef[:, oc % 2, 0:n2])))(),
                        waits=[f"{pre}_uadd_{ni}_{oc}"])
            d.paranoid = False
            d.phase_barrier()

    def _p7(self):
        d, abf, af = self.d, self.abf, self.af
        ps = self.ps
        yb = af.alloc([128, 8, 512])
        ub = af.alloc([128, 8, 512])
        xabf = abf.alloc([128, 2, 512])
        sqb = abf.alloc([128, 2, 512])
        row = af.alloc([1, 6, 512])
        for t in range(4):
            pre = f"p7t{t}"
            for ic in range(8):
                self.ls((lambda ic=ic, t=t: (lambda e: e.dma_start(
                    out=yb[:, ic, :],
                    in_=self.ysp[ic * 128:(ic + 1) * 128,
                                 HALO + t * 512:HALO + (t + 1) * 512])))(),
                    waits=[f"p7t{t-1}_spz"] if (t >= 1 and ic == 0) else ())
                self.ls((lambda ic=ic, t=t: (lambda e: e.dma_start(
                    out=ub[:, ic, :],
                    in_=self.u2sp[ic * 128:(ic + 1) * 128,
                                  1026 + t * 512:1026 + (t + 1) * 512])))())
            d.mark(pre + "_ld", "LS")
            for ic in range(8):
                d.op("dve", (lambda ic=ic: (lambda e: e.tensor_add(
                    yb[:, ic, :], yb[:, ic, :], ub[:, ic, :])))(),
                    waits=[pre + "_ld"] if ic == 0 else (), inc=f"{pre}_w_{ic}")
                d.op("act", (lambda ic=ic: (lambda e: e.copy(
                    out=xabf[:, ic % 2, :], in_=yb[:, ic, :])))(),
                    waits=[f"{pre}_w_{ic}"] +
                          ([f"{pre}_st_{ic-2}"] if ic >= 2 else
                           ([f"p7t{t-1}_st_{6+ic}"] if t >= 1 else [])),
                    inc=f"{pre}_cb_{ic}")
                d.op("act", (lambda ic=ic: (lambda e: e.activation(
                    out=sqb[:, ic % 2, :], in_=yb[:, ic, :], func=AF.Square)))(),
                    inc=f"{pre}_sb_{ic}")
                d.op("te", (lambda ic=ic, t=t: (lambda e: e.matmul(
                    ps[0:1, 1024:1536], self.o128b[:, :], xabf[:, ic % 2, :],
                    start=(ic == 0), stop=(ic == 7))))(),
                    waits=[f"{pre}_cb_{ic}"] +
                          ([f"p7t{t-1}_ms"] if (t >= 1 and ic == 0) else []))
                d.op("te", (lambda ic=ic: (lambda e: e.matmul(
                    ps[0:1, 1536:2048], self.o128b[:, :], sqb[:, ic % 2, :],
                    start=(ic == 0), stop=(ic == 7))))(),
                    waits=[f"{pre}_sb_{ic}"], inc=f"{pre}_st_{ic}")
            d.op("act", (lambda t=t: (lambda e: e.mul(
                out=row[:, 0, :], in_=ps[0:1, 1024:1536], mul=1.0 / 1024)))(),
                waits=[f"{pre}_st_7"] + ([f"p7t{t-1}_ab"] if t >= 1 else []),
                inc=f"{pre}_mu")
            d.op("act", lambda e: e.mul(out=row[:, 1, :], in_=ps[0:1, 1536:2048],
                                        mul=1.0 / 1024))
            d.op("act", lambda e: e.activation(out=row[:, 2, :], in_=row[:, 0, :],
                                               func=AF.Square), inc=f"{pre}_ms")
            d.op("dve", lambda e: e.tensor_sub(row[:, 3, :], row[:, 1, :], row[:, 2, :]),
                 waits=[f"{pre}_ms"], inc=f"{pre}_var")
            d.op("dve", lambda e: e.tensor_scalar_add(row[:, 3, :], row[:, 3, :], 1e-5),
                 waits=[f"{pre}_var"], inc=f"{pre}_vare")
            d.op("act", lambda e: e.sqrt(out=row[:, 3, :], in_=row[:, 3, :]),
                 waits=[f"{pre}_vare"], inc=f"{pre}_sd")
            d.op("dve", lambda e: e.reciprocal(row[:, 4, :], row[:, 3, :]),
                 waits=[f"{pre}_sd"])
            d.op("dve", lambda e: e.tensor_mul(row[:, 5, :], row[:, 0, :], row[:, 4, :]),
                 inc=f"{pre}_ab")
            d.op("te", (lambda t=t: (lambda e: e.matmul(
                ps[:, 2048:2560], self.one1f[:, :], row[:, 4, :],
                start=True, stop=True)))(),
                waits=[f"{pre}_ab"] + ([f"p7t{t-1}_yz"] if t >= 1 else []),
                inc=f"{pre}_bca")
            d.op("te", (lambda t=t: (lambda e: e.matmul(
                ps[:, 2560:3072], self.one1f[:, :], row[:, 5, :],
                start=True, stop=True)))(), inc=f"{pre}_bcb")
            for ic in range(8):
                d.op("dve", (lambda ic=ic: (lambda e: e.tensor_mul(
                    yb[:, ic, :], yb[:, ic, :], ps[:, 2048:2560])))(),
                    waits=[f"{pre}_bca"] if ic == 0 else ())
                d.op("dve", (lambda ic=ic: (lambda e: e.tensor_sub(
                    yb[:, ic, :], yb[:, ic, :], ps[:, 2560:3072])))(),
                    waits=[f"{pre}_bcb"] if ic == 0 else (),
                    inc=f"{pre}_y_{ic}")
            d.mark(pre + "_yz", "DVE")
            for ic in range(8):
                self.sp((lambda ic=ic, t=t: (lambda e: e.dma_start(
                    out=self.out[ic * 128:(ic + 1) * 128, t * 512:(t + 1) * 512],
                    in_=yb[:, ic, :])))(),
                    waits=[f"{pre}_y_{ic}"])
            d.mark(pre + "_spz", "SP")


def kernel(**inputs):
    maps = host_prepare(inputs)
    b = Builder()
    nc = b.build()
    res = run_bass_kernel_spmd(nc, maps, core_ids=list(range(NCORES)))
    full = np.zeros((2, 8192, 1024), np.float32)
    for c in range(NCORES):
        bb, j = c // 4, c % 4
        full[bb, j * L:(j + 1) * L, :] = np.asarray(res.results[c]["out"]).T
    return full


# version 03 (final)
# speedup vs baseline: 2.0412x; 2.0412x over previous
import numpy as np
import ml_dtypes
import concourse.bass as bass
import concourse.mybir as mybir
from concourse.bass_utils import run_bass_kernel_spmd
from contextlib import ExitStack

F32 = mybir.dt.float32
BF16 = mybir.dt.bfloat16
AF = mybir.ActivationFunctionType
ALU = mybir.AluOpType
BFNP = ml_dtypes.bfloat16

NCORES = 8
L = 2048
D = 1024
HALO = 2049
NQ = HALO + L
NK = 8192
CAUS = NK - NQ
QT_A = [(0, 1)] + [(1 + 512 * i, 512) for i in range(8)]
TB = 3074
KVB = 4096
RING = 1536
HID = 2730
NBLK = 21
TAIL = HID - NBLK * 128
DEN_EPS = 1e-30


def _bf(a):
    return np.ascontiguousarray(np.asarray(a)).astype(BFNP)


def _f32(a):
    return np.ascontiguousarray(np.asarray(a), dtype=np.float32)


def _slice_pad(xt, lo, hi):
    n = hi - lo
    out = np.zeros((xt.shape[0], n), dtype=np.float32)
    s = max(lo, 0)
    if hi > s:
        out[:, s - lo:] = xt[:, s:hi]
    return np.ascontiguousarray(out)


def _perm_w1_cols(w1):
    a, g = w1[:, :HID], w1[:, HID:]
    blocks = []
    for k in range(NBLK):
        blocks += [a[:, k * 128:(k + 1) * 128], g[:, k * 128:(k + 1) * 128]]
    blocks += [a[:, NBLK * 128:], g[:, NBLK * 128:]]
    return np.concatenate(blocks, axis=1)


def _perm_b1(b1):
    a, g = b1[:HID], b1[HID:]
    parts = []
    for k in range(NBLK):
        parts += [a[k * 128:(k + 1) * 128], g[k * 128:(k + 1) * 128]]
    parts += [a[NBLK * 128:], g[NBLK * 128:]]
    return np.concatenate(parts)


def _rot_tables():
    d = np.arange(64)
    half = (d % 32) * 2.0
    inv_freq = 1.0 / 10000.0 ** (half / 64.0)
    t = np.arange(NK, dtype=np.float64)
    fr = t[None, :] * inv_freq[:, None]
    base = (half + 0.4 * 64.0) / (1.4 * 64.0)
    power = (t - NK // 2) / 256.0
    scale = base[:, None] ** power[None, :]
    cos, sin = np.cos(fr), np.sin(fr)
    return [_f32(np.tile(m, (2, 1))) for m in
            (cos * scale, sin * scale, cos / scale, sin / scale)]


def host_prepare(inputs):
    x0t = np.ascontiguousarray(
        np.asarray(inputs["L0_attn_x"], dtype=np.float32).transpose(0, 2, 1))
    wq = np.asarray(inputs["L0_attn_wq"], dtype=np.float32)
    wk = np.asarray(inputs["L0_attn_wk"], dtype=np.float32)
    wv = np.asarray(inputs["L0_attn_wv"], dtype=np.float32)
    wo = np.asarray(inputs["L0_attn_wo"], dtype=np.float32)

    shared = {
        "wqT": _bf(wq.transpose(2, 0, 1).reshape(1024, 1024)),
        "wkT": _bf(wk.transpose(2, 0, 1).reshape(1024, 1024)),
        "wvT": _bf(wv.transpose(2, 0, 1).reshape(1024, 1024)),
        "woT": _bf(wo.T),
        "IDTF": _f32(np.eye(128)),
        "ONE1": _bf(np.ones((1, 128))),
        "ONE1F": _f32(np.ones((1, 128))),
    }
    P = np.zeros((128, 128), dtype=np.float32)
    for dd in range(128):
        sig = dd + 32 if (dd % 64) < 32 else dd - 32
        P[sig, dd] = 1.0
    shared["PERMT"] = _bf(P)
    S = np.zeros((8, 128, 8), dtype=np.float32)
    for c in range(8):
        S[c, :, c] = 1.0
    shared["SELPA"] = _bf(S)
    S2 = np.zeros((4, 128, 8), dtype=np.float32)
    for c in range(4):
        S2[c, 0:64, 2 * c] = 1.0
        S2[c, 64:128, 2 * c + 1] = 1.0
    shared["SELB"] = _bf(S2)
    S3 = np.zeros((4, 8, 128), dtype=np.float32)
    for c in range(4):
        S3[c, 2 * c, 0:64] = 1.0
        S3[c, 2 * c + 1, 64:128] = 1.0
    shared["SELBR"] = _f32(S3)

    tq_c, tq_s, tk_c, tk_s = _rot_tables()

    for l in range(2):
        w1 = np.asarray(inputs["L0_ffn_w1"][l], np.float32)
        w2 = np.asarray(inputs["L0_ffn_w2"][l], np.float32)
        shared[f"wqkv{l}"] = _bf(inputs["L0_ffn_wqkv"][l])
        shared[f"woB{l}"] = _bf(inputs["L0_ffn_wo"][l])
        shared[f"w1p{l}"] = _bf(_perm_w1_cols(w1))
        shared[f"b1p{l}"] = _f32(_perm_b1(np.asarray(inputs["L0_ffn_b1"][l], np.float32)))
        w2pad = np.concatenate(
            [w2[k * 128:(k + 1) * 128] for k in range(NBLK)]
            + [w2[NBLK * 128:], np.zeros((128 - TAIL, 1024), np.float32)], axis=0)
        shared[f"w2p{l}"] = _bf(w2pad)
        shared[f"b2_{l}"] = _f32(inputs["L0_ffn_b2"][l])
        shared[f"lg{l}"] = _f32(inputs["L0_ffn_ln_g"][l])
        shared[f"lb{l}"] = _f32(inputs["L0_ffn_ln_b"][l])
        shared[f"qs{l}"] = _f32(inputs["L0_ffn_q_scale"][l])
        shared[f"ks{l}"] = _f32(inputs["L0_ffn_k_scale"][l])

    per_core = []
    for c in range(NCORES):
        b, j = c // 4, c % 4
        S0, E0 = j * L, (j + 1) * L
        m = dict(shared)
        m["xq"] = _slice_pad(x0t[b], E0 - NQ, E0)
        m["xk"] = _slice_pad(x0t[b], E0 - NK, E0)
        oa = np.zeros((NK, 8), dtype=np.float32)
        oa[max(NK - E0, 0):, :] = 1.0
        m["onesA"] = _bf(oa)
        ob = np.zeros((KVB, 8), dtype=np.float32)
        ob[max(2048 - S0, 0):, :] = 1.0
        m["onesB"] = _bf(ob)
        vm = np.zeros((1, TB), dtype=np.float32)
        vm[0, max(1026 - S0, 0):] = 1.0
        m["vmaskB"] = _bf(vm)

        def tsl(tab):
            lo2 = S0 - 2048
            o = np.zeros((128, KVB), dtype=np.float32)
            s = max(lo2, 0)
            o[:, s - lo2:] = tab[:, s:E0]
            return _f32(o)
        m["TQC"], m["TQS"], m["TKC"], m["TKS"] = (tsl(t) for t in (tq_c, tq_s, tk_c, tk_s))
        per_core.append(m)
    return per_core


class Dev:
    ENGS = ("te", "act", "dve", "pool", "sync")
    DEF_CTR = {"te": "PE", "act": "ACT", "dve": "DVE", "pool": "POOL", "sync": "LS"}
    CTRS = ("LD", "LS", "PE", "ACT", "DVE", "POOL", "SP")

    def __init__(self):
        self.q = {e: [] for e in self.ENGS}
        self.cnt = {k: 0 for k in self.CTRS}
        self.val = {}
        self.paranoid = False

    def op(self, eng, fn, waits=(), inc=None, ctr=None):
        if self.paranoid:
            waits = list(waits) + self.barrier()
        if ctr is None:
            ctr = self.DEF_CTR[eng]
        amt = 16 if ctr in ("LD", "LS", "SP") else 1
        self.cnt[ctr] += amt
        if inc is not None:
            assert inc not in self.val, f"dup label {inc}"
            self.val[inc] = (ctr, self.cnt[ctr])
        self.q[eng].append({"fn": fn, "waits": list(waits), "ctr": ctr, "amt": amt})

    def mark(self, label, ctr):
        assert label not in self.val
        self.val[label] = (ctr, self.cnt[ctr])

    def barrier(self):
        return [(k, self.cnt[k]) for k in self.CTRS]

    def phase_barrier(self):
        b = self.barrier()
        for e in self.ENGS:
            self.q[e].append({"fn": None, "waits": b})

    def replay(self, blk, sems):
        def resolve(w):
            return w if isinstance(w, tuple) else self.val[w]

        def run(e, name):
            for rec in self.q[name]:
                for w in rec["waits"]:
                    ctr, v = resolve(w)
                    if v > 0:
                        e.wait_ge(sems[ctr], v)
                if rec["fn"] is not None:
                    rec["fn"](e).then_inc(sems[rec["ctr"]], rec["amt"])
        blk.tensor(lambda e: run(e, "te"))
        blk.scalar(lambda e: run(e, "act"))
        blk.vector(lambda e: run(e, "dve"))
        blk.gpsimd(lambda e: run(e, "pool"))
        blk.sync(lambda e: run(e, "sync"))


class Arena:
    def __init__(self, nc, ctx, name, dtype, cols):
        self.t = ctx.enter_context(nc.sbuf_tensor(name, [128, cols], dtype))
        self.cols = cols
        self.off = 0
        self.stack = []
        self.hi = 0

    def push(self):
        self.stack.append(self.off)

    def pop(self):
        self.off = self.stack.pop()

    def alloc(self, shape):
        p = shape[0]
        n = 1
        for s in shape[1:]:
            n *= s
        off = (self.off + 15) & ~15
        self.off = off + n
        self.hi = max(self.hi, self.off)
        assert self.off <= self.cols, f"arena overflow {self.off} > {self.cols}"
        v = self.t[0:p, off:off + n]
        if len(shape) == 3:
            v = v.rearrange("p (a b) -> p a b", a=shape[1])
        elif len(shape) == 4:
            v = v.rearrange("p (a b c) -> p a b c", a=shape[1], b=shape[2])
        elif len(shape) == 5:
            v = v.rearrange("p (a b c dd) -> p a b c dd",
                            a=shape[1], b=shape[2], c=shape[3])
        else:
            assert len(shape) == 2
        return v


class Builder:
    def __init__(self, upto="p7", debug=()):
        self.upto = upto
        self.debug = set(debug)
        self.nc = bass.Bass("TRN2", target_bir_lowering=False, debug=False)
        self.d = Dev()

    def _decl(self):
        nc = self.nc

        def di(name, shape, dt):
            return nc.dram_tensor(name, shape, dt, kind="ExternalInput").ap()
        self.xq = di("xq", [D, NQ], F32)
        self.xk = di("xk", [D, NK], F32)
        self.wqT = di("wqT", [D, D], BF16)
        self.wkT = di("wkT", [D, D], BF16)
        self.wvT = di("wvT", [D, D], BF16)
        self.woT = di("woT", [D, D], BF16)
        self.onesA = di("onesA", [NK, 8], BF16)
        self.onesB = di("onesB", [KVB, 8], BF16)
        self.vmaskB = di("vmaskB", [1, TB], BF16)
        self.IDTF = di("IDTF", [128, 128], F32)
        self.ONE1 = di("ONE1", [1, 128], BF16)
        self.ONE1F = di("ONE1F", [1, 128], F32)
        self.PERMT = di("PERMT", [128, 128], BF16)
        self.SELPA = di("SELPA", [8, 128, 8], BF16)
        self.SELB = di("SELB", [4, 128, 8], BF16)
        self.SELBR = di("SELBR", [4, 8, 128], F32)
        self.TQC = di("TQC", [128, KVB], F32)
        self.TQS = di("TQS", [128, KVB], F32)
        self.TKC = di("TKC", [128, KVB], F32)
        self.TKS = di("TKS", [128, KVB], F32)
        self.W = {}
        for l in range(2):
            self.W[l] = dict(
                wqkv=di(f"wqkv{l}", [D, 1536], BF16),
                woB=di(f"woB{l}", [512, D], BF16),
                w1p=di(f"w1p{l}", [D, 2 * HID], BF16),
                b1p=di(f"b1p{l}", [2 * HID], F32),
                w2p=di(f"w2p{l}", [NBLK * 128 + 128, D], BF16),
                b2=di(f"b2_{l}", [D], F32),
                lg=di(f"lg{l}", [D], F32),
                lb=di(f"lb{l}", [D], F32),
                qs=di(f"qs{l}", [64], F32),
                ks=di(f"ks{l}", [64], F32),
            )
        self.out = nc.dram_tensor("out", [D, L], F32, kind="ExternalOutput").ap()

        def scr(name, shape, dt):
            kind = "ExternalOutput" if name in self.debug else "Internal"
            return nc.dram_tensor(name, shape, dt, kind=kind).ap()
        self.kTsp = scr("kTsp", [D, NK], BF16)
        self.Vsp = scr("Vsp", [NK, D], BF16)
        self.qTsp = scr("qTsp", [D, NQ], BF16)
        self.ysp = scr("ysp", [D, NQ], F32)
        self.uA0sp = scr("uA0sp", [D, TB], F32)
        self.u1sp = scr("u1sp", [D, TB], F32)
        self.uA1sp = scr("uA1sp", [D, TB], F32)
        self.u2sp = scr("u2sp", [D, TB], F32)
        self.bounceA = scr("bounceA", [8, 9], BF16)
        self.bounceB = scr("bounceB", [2, 48, 8], BF16)

    def ld(self, fn, waits=(), inc=None):
        self.d.op("pool", fn, waits, inc, ctr="LD")

    def ls(self, fn, waits=(), inc=None):
        self.d.op("sync", fn, waits, inc, ctr="LS")

    def sp(self, fn, waits=(), inc=None):
        self.d.op("sync", fn, waits, inc, ctr="SP")

    def build(self):
        self._decl()
        nc, d = self.nc, self.d
        with ExitStack() as top:
            E = top.enter_context
            self.abf = Arena(nc, top, "abf", BF16, 69 * 1024)
            self.af = Arena(nc, top, "af", F32, 16 * 1024)
            self.ps = E(nc.psum_tensor("ps", [128, 4096], F32))

            self.idtf = E(nc.sbuf_tensor("idtf", [128, 128], F32))
            self.one1 = E(nc.sbuf_tensor("one1", [1, 128], BF16))
            self.one1f = E(nc.sbuf_tensor("one1f", [1, 128], F32))
            self.permt = E(nc.sbuf_tensor("permt", [128, 128], BF16))
            self.selpa = E(nc.sbuf_tensor("selpa", [128, 8, 8], BF16))
            self.selb = E(nc.sbuf_tensor("selb", [128, 4, 8], BF16))
            self.selbr = E(nc.sbuf_tensor("selbr", [8, 4, 128], F32))
            self.biasq = E(nc.sbuf_tensor("biasq", [128, 8 * 9], F32))
            self.o128f = E(nc.sbuf_tensor("o128f", [128, 1], F32))
            self.o128b = E(nc.sbuf_tensor("o128b", [128, 1], BF16))

            self._p0()
            stop = {"p1": 1, "p2": 2, "p34": 3, "b0": 4, "b0f": 5,
                    "b1": 6, "b1f": 7, "p7": 8}[self.upto]
            import os
            paraset = set(os.environ.get("KPARANOID", "").split(","))

            def para(name, f):
                def g():
                    self.d.paranoid = name in paraset
                    f()
                    self.d.paranoid = False
                return g
            steps = [para("p1", self._p1), para("p2", self._p2),
                     para("p34", self._p34),
                     para("b0", lambda: self._bpass1(0)),
                     para("b0f", lambda: self._bpass2(0)),
                     para("b1", lambda: self._bpass1(1)),
                     para("b1f", lambda: self._bpass2(1)),
                     para("p7", self._p7)]
            for i, f in enumerate(steps[:stop]):
                self.abf.push()
                self.af.push()
                f()
                self.abf.pop()
                self.af.pop()
                d.phase_barrier()

            sems = {k: E(nc.semaphore("sem" + k)) for k in Dev.CTRS}
            blk = E(nc.Block())
            with nc.allow_non_contiguous_dma(reason="unit-width edge tiles"):
                d.replay(blk, sems)
            fin = dict(d.cnt)
            blk.sync(lambda e: (e.wait_ge(sems["SP"], fin["SP"]),
                                e.wait_ge(sems["LS"], fin["LS"]),
                                e.wait_ge(sems["LD"], fin["LD"]),
                                e.wait_ge(sems["DVE"], fin["DVE"]),
                                e.wait_ge(sems["ACT"], fin["ACT"]),
                                e.wait_ge(sems["PE"], fin["PE"]),
                                e.wait_ge(sems["POOL"], fin["POOL"])))
        return nc

    def _p0(self):
        for dst, src in ((self.idtf[:, :], self.IDTF),
                         (self.one1[:, :], self.ONE1), (self.one1f[:, :], self.ONE1F),
                         (self.permt[:, :], self.PERMT),
                         (self.selpa[:, :, :], self.SELPA.rearrange("c p s -> p c s")),
                         (self.selb[:, :, :], self.SELB.rearrange("c p s -> p c s")),
                         (self.selbr[:, :, :], self.SELBR.rearrange("c p s -> p c s")),
                         (self.o128f[:, :], self.ONE1F.rearrange("o p -> p o")),
                         (self.o128b[:, :], self.ONE1.rearrange("o p -> p o"))):
            self.ls((lambda dst=dst, src=src: (lambda e: e.dma_start(out=dst, in_=src)))())
        self.d.mark("const", "LS")

    def _p1(self):
        d, abf, af = self.d, self.abf, self.af
        wq_sb = abf.alloc([128, 8, 1024])
        xqb = abf.alloc([128, 2, 8, 512])
        sqv = abf.alloc([128, 2, 512])
        qtb = abf.alloc([128, 2, 8, 512])
        maxq2 = af.alloc([8, 9])
        negb = abf.alloc([8, 9])
        negbf = abf.alloc([1, 72])
        ps = self.ps

        for ic in range(8):
            self.ld((lambda ic=ic: (lambda e: e.dma_start(
                out=wq_sb[:, ic, :], in_=self.wqT[ic * 128:(ic + 1) * 128, :])))())
        d.mark("p1wq", "LD")

        for t, (q0, nq) in enumerate(QT_A):
            w = [f"p1selz_{t-2}"] if t >= 2 else []
            for ic in range(8):
                self.ld((lambda t=t, ic=ic, q0=q0, nq=nq: (lambda e: e.dma_start(
                    out=xqb[:, t % 2, ic, 0:nq],
                    in_=self.xq[ic * 128:(ic + 1) * 128, q0:q0 + nq])))(),
                    waits=w if ic == 0 else ())
            d.mark(f"p1x_{t}", "LD")
            for oc in range(8):
                w = [f"p1x_{t}", "const", "p1wq"]
                k = t * 8 + oc
                if k >= 2:
                    pt, po = divmod(k - 2, 8)
                    w.append(f"p1cp_{pt}_{po}")
                if oc == 0 and t >= 2:
                    w.append(f"p1rm_{t-2}")
                for ic in range(8):
                    d.op("te", (lambda t=t, oc=oc, ic=ic, nq=nq: (lambda e: e.matmul(
                        ps[:, (oc % 2) * 512:(oc % 2) * 512 + nq],
                        wq_sb[:, ic, oc * 128:(oc + 1) * 128],
                        xqb[:, t % 2, ic, 0:nq],
                        start=(ic == 0), stop=(ic == 7))))(),
                        waits=w if ic == 0 else (),
                        inc=f"p1mm_{t}_{oc}" if ic == 7 else None)
                w2 = [f"p1mm_{t}_{oc}"]
                if k >= 2:
                    pt, po = divmod(k - 2, 8)
                    w2.append(f"p1sel_{pt}_{po}")
                if oc == 0 and t >= 2:
                    w2.append(f"p1spq_{t-2}")
                d.op("act", (lambda t=t, oc=oc, nq=nq: (lambda e: e.activation(
                    out=sqv[:, oc % 2, 0:nq],
                    in_=ps[:, (oc % 2) * 512:(oc % 2) * 512 + nq],
                    func=AF.Square)))(), waits=w2, inc=f"p1sq_{t}_{oc}")
                d.op("act", (lambda t=t, oc=oc, nq=nq: (lambda e: e.copy(
                    out=qtb[:, t % 2, oc, 0:nq],
                    in_=ps[:, (oc % 2) * 512:(oc % 2) * 512 + nq])))(),
                    inc=f"p1cp_{t}_{oc}")
                d.op("te", (lambda t=t, oc=oc, nq=nq: (lambda e: e.matmul(
                    ps[:, 1024 + (t % 2) * 512:1024 + (t % 2) * 512 + nq][0:8, :],
                    self.selpa[:, oc, :], sqv[:, oc % 2, 0:nq],
                    start=(oc == 0), stop=(oc == 7))))(),
                    waits=[f"p1sq_{t}_{oc}"], inc=f"p1sel_{t}_{oc}")
            d.mark(f"p1selz_{t}", "PE")
            d.op("dve", (lambda t=t, nq=nq: (lambda e: e.reduce_max(
                maxq2[:, t:t + 1],
                self.ps[:, 1024 + (t % 2) * 512:1024 + (t % 2) * 512 + nq][0:8, :],
                mybir.AxisListType.X)))(),
                waits=[f"p1sel_{t}_7"], inc=f"p1rm_{t}")
            self.sp((lambda t=t, q0=q0, nq=nq: (lambda e: e.dma_start(
                out=self.qTsp.rearrange("(c p) t -> p c t", p=128)[:, :, q0:q0 + nq],
                in_=qtb[:, t % 2, :, 0:nq])))(),
                waits=[f"p1cp_{t}_7"], inc=f"p1spq_{t}")
        nt = len(QT_A)
        d.op("act", lambda e: e.sqrt(out=maxq2[:, :], in_=maxq2[:, :]),
             waits=[f"p1rm_{nt-1}"])
        d.op("act", lambda e: e.mul(out=maxq2[:, :], in_=maxq2[:, :], mul=-2.0))
        d.op("act", lambda e: e.copy(out=negb[:, :], in_=maxq2[:, :]), inc="p1negb")
        self.sp(lambda e: e.dma_start(out=self.bounceA, in_=negb[:, :]),
                waits=["p1negb"], inc="p1bnc")
        self.ls(lambda e: e.dma_start(
            out=negbf[:, :],
            in_=self.bounceA.rearrange("h q -> (h q)").rearrange("(o x) -> o x", o=1)),
            waits=["p1bnc"], inc="p1bncr")
        d.op("te", lambda e: e.matmul(self.ps[:, 0:72], self.one1[:, :],
                                      negbf[:, 0:72], start=True, stop=True),
             waits=["p1bncr", f"p1selz_{nt-1}", f"p1cp_{nt-1}_7"], inc="p1bmm")
        d.op("dve", lambda e: e.tensor_copy(self.biasq[:, :], self.ps[:, 0:72]),
             waits=["p1bmm"], inc="p1bias")

    def _p2(self):
        d, abf = self.d, self.abf
        wk_sb = abf.alloc([128, 8, 1024])
        wv_sb = abf.alloc([128, 8, 1024])
        xkb = abf.alloc([128, 2, 8, 512])
        ktb = abf.alloc([128, 2, 8, 512])
        vtb = abf.alloc([128, 2, 4, 1024])
        ps = self.ps

        for ic in range(8):
            self.ld((lambda ic=ic: (lambda e: e.dma_start(
                out=wk_sb[:, ic, :], in_=self.wkT[ic * 128:(ic + 1) * 128, :])))())
            self.ld((lambda ic=ic: (lambda e: e.dma_start(
                out=wv_sb[:, ic, :], in_=self.wvT[ic * 128:(ic + 1) * 128, :])))())
        d.mark("p2w", "LD")

        for kt in range(NK // 512):
            w = [f"p2pe_{kt-2}"] if kt >= 2 else []
            for ic in range(8):
                self.ld((lambda kt=kt, ic=ic: (lambda e: e.dma_start(
                    out=xkb[:, kt % 2, ic, :],
                    in_=self.xk[ic * 128:(ic + 1) * 128, kt * 512:(kt + 1) * 512])))(),
                    waits=w if ic == 0 else ())
            d.mark(f"p2x_{kt}", "LD")
            for oc in range(8):
                w2 = [f"p2x_{kt}", "p2w"]
                k = kt * 8 + oc
                if k >= 2:
                    pt, po = divmod(k - 2, 8)
                    w2.append(f"p2kcp_{pt}_{po}")
                for ic in range(8):
                    d.op("te", (lambda kt=kt, oc=oc, ic=ic: (lambda e: e.matmul(
                        ps[:, (oc % 2) * 512:(oc % 2 + 1) * 512],
                        wk_sb[:, ic, oc * 128:(oc + 1) * 128],
                        xkb[:, kt % 2, ic, :],
                        start=(ic == 0), stop=(ic == 7))))(),
                        waits=w2 if ic == 0 else (),
                        inc=f"p2kmm_{kt}_{oc}" if ic == 7 else None)
                w3 = [f"p2kmm_{kt}_{oc}"]
                if oc == 0 and kt >= 2:
                    w3.append(f"p2spk_{kt-2}")
                d.op("act", (lambda kt=kt, oc=oc: (lambda e: e.copy(
                    out=ktb[:, kt % 2, oc, :],
                    in_=ps[:, (oc % 2) * 512:(oc % 2 + 1) * 512])))(),
                    waits=w3, inc=f"p2kcp_{kt}_{oc}")
            self.sp((lambda kt=kt: (lambda e: e.dma_start(
                out=self.kTsp.rearrange("(c p) t -> p c t", p=128)[:, :, kt * 512:(kt + 1) * 512],
                in_=ktb[:, kt % 2, :, :])))(),
                waits=[f"p2kcp_{kt}_7"], inc=f"p2spk_{kt}")
            for st in range(4):
                for hf in range(2):
                    j = st * 2 + hf
                    w2 = []
                    if kt * 8 + j >= 2:
                        pt, pj = divmod(kt * 8 + j - 2, 8)
                        w2 = [f"p2vcp_{pt}_{pj}"]
                    for ic in range(8):
                        d.op("te", (lambda kt=kt, st=st, hf=hf, ic=ic: (lambda e: e.matmul(
                            ps[:, 1024 + hf * 512:1024 + (hf + 1) * 512],
                            xkb[:, kt % 2, ic, st * 128:(st + 1) * 128],
                            wv_sb[:, ic, hf * 512:(hf + 1) * 512],
                            start=(ic == 0), stop=(ic == 7))))(),
                            waits=w2 if ic == 0 else (),
                            inc=f"p2vmm_{kt}_{j}" if ic == 7 else None)
                    w3 = [f"p2vmm_{kt}_{j}"]
                    if j == 0 and kt >= 2:
                        w3.append(f"p2spv_{kt-2}_3")
                    d.op("act", (lambda kt=kt, st=st, hf=hf: (lambda e: e.copy(
                        out=vtb[:, kt % 2, st, hf * 512:(hf + 1) * 512],
                        in_=ps[:, 1024 + hf * 512:1024 + (hf + 1) * 512])))(),
                        waits=w3, inc=f"p2vcp_{kt}_{j}")
                self.sp((lambda kt=kt, st=st: (lambda e: e.dma_start(
                    out=self.Vsp[kt * 512 + st * 128: kt * 512 + (st + 1) * 128, :],
                    in_=vtb[:, kt % 2, st, :])))(),
                    waits=[f"p2vcp_{kt}_{st*2+1}"], inc=f"p2spv_{kt}_{st}")
            d.val[f"p2pe_{kt}"] = d.val[f"p2vmm_{kt}_7"]

    def _p34(self):
        d, abf, af = self.d, self.abf, self.af
        ps = self.ps
        self.oT = abf.alloc([128, 8, NQ])
        rcp1 = af.alloc([1, 512])
        rcpb = af.alloc([64, 512])
        rcpb2 = af.alloc([128, 512])

        def psS(hh, n):
            return ps[:, hh * 512:hh * 512 + n]

        def psSb(hh, n):
            return ps[0:64, hh * 512:hh * 512 + n]

        def psA(hh, n):
            return ps[0:65, 2048 + hh * 512:2048 + hh * 512 + n]

        def psAd(hh, n):
            return ps[64:65, 2048 + hh * 512:2048 + hh * 512 + n]

        def psAn(hh, n):
            return ps[0:64, 2048 + hh * 512:2048 + hh * 512 + n]

        abf.push()
        Kb0 = abf.alloc([128, 2, 4096])
        Vb0 = abf.alloc([128, 32, 4, 65])
        Qb0 = abf.alloc([128, 2, 1])
        Pb_s = abf.alloc([128, 4, 32])
        for g in range(4):
            lw = [f"p3so_{g-1}_3"] if g >= 1 else ["p2spk_15", "p2spv_15_3"]
            for ch in range(2):
                self.ls((lambda g=g, ch=ch: (lambda e: e.dma_start(
                    out=Kb0[:, ch, :],
                    in_=self.kTsp[g * 256 + ch * 128: g * 256 + (ch + 1) * 128, 0:4096])))(),
                    waits=lw if ch == 0 else ())
                self.ls((lambda g=g, ch=ch: (lambda e: e.dma_start(
                    out=Qb0[:, ch, :],
                    in_=self.qTsp[g * 256 + ch * 128: g * 256 + (ch + 1) * 128, 0:1])))(),
                    waits=["p1spq_0"] if (g == 0 and ch == 0) else ())
            for st in range(32):
                self.ls((lambda g=g, st=st: (lambda e: e.dma_start(
                    out=Vb0[:, st, :, 0:64],
                    in_=self.Vsp[st * 128:(st + 1) * 128,
                                 g * 256:(g + 1) * 256].rearrange("p (h e) -> p h e", h=4))))())
                self.ls((lambda st=st: (lambda e: e.dma_start(
                    out=Vb0[:, st, :, 64:65],
                    in_=self.onesA[st * 128:(st + 1) * 128, 0:4].rearrange(
                        "p (h o) -> p h o", o=1))))())
            d.mark(f"p3sld_{g}", "LS")
            for hh in range(4):
                h = g * 4 + hh
                rh = (hh % 2) * 64
                ck = hh // 2
                for kc in range(32):
                    w = []
                    if kc == 0:
                        w = [f"p3sld_{g}", "p1bias"]
                        if g >= 1:
                            w += [f"p3se_{g-1}_{hh}", f"p3so_{g-1}_{hh}"]
                    d.op("te", (lambda hh=hh, rh=rh, ck=ck, kc=kc: (lambda e: e.matmul(
                        ps[:, hh * 512 + kc: hh * 512 + kc + 1],
                        Kb0[rh:rh + 64, ck, kc * 128:(kc + 1) * 128],
                        Qb0[rh:rh + 64, ck, :],
                        start=True, stop=True, tile_position=(rh, 0))))(),
                        waits=w, inc=f"p3sm_{g}_{hh}" if kc == 31 else None)
                d.op("act", (lambda g=g, hh=hh, h=h: (lambda e: e.activation(
                    out=Pb_s[:, hh, 0:32], in_=ps[:, hh * 512:hh * 512 + 32],
                    func=AF.Exp, bias=self.biasq[:, (h // 2) * 9:(h // 2) * 9 + 1],
                    scale=0.125)))(),
                    waits=[f"p3sm_{g}_{hh}"] + ([f"p3sav_{g-1}_{hh}"] if g >= 1 else []),
                    inc=f"p3se_{g}_{hh}")
                for kc in range(32):
                    d.op("te", (lambda hh=hh, kc=kc: (lambda e: e.matmul(
                        psA(hh, 1),
                        Vb0[:, kc, hh, :], Pb_s[:, hh, kc:kc + 1],
                        start=(kc == 0), stop=(kc == 31))))(),
                        waits=[f"p3se_{g}_{hh}"] if kc == 0 else (),
                        inc=f"p3sav_{g}_{hh}" if kc == 31 else None)
                d.op("dve", (lambda hh=hh: (lambda e: e.tensor_scalar_max(
                    rcp1[:, 0:1], psAd(hh, 1), DEN_EPS)))(),
                    waits=[f"p3sav_{g}_{hh}"])
                d.op("dve", lambda e: e.reciprocal(rcp1[:, 0:1], rcp1[:, 0:1]),
                     inc=f"p3src_{g}_{hh}")
                d.op("te", (lambda hh=hh: (lambda e: e.matmul(
                    psSb(hh, 1), self.one1f[:, 0:64], rcp1[:, 0:1],
                    start=True, stop=True)))(),
                    waits=[f"p3src_{g}_{hh}", f"p3se_{g}_{hh}"], inc=f"p3sbc_{g}_{hh}")
                d.op("dve", (lambda hh=hh: (lambda e: e.tensor_copy(
                    rcpb[:, 0:1], psSb(hh, 1))))(),
                    waits=[f"p3sbc_{g}_{hh}"])
                d.op("dve", (lambda hh=hh, h=h, rh=rh: (lambda e: e.tensor_mul(
                    self.oT[rh:rh + 64, h // 2, 0:1],
                    psAn(hh, 1), rcpb[:, 0:1])))(),
                    inc=f"p3so_{g}_{hh}")
        abf.pop()

        Kb = abf.alloc([128, 3, 2, 512])
        Vb = abf.alloc([128, 3, 4, 4, 65])
        Qb = abf.alloc([128, 2, 512])
        Pb = abf.alloc([128, 2, 2, 1024])
        for ti in range(1, 9):
            q0, nq = QT_A[ti]
            ext = q0 + nq + CAUS
            assert ext % 512 == 0
            nks = ext // 512
            ngc = nks * 4
            for g in range(4):
                prevg = (f"p3o_{ti}_{g-1}_3" if g >= 1 else
                         (f"p3o_{ti-1}_3_3" if ti >= 2 else "p3so_3_3"))

                def prevexp(p, g=g, ti=ti, ngc=ngc):
                    if g >= 1:
                        return f"p3e_{ti}_{g-1}_{ngc-1}_{p}"
                    if ti >= 2:
                        return f"p3e_{ti-1}_3_{4*(8+ti-1)-1}_{p}"
                    return "p3so_3_3"
                prevm = (f"p3mm_{ti}_{g-1}" if g >= 1 else
                         (f"p3mm_{ti-1}_3" if ti >= 2 else "p3sm_3_3"))
                for ch in range(2):
                    self.ls((lambda g=g, ch=ch, q0=q0, nq=nq: (lambda e: e.dma_start(
                        out=Qb[:, ch, 0:nq],
                        in_=self.qTsp[g * 256 + ch * 128: g * 256 + (ch + 1) * 128,
                                      q0:q0 + nq])))(),
                        waits=[prevm, f"p1spq_{ti}"] if ch == 0 else ())

                def emit_av(gj, last_set):
                    ksj, kcj = divmod(gj, 4)
                    k0j = gj * 128
                    basej = q0 + CAUS - k0j
                    maskedj = basej < 127
                    for hh in range(4):
                        p = hh // 2
                        eh = hh % 2
                        avw = [f"p3e_{ti}_{g}_{gj}_{p}"]
                        if maskedj:
                            avw.append(f"p3k_{ti}_{g}_{gj}_{p}")
                        if gj == 0 and hh == 0:
                            avw.append(prevg)
                        d.op("te", (lambda hh=hh, p=p, eh=eh, ksj=ksj, kcj=kcj, nq=nq,
                                    first=(gj == 0), last=last_set:
                                    (lambda e: e.matmul(
                                        psA(hh, nq),
                                        Vb[:, ksj % 3, kcj, hh, :],
                                        Pb[:, gj % 2, p, eh * 512:eh * 512 + nq],
                                        start=first, stop=last)))(),
                            waits=avw,
                            inc=f"p3av_{ti}_{g}_{gj}_{hh}")

                for ks in range(nks):
                    lw = [f"p3av_{ti}_{g}_{(ks-3)*4+3}_3"] if ks >= 3 else []
                    for ch in range(2):
                        self.ls((lambda g=g, ks=ks, ch=ch: (lambda e: e.dma_start(
                            out=Kb[:, ks % 3, ch, :],
                            in_=self.kTsp[g * 256 + ch * 128: g * 256 + (ch + 1) * 128,
                                          ks * 512:(ks + 1) * 512])))(),
                            waits=lw if ch == 0 else ())
                    for st in range(4):
                        self.ls((lambda g=g, ks=ks, st=st: (lambda e: e.dma_start(
                            out=Vb[:, ks % 3, st, :, 0:64],
                            in_=self.Vsp[ks * 512 + st * 128: ks * 512 + (st + 1) * 128,
                                         g * 256:(g + 1) * 256].rearrange(
                                "p (h e) -> p h e", h=4))))())
                        self.ls((lambda ks=ks, st=st: (lambda e: e.dma_start(
                            out=Vb[:, ks % 3, st, :, 64:65],
                            in_=self.onesA[ks * 512 + st * 128: ks * 512 + (st + 1) * 128,
                                           0:4].rearrange("p (h o) -> p h o", o=1))))())
                    d.mark(f"p3kv_{ti}_{g}_{ks}", "LS")
                    for kc in range(4):
                        gi = ks * 4 + kc
                        k0 = gi * 128
                        base = q0 + CAUS - k0
                        masked = base < 127
                        for p in range(2):
                            sw = []
                            if kc == 0:
                                sw.append(f"p3kv_{ti}_{g}_{ks}")
                            if gi == 0:
                                sw.append("p1bias")
                                sw.append(prevexp(p))
                            if gi >= 1:
                                sw.append(f"p3e_{ti}_{g}_{gi-1}_{p}")
                            for eh in range(2):
                                hh = 2 * p + eh
                                rh = (hh % 2) * 64
                                ck = hh // 2
                                d.op("te", (lambda p=p, eh=eh, rh=rh, ck=ck, ks=ks,
                                            kc=kc, nq=nq: (lambda e: e.matmul(
                                                ps[:, p * 1024 + eh * 512:
                                                   p * 1024 + eh * 512 + nq],
                                                Kb[rh:rh + 64, ks % 3, ck,
                                                   kc * 128:(kc + 1) * 128],
                                                Qb[rh:rh + 64, ck, 0:nq],
                                                start=True, stop=True,
                                                tile_position=(rh, 0))))(),
                                    waits=sw if eh == 0 else (),
                                    inc=f"p3m_{ti}_{g}_{gi}_{p}" if eh == 1 else None)
                        for p in range(2):
                            ew = [f"p3m_{ti}_{g}_{gi}_{p}"]
                            if gi >= 2:
                                ew.append(f"p3av_{ti}_{g}_{gi-2}_{2*p+1}")
                            d.op("act", (lambda ti=ti, g=g, p=p, gi=gi: (lambda e: e.activation(
                                out=Pb[:, gi % 2, p, :],
                                in_=ps[:, p * 1024:p * 1024 + 1024],
                                func=AF.Exp,
                                bias=self.biasq[:, (g * 2 + p) * 9 + ti:
                                                (g * 2 + p) * 9 + ti + 1],
                                scale=0.125)))(),
                                waits=ew, inc=f"p3e_{ti}_{g}_{gi}_{p}")
                            if masked:
                                d.op("pool", (lambda p=p, gi=gi, base=base, nq=nq:
                                              (lambda e: e.affine_select(
                                                  out=Pb[:, gi % 2, p, :].rearrange(
                                                      "x (eh n) -> x eh n", n=512),
                                                  in_=Pb[:, gi % 2, p, :].rearrange(
                                                      "x (eh n) -> x eh n", n=512),
                                                  compare_op=ALU.is_ge, fill=0.0,
                                                  base=base, pattern=[[0, 2], [1, 512]],
                                                  channel_multiplier=-1)))(),
                                    waits=[f"p3e_{ti}_{g}_{gi}_{p}"],
                                    inc=f"p3k_{ti}_{g}_{gi}_{p}")
                        if gi >= 1:
                            emit_av(gi - 1, False)
                    d.mark(f"p3pe_{ti}_{g}_{ks}", "PE")
                emit_av(ngc - 1, True)
                d.mark(f"p3mm_{ti}_{g}", "PE")
                for hh in range(4):
                    h = g * 4 + hh
                    rh = (hh % 2) * 64
                    d.op("dve", (lambda hh=hh, nq=nq, ti=ti, g=g, ngc=ngc:
                                 (lambda e: e.tensor_scalar_max(
                                     rcp1[:, 0:nq], psAd(hh, nq), DEN_EPS)))(),
                        waits=[f"p3av_{ti}_{g}_{ngc-1}_{hh}"])
                    d.op("dve", (lambda nq=nq: (lambda e: e.reciprocal(
                        rcp1[:, 0:nq], rcp1[:, 0:nq])))(), inc=f"p3rc_{ti}_{g}_{hh}")
                    d.op("te", (lambda hh=hh, nq=nq: (lambda e: e.matmul(
                        ps[64:128, 2048 + hh * 512:2048 + hh * 512 + nq],
                        self.one1f[:, 0:64], rcp1[:, 0:nq],
                        start=True, stop=True, tile_position=(0, 64))))(),
                        waits=[f"p3rc_{ti}_{g}_{hh}"],
                        inc=f"p3bc_{ti}_{g}_{hh}")
                    d.op("dve", (lambda hh=hh, nq=nq: (lambda e: e.tensor_copy(
                        rcpb2[64:128, 0:nq],
                        ps[64:128, 2048 + hh * 512:2048 + hh * 512 + nq])))(),
                        waits=[f"p3bc_{ti}_{g}_{hh}"])
                    d.op("dve", (lambda hh=hh, h=h, rh=rh, q0=q0, nq=nq:
                                 (lambda e: e.tensor_mul(
                                     self.oT[rh:rh + 64, h // 2, q0:q0 + nq],
                                     psAn(hh, nq), rcpb2[64:128, 0:nq])))(),
                        inc=f"p3o_{ti}_{g}_{hh}" if hh == 3 else None)
        d.phase_barrier()

        woT_sb = abf.alloc([128, 8, 1024])
        xa = af.alloc([128, 8, 512])
        xabf = abf.alloc([128, 2, 512])
        sqb = abf.alloc([128, 2, 512])
        row = af.alloc([1, 6, 512])
        for ic in range(8):
            self.ls((lambda ic=ic: (lambda e: e.dma_start(
                out=woT_sb[:, ic, :], in_=self.woT[ic * 128:(ic + 1) * 128, :])))())
        d.mark("p4w", "LS")
        for t, (q0, nq) in enumerate(QT_A):
            for oc in range(8):
                self.ls((lambda t=t, oc=oc, q0=q0, nq=nq: (lambda e: e.dma_start(
                    out=xa[0:128, oc, 0:nq].bitcast(F32) if False else xa[:, oc, 0:nq],
                    in_=self.xq[oc * 128:(oc + 1) * 128, q0:q0 + nq])))(),
                    waits=[f"p4res_{t-1}"] if (oc == 0 and t >= 1) else ())
            d.mark(f"p4x_{t}", "LS")
            for oc in range(8):
                w = ["p4w", f"p4x_{t}"]
                if t * 8 + oc >= 2:
                    pt, po = divmod(t * 8 + oc - 2, 8)
                    w.append(f"p4add_{pt}_{po}")
                for ic in range(8):
                    d.op("te", (lambda t=t, oc=oc, ic=ic, q0=q0, nq=nq: (lambda e: e.matmul(
                        ps[:, (oc % 2) * 512:(oc % 2) * 512 + nq],
                        woT_sb[:, ic, oc * 128:(oc + 1) * 128],
                        self.oT[:, ic, q0:q0 + nq],
                        start=(ic == 0), stop=(ic == 7))))(),
                        waits=w if ic == 0 else (),
                        inc=f"p4mm_{t}_{oc}" if ic == 7 else None)
                d.op("dve", (lambda t=t, oc=oc, nq=nq: (lambda e: e.tensor_add(
                    xa[:, oc, 0:nq], ps[:, (oc % 2) * 512:(oc % 2) * 512 + nq],
                    xa[:, oc, 0:nq])))(),
                    waits=[f"p4mm_{t}_{oc}"], inc=f"p4add_{t}_{oc}")
                d.op("act", (lambda t=t, oc=oc, nq=nq: (lambda e: e.copy(
                    out=xabf[:, oc % 2, 0:nq], in_=xa[:, oc, 0:nq])))(),
                    waits=[f"p4add_{t}_{oc}"] +
                          ([f"p4st_{t}_{oc-2}"] if oc >= 2 else
                           ([f"p4st_{t-1}_{oc+6}"] if t >= 1 else [])),
                    inc=f"p4cb_{t}_{oc}")
                d.op("act", (lambda t=t, oc=oc, nq=nq: (lambda e: e.activation(
                    out=sqb[:, oc % 2, 0:nq], in_=xa[:, oc, 0:nq], func=AF.Square)))(),
                    inc=f"p4sb_{t}_{oc}")
                d.op("te", (lambda t=t, oc=oc, nq=nq: (lambda e: e.matmul(
                    ps[0:1, 1024:1024 + nq], self.o128b[:, :], xabf[:, oc % 2, 0:nq],
                    start=(oc == 0), stop=(oc == 7))))(),
                    waits=[f"p4cb_{t}_{oc}"] +
                          ([f"p4ms_{t-1}"] if (oc == 0 and t >= 1) else []))
                d.op("te", (lambda t=t, oc=oc, nq=nq: (lambda e: e.matmul(
                    ps[0:1, 1536:1536 + nq], self.o128b[:, :], sqb[:, oc % 2, 0:nq],
                    start=(oc == 0), stop=(oc == 7))))(),
                    waits=[f"p4sb_{t}_{oc}"],
                    inc=f"p4st_{t}_{oc}")
            d.op("act", (lambda nq=nq: (lambda e: e.mul(
                out=row[:, 0, 0:nq], in_=ps[0:1, 1024:1024 + nq], mul=1.0 / 1024)))(),
                waits=[f"p4st_{t}_7"] + ([f"p4ab_{t-1}"] if t >= 1 else []),
                inc=f"p4mu_{t}")
            d.op("act", (lambda nq=nq: (lambda e: e.mul(
                out=row[:, 1, 0:nq], in_=ps[0:1, 1536:1536 + nq], mul=1.0 / 1024)))())
            d.op("act", (lambda nq=nq: (lambda e: e.activation(
                out=row[:, 2, 0:nq], in_=row[:, 0, 0:nq], func=AF.Square)))(),
                inc=f"p4ms_{t}")
            d.op("dve", (lambda nq=nq: (lambda e: e.tensor_sub(
                row[:, 3, 0:nq], row[:, 1, 0:nq], row[:, 2, 0:nq])))(),
                waits=[f"p4ms_{t}"], inc=f"p4var_{t}")
            d.op("dve", (lambda nq=nq: (lambda e: e.tensor_scalar_add(
                row[:, 3, 0:nq], row[:, 3, 0:nq], 1e-5)))(),
                waits=[f"p4var_{t}"], inc=f"p4vare_{t}")
            d.op("act", (lambda nq=nq: (lambda e: e.sqrt(
                out=row[:, 3, 0:nq], in_=row[:, 3, 0:nq])))(),
                waits=[f"p4vare_{t}"], inc=f"p4sd_{t}")
            d.op("dve", (lambda nq=nq: (lambda e: e.reciprocal(
                row[:, 4, 0:nq], row[:, 3, 0:nq])))(),
                waits=[f"p4sd_{t}"])
            d.op("dve", (lambda nq=nq: (lambda e: e.tensor_mul(
                row[:, 5, 0:nq], row[:, 0, 0:nq], row[:, 4, 0:nq])))(),
                inc=f"p4ab_{t}")
            d.op("te", (lambda nq=nq: (lambda e: e.matmul(
                ps[:, 2048:2048 + nq], self.one1f[:, :], row[:, 4, 0:nq],
                start=True, stop=True)))(),
                waits=[f"p4ab_{t}", f"p4y_{t-1}" if t >= 1 else ("DVE", 0)],
                inc=f"p4bca_{t}")
            d.op("te", (lambda nq=nq: (lambda e: e.matmul(
                ps[:, 2560:2560 + nq], self.one1f[:, :], row[:, 5, 0:nq],
                start=True, stop=True)))(), inc=f"p4bcb_{t}")
            for oc in range(8):
                d.op("dve", (lambda oc=oc, nq=nq: (lambda e: e.tensor_mul(
                    xa[:, oc, 0:nq], xa[:, oc, 0:nq], ps[:, 2048:2048 + nq])))(),
                    waits=[f"p4bca_{t}"] if oc == 0 else ())
                d.op("dve", (lambda oc=oc, nq=nq: (lambda e: e.tensor_sub(
                    xa[:, oc, 0:nq], xa[:, oc, 0:nq], ps[:, 2560:2560 + nq])))(),
                    waits=[f"p4bcb_{t}"] if oc == 0 else (),
                    inc=f"p4yc_{t}_{oc}")
            d.mark(f"p4y_{t}", "DVE")
            self.sp((lambda t=t, q0=q0, nq=nq: (lambda e: e.dma_start(
                out=self.ysp.rearrange("(c p) t -> p c t", p=128)[:, :, q0:q0 + nq],
                in_=xa[:, :, 0:nq])))(),
                waits=[f"p4yc_{t}_7"], inc=f"p4sp_{t}")
            d.mark(f"p4res_{t}", "SP")

    def _bpass1(self, l):
        d, abf, af = self.d, self.abf, self.af
        ps = self.ps
        W = self.W[l]
        P = f"b{l}"
        kvofs = 0 if l == 0 else 1024
        ntt = 8 if l == 0 else 6
        nsp = 2 if l == 0 else 1
        SRC = self.ysp if l == 0 else self.u1sp
        srcofs = 1 if l == 0 else 2
        DST = self.uA0sp if l == 0 else self.uA1sp
        uofs = -1022 if l == 0 else 2

        wqkv_sb = abf.alloc([128, 8, 1536])
        woB_sb = abf.alloc([128, 4, 1024])
        kBr = abf.alloc([128, 4, RING])
        VBr = abf.alloc([128, 12, 8, 65])
        rkT = af.alloc([128, 12, 8])
        qs_st = af.alloc([128, 2, 1])
        ks_st = af.alloc([128, 2, 1])
        selwq = abf.alloc([128, 4, 8])
        qs2b = abf.alloc([128, 1])
        ksrow = af.alloc([1, 64])
        maxks8 = af.alloc([8, 1])
        ibf = abf.alloc([128, 2, 8, 513])
        rawb = abf.alloc([128, 2, 8, 512])
        sqv = abf.alloc([128, 8, 512])
        tab = af.alloc([128, 2, 4, 512])
        qq = af.alloc([128, 2, 4, 512])
        tmpA = af.alloc([128, 512])
        tmpB = af.alloc([128, 512])
        qtb = abf.alloc([128, 2, 4, 512])
        r8 = af.alloc([8, 6, 512])
        maxqc = af.alloc([8, 1])
        negc = af.alloc([8, 1])
        negT = af.alloc([1, 8])
        negP = af.alloc([1, 4])
        biasB = af.alloc([128, 2, 4])
        PbB = abf.alloc([128, 2, 2, 1024])
        PbS = abf.alloc([128, 4, 128])
        otb = abf.alloc([128, 2, 4, 512])
        otbs = abf.alloc([128, 4, 2])
        basef = af.alloc([128, 2, 2, 512])
        rcp1 = af.alloc([1, 128])
        rcpb = af.alloc([64, 128])
        rcpw = af.alloc([1, 512])
        rcpbw = af.alloc([64, 512])

        for ic in range(8):
            self.ls((lambda ic=ic, W=W: (lambda e: e.dma_start(
                out=wqkv_sb[:, ic, :], in_=W["wqkv"][ic * 128:(ic + 1) * 128, :])))())
        for ic in range(4):
            self.ls((lambda ic=ic, W=W: (lambda e: e.dma_start(
                out=woB_sb[:, ic, :], in_=W["woB"][ic * 128:(ic + 1) * 128, :])))())
        for rep in range(2):
            self.ls((lambda rep=rep, W=W: (lambda e: e.dma_start(
                out=qs_st[rep * 64:(rep + 1) * 64, 0, :],
                in_=W["qs"].rearrange("(k o) -> k o", o=1))))())
            self.ls((lambda rep=rep, W=W: (lambda e: e.dma_start(
                out=ks_st[rep * 64:(rep + 1) * 64, 0, :],
                in_=W["ks"].rearrange("(k o) -> k o", o=1))))())
            self.ls((lambda rep=rep, W=W: (lambda e: e.dma_start(
                out=qs_st[rep * 64:rep * 64 + 32, 1, :],
                in_=W["qs"].rearrange("(k o) -> k o", o=1)[32:64])))())
            self.ls((lambda rep=rep, W=W: (lambda e: e.dma_start(
                out=qs_st[rep * 64 + 32:rep * 64 + 64, 1, :],
                in_=W["qs"].rearrange("(k o) -> k o", o=1)[0:32])))())
            self.ls((lambda rep=rep, W=W: (lambda e: e.dma_start(
                out=ks_st[rep * 64:rep * 64 + 32, 1, :],
                in_=W["ks"].rearrange("(k o) -> k o", o=1)[32:64])))())
            self.ls((lambda rep=rep, W=W: (lambda e: e.dma_start(
                out=ks_st[rep * 64 + 32:rep * 64 + 64, 1, :],
                in_=W["ks"].rearrange("(k o) -> k o", o=1)[0:32])))())
        self.ls((lambda W=W: (lambda e: e.dma_start(
            out=ksrow[:, :], in_=W["ks"].rearrange("(o k) -> o k", o=1))))())
        d.mark(P + "sld", "LS")
        for b0 in (0, 64):
            d.op("act", (lambda b0=b0: (lambda e: e.mul(
                out=qs_st[b0:b0 + 32, 1, :], in_=qs_st[b0:b0 + 32, 1, :], mul=-1.0)))(),
                waits=[P + "sld"] if b0 == 0 else ())
            d.op("act", (lambda b0=b0: (lambda e: e.mul(
                out=ks_st[b0:b0 + 32, 1, :], in_=ks_st[b0:b0 + 32, 1, :], mul=-1.0)))())
        d.op("act", lambda e: e.activation(out=qs2b[:, :], in_=qs_st[:, 0, :],
                                           func=AF.Square), inc=P + "qs2")
        d.op("dve", lambda e: e.memset(selwq[:, :, :], 0.0), waits=[P + "qs2"],
             inc=P + "selz")
        for c in range(4):
            d.op("act", (lambda c=c: (lambda e: e.copy(
                out=selwq[0:64, c, 2 * c:2 * c + 1], in_=qs2b[0:64, :])))(),
                waits=[P + "selz"] if c == 0 else ())
            d.op("act", (lambda c=c: (lambda e: e.copy(
                out=selwq[64:128, c, 2 * c + 1:2 * c + 2], in_=qs2b[64:128, :])))())
        d.mark(P + "selwq_a", "ACT")
        d.op("dve", lambda e: e.reduce_max(maxks8[0:1, :], ksrow[:, :],
                                           mybir.AxisListType.X,
                                           apply_absolute_value=True),
             waits=[P + "sld"], inc=P + "mk1")
        d.op("te", lambda e: e.matmul(ps[0:8, 0:1], self.one1f[:, 0:8],
                                      maxks8[0:1, 0:1], start=True, stop=True),
             waits=[P + "mk1"], inc=P + "mk2")
        d.op("dve", lambda e: e.tensor_copy(maxks8[:, :], ps[0:8, 0:1]),
             waits=[P + "mk2"], inc=P + "mk")
        d.mark(P + "setup_dve", "DVE")

        marks = {}

        def attend_small(lblp, qslot, nqv, qtbpar, qcol0, biaslbl, biaspar, kvrdy):
            kbase0 = (qslot - 512) // 128 * 128
            kcs = [kbase0 + 128 * k for k in range(5)]
            for grp in range(2):
                for kci, k0 in enumerate(kcs):
                    rc = (k0 // 128) % 12
                    ring0 = k0 % RING
                    for hh in range(4):
                        h = grp * 4 + hh
                        rh = (h % 2) * 64
                        ck = h // 2
                        w = []
                        if kci == 0:
                            w = list(kvrdy) + [biaslbl]
                            if grp >= 1:
                                w.append(f"{lblp}_o_0_{hh}")
                        else:
                            w.append(f"{lblp}_e_{grp}_{kci-1}_{hh}")
                        d.op("te", (lambda hh=hh, rh=rh, ck=ck, ring0=ring0,
                                    qtbpar=qtbpar, qcol0=qcol0, nqv=nqv:
                                    (lambda e: e.matmul(
                                        ps[:, hh * 512:hh * 512 + nqv],
                                        kBr[rh:rh + 64, ck, ring0:ring0 + 128],
                                        qtb[rh:rh + 64, qtbpar, ck, qcol0:qcol0 + nqv],
                                        start=True, stop=True, tile_position=(rh, 0))))(),
                            waits=w, inc=f"{lblp}_m_{grp}_{kci}_{hh}")
                        ew = [f"{lblp}_m_{grp}_{kci}_{hh}"]
                        if kci >= 1:
                            ew.append(f"{lblp}_av_{grp}_{kci-1}_{hh}")
                        d.op("act", (lambda h=h, hh=hh, kci=kci, rc=rc, nqv=nqv,
                                     biaspar=biaspar: (lambda e: e.activation(
                                         out=PbS[:, hh, 0:nqv],
                                         in_=ps[:, hh * 512:hh * 512 + nqv],
                                         func=AF.Exp,
                                         bias=biasB[:, biaspar, h // 2:h // 2 + 1],
                                         scale=rkT[:, rc, h:h + 1])))(),
                            waits=ew, inc=f"{lblp}_e_{grp}_{kci}_{hh}")
                        avw = [f"{lblp}_e_{grp}_{kci}_{hh}"]
                        if kci == 0 or kci == 4:
                            cm = 1 if kci == 0 else -1
                            base = (k0 - qslot + 512) if kci == 0 else (qslot - k0)
                            d.op("pool", (lambda hh=hh, kci=kci, cm=cm, base=base,
                                          nqv=nqv: (lambda e: e.affine_select(
                                              out=PbS[:, hh, 0:nqv],
                                              in_=PbS[:, hh, 0:nqv],
                                              compare_op=ALU.is_ge, fill=0.0,
                                              base=base, pattern=[[-cm, nqv]],
                                              channel_multiplier=cm)))(),
                                waits=[f"{lblp}_e_{grp}_{kci}_{hh}"],
                                inc=f"{lblp}_k_{grp}_{kci}_{hh}")
                            avw.append(f"{lblp}_k_{grp}_{kci}_{hh}")
                        if kci == 0 and grp >= 1:
                            avw.append(f"{lblp}_oav_0_{hh}")
                        d.op("te", (lambda hh=hh, kci=kci, rc=rc, nqv=nqv, h=h,
                                    first=(kci == 0), last=(kci == 4):
                                    (lambda e: e.matmul(
                                        ps[0:65, 2048 + hh * 512:2048 + hh * 512 + nqv],
                                        VBr[:, rc, h, :], PbS[:, hh, 0:nqv],
                                        start=first, stop=last)))(),
                            waits=avw, inc=f"{lblp}_av_{grp}_{kci}_{hh}")
                for hh in range(4):
                    h = grp * 4 + hh
                    rh = (h % 2) * 64
                    d.op("dve", (lambda hh=hh, nqv=nqv: (lambda e: e.tensor_scalar_max(
                        rcp1[:, 0:nqv],
                        ps[64:65, 2048 + hh * 512:2048 + hh * 512 + nqv], DEN_EPS)))(),
                        waits=[f"{lblp}_av_{grp}_4_{hh}"])
                    d.op("dve", (lambda nqv=nqv: (lambda e: e.reciprocal(
                        rcp1[:, 0:nqv], rcp1[:, 0:nqv])))(), inc=f"{lblp}_rc_{grp}_{hh}")
                    d.op("te", (lambda hh=hh, nqv=nqv: (lambda e: e.matmul(
                        ps[0:64, hh * 512:hh * 512 + nqv], self.one1f[:, 0:64],
                        rcp1[:, 0:nqv], start=True, stop=True)))(),
                        waits=[f"{lblp}_rc_{grp}_{hh}", f"{lblp}_e_{grp}_4_{hh}"],
                        inc=f"{lblp}_bc_{grp}_{hh}")
                    d.op("dve", (lambda hh=hh, nqv=nqv: (lambda e: e.tensor_copy(
                        rcpb[:, 0:nqv], ps[0:64, hh * 512:hh * 512 + nqv])))(),
                        waits=[f"{lblp}_bc_{grp}_{hh}"])
                    d.op("dve", (lambda hh=hh, h=h, rh=rh, nqv=nqv:
                                 (lambda e: e.tensor_mul(
                                     otbs[rh:rh + 64, h // 2, 0:nqv],
                                     ps[0:64, 2048 + hh * 512:2048 + hh * 512 + nqv],
                                     rcpb[:, 0:nqv])))(),
                        inc=f"{lblp}_o_{grp}_{hh}")
                    d.mark(f"{lblp}_oav_{grp}_{hh}", "DVE")

        def attend512(lblp, qslot, qtbpar, biaslbl, biaspar, kvrdy, prevo, o_par):
            for grp in range(2):
                def emit_av(kcj, last):
                    k0j = qslot - 512 + 128 * kcj
                    rcj = (k0j // 128) % 12
                    for hh in range(4):
                        p, eh = hh // 2, hh % 2
                        h = grp * 4 + hh
                        avw = [f"{lblp}_e_{grp}_{kcj}_{p}", f"{lblp}_k_{grp}_{kcj}_{p}"]
                        if kcj == 0 and hh == 0:
                            if grp >= 1:
                                avw.append(f"{lblp}_o_0_3")
                            elif prevo is not None:
                                avw.append(prevo + "_o_1_3")
                        d.op("te", (lambda hh=hh, p=p, eh=eh, rcj=rcj, h=h,
                                    kcj=kcj, first=(kcj == 0), last=last:
                                    (lambda e: e.matmul(
                                        ps[0:65, 2048 + hh * 512:2048 + hh * 512 + 512],
                                        VBr[:, rcj, h, :],
                                        PbB[:, kcj % 2, p, eh * 512:(eh + 1) * 512],
                                        start=first, stop=last)))(),
                            waits=avw, inc=f"{lblp}_av_{grp}_{kcj}_{hh}")

                for kci in range(8):
                    k0 = qslot - 512 + 128 * kci
                    ring0 = k0 % RING
                    for p in range(2):
                        sw = []
                        if kci == 0:
                            sw = list(kvrdy) + [biaslbl]
                            if grp >= 1:
                                sw.append(f"{lblp}_e_0_7_{p}")
                                sw.append(f"{lblp}_o_0_3")
                        else:
                            sw.append(f"{lblp}_e_{grp}_{kci-1}_{p}")
                        for eh in range(2):
                            hh = 2 * p + eh
                            h = grp * 4 + hh
                            rh = (h % 2) * 64
                            ck = h // 2
                            d.op("te", (lambda p=p, eh=eh, rh=rh, ck=ck, ring0=ring0,
                                        qtbpar=qtbpar: (lambda e: e.matmul(
                                            ps[:, p * 1024 + eh * 512:
                                               p * 1024 + (eh + 1) * 512],
                                            kBr[rh:rh + 64, ck, ring0:ring0 + 128],
                                            qtb[rh:rh + 64, qtbpar, ck, :],
                                            start=True, stop=True,
                                            tile_position=(rh, 0))))(),
                                waits=sw if eh == 0 else (),
                                inc=f"{lblp}_m_{grp}_{kci}_{p}" if eh == 1 else None)
                    for p in range(2):
                        k0p = qslot - 512 + 128 * kci
                        rc = (k0p // 128) % 12
                        hsc = grp * 4 + 2 * p
                        ew = [f"{lblp}_m_{grp}_{kci}_{p}"]
                        if kci >= 2:
                            ew.append(f"{lblp}_av_{grp}_{kci-2}_{2*p+1}")
                        for eh in range(2):
                            h = grp * 4 + 2 * p + eh
                            d.op("act", (lambda p=p, eh=eh, h=h, kci=kci, rc=rc:
                                         (lambda e: e.activation(
                                             out=PbB[:, kci % 2, p,
                                                     eh * 512:(eh + 1) * 512],
                                             in_=ps[:, p * 1024 + eh * 512:
                                                    p * 1024 + (eh + 1) * 512],
                                             func=AF.Exp,
                                             bias=biasB[:, biaspar,
                                                        h // 2:h // 2 + 1],
                                             scale=rkT[:, rc, h:h + 1])))(),
                                waits=ew if eh == 0 else (),
                                inc=f"{lblp}_e_{grp}_{kci}_{p}" if eh == 1 else None)
                        anti = kci < 4
                        cm = 1 if anti else -1
                        base = (128 * kci) if anti else (512 - 128 * kci)
                        d.op("pool", (lambda p=p, kci=kci, cm=cm, base=base:
                                      (lambda e: e.affine_select(
                                          out=PbB[:, kci % 2, p, :].rearrange(
                                              "x (eh n) -> x eh n", n=512),
                                          in_=PbB[:, kci % 2, p, :].rearrange(
                                              "x (eh n) -> x eh n", n=512),
                                          compare_op=ALU.is_ge, fill=0.0,
                                          base=base, pattern=[[0, 2], [-cm, 512]],
                                          channel_multiplier=cm)))(),
                            waits=[f"{lblp}_e_{grp}_{kci}_{p}"],
                            inc=f"{lblp}_k_{grp}_{kci}_{p}")
                    if kci >= 1:
                        emit_av(kci - 1, False)
                emit_av(7, True)
                for hh in range(4):
                    h = grp * 4 + hh
                    rh = (h % 2) * 64
                    d.op("dve", (lambda hh=hh: (lambda e: e.tensor_scalar_max(
                        rcpw[:, :],
                        ps[64:65, 2048 + hh * 512:2048 + hh * 512 + 512], DEN_EPS)))(),
                        waits=[f"{lblp}_av_{grp}_7_{hh}"])
                    d.op("dve", lambda e: e.reciprocal(rcpw[:, :], rcpw[:, :]),
                         inc=f"{lblp}_rc_{grp}_{hh}")
                    d.op("te", (lambda hh=hh: (lambda e: e.matmul(
                        ps[0:64, hh * 512:(hh + 1) * 512], self.one1f[:, 0:64],
                        rcpw[:, :], start=True, stop=True)))(),
                        waits=[f"{lblp}_rc_{grp}_{hh}", f"{lblp}_e_{grp}_7_{hh//2}",
                               f"{lblp}_k_{grp}_7_{hh//2}"],
                        inc=f"{lblp}_bc_{grp}_{hh}")
                    d.op("dve", (lambda hh=hh: (lambda e: e.tensor_copy(
                        rcpbw[:, :], ps[0:64, hh * 512:(hh + 1) * 512])))(),
                        waits=[f"{lblp}_bc_{grp}_{hh}"])
                    d.op("dve", (lambda hh=hh, h=h, rh=rh, o_par=o_par:
                                 (lambda e: e.tensor_mul(
                                     otb[rh:rh + 64, o_par, h // 2, :],
                                     ps[0:64, 2048 + hh * 512:2048 + hh * 512 + 512],
                                     rcpbw[:, :])))(),
                        inc=f"{lblp}_o_{grp}_{hh}")

        def wo_res(pre, src_c0, dst_c0, nqv, opar, use_otbs, owaits, bpar):
            for oc in range(8):
                self.ls((lambda oc=oc, src_c0=src_c0, nqv=nqv, bpar=bpar:
                         (lambda e: e.dma_start(
                             out=basef[:, bpar, oc % 2, 0:nqv],
                             in_=SRC[oc * 128:(oc + 1) * 128, src_c0:src_c0 + nqv])))(),
                    inc=f"{pre}_base_{oc}")
                w = list(owaits) if oc == 0 else []
                if oc >= 2:
                    w.append(f"{pre}_res_{oc-2}")
                for ic in range(4):
                    d.op("te", (lambda oc=oc, ic=ic, opar=opar, nqv=nqv,
                                use_otbs=use_otbs: (lambda e: e.matmul(
                                    ps[:, (oc % 2) * 512:(oc % 2) * 512 + nqv],
                                    woB_sb[:, ic, oc * 128:(oc + 1) * 128],
                                    otbs[:, ic, 0:nqv] if use_otbs
                                    else otb[:, opar, ic, 0:nqv],
                                    start=(ic == 0), stop=(ic == 3))))(),
                        waits=w if ic == 0 else (),
                        inc=f"{pre}_wmm_{oc}" if ic == 3 else None)
                d.op("dve", (lambda oc=oc, nqv=nqv, bpar=bpar: (lambda e: e.tensor_add(
                    basef[:, bpar, oc % 2, 0:nqv],
                    ps[:, (oc % 2) * 512:(oc % 2) * 512 + nqv],
                    basef[:, bpar, oc % 2, 0:nqv])))(),
                    waits=[f"{pre}_wmm_{oc}", f"{pre}_base_{oc}"], inc=f"{pre}_res_{oc}")
                self.sp((lambda oc=oc, dst_c0=dst_c0, nqv=nqv, bpar=bpar:
                         (lambda e: e.dma_start(
                             out=DST[oc * 128:(oc + 1) * 128, dst_c0:dst_c0 + nqv],
                             in_=basef[:, bpar, oc % 2, 0:nqv])))(),
                    waits=[f"{pre}_res_{oc}"],
                    inc=f"{pre}_sp_{oc}" if oc == 7 else None)

        def tt_body(ti):
            pre = f"{P}t{ti}"
            s0 = 512 * ti
            qact = ti >= 1
            attn = ti >= 2
            par = ti % 2
            if ti >= 2:
                snap = marks[ti - 2]["snap"]
                for eng in Dev.ENGS:
                    d.q[eng].append({"fn": None, "waits": snap})
            mprev = marks.get(ti - 1)
            pw = [("PE", mprev["pe"])] if mprev else []
            for ic in range(8):
                self.ld((lambda ic=ic, s0=s0: (lambda e: e.dma_start(
                    out=ibf[:, ti % 2, ic, :],
                    in_=SRC[ic * 128:(ic + 1) * 128,
                            s0 + srcofs - 1: s0 + srcofs + 512])))(),
                    waits=pw if ic == 0 else ())
            for j, T in enumerate((self.TQC, self.TQS, self.TKC, self.TKS)):
                if j < 2 and not qact:
                    continue
                self.ls((lambda j=j, T=T, s0=s0: (lambda e: e.dma_start(
                    out=tab[:, ti % 2, j, :],
                    in_=T[:, s0 + kvofs: s0 + kvofs + 512])))())
            d.mark(pre + "_ld", "LD")
            d.mark(pre + "_ls", "LS")
            ocs = (list(range(4)) if qact else []) + list(range(4, 8))
            first_pe = True
            first_act = True
            for oc in ocs:
                w = [pre + "_ld", P + "sld"]
                if first_pe:
                    if mprev:
                        w += [("ACT", mprev["act"]), ("DVE", mprev["dve"])]
                    first_pe = False
                prevs = [o for o in ocs if o < oc]
                if len(prevs) >= 2:
                    w.append(f"{pre}_qkcp_{prevs[-2]}")
                for ic in range(8):
                    ofs = 1 if ic < 4 else 0
                    d.op("te", (lambda oc=oc, ic=ic, ofs=ofs, par=par:
                                (lambda e: e.matmul(
                                    ps[:, (oc % 2) * 512:(oc % 2 + 1) * 512],
                                    wqkv_sb[:, ic, oc * 128:(oc + 1) * 128],
                                    ibf[:, par, ic, ofs:ofs + 512],
                                    start=(ic == 0), stop=(ic == 7))))(),
                        waits=w if ic == 0 else (),
                        inc=f"{pre}_qkmm_{oc}" if ic == 7 else None)
                w2 = [f"{pre}_qkmm_{oc}"]
                if first_act:
                    if mprev:
                        w2.append(("PE", mprev["pe"]))
                    first_act = False
                d.op("act", (lambda oc=oc: (lambda e: e.activation(
                    out=sqv[:, oc, :],
                    in_=ps[:, (oc % 2) * 512:(oc % 2 + 1) * 512],
                    func=AF.Square)))(), waits=w2, inc=f"{pre}_sq_{oc}")
                d.op("act", (lambda oc=oc, par=par: (lambda e: e.copy(
                    out=rawb[:, par, oc, :],
                    in_=ps[:, (oc % 2) * 512:(oc % 2 + 1) * 512])))(),
                    inc=f"{pre}_qkcp_{oc}")
            for oc in ocs:
                if oc < 4:
                    d.op("te", (lambda oc=oc: (lambda e: e.matmul(
                        ps[0:8, 1024:1536], self.selb[:, oc, :], sqv[:, oc, :],
                        start=(oc == 0), stop=(oc == 3))))(),
                        waits=[f"{pre}_sq_{oc}", P + "mk2", P + "selwq_a"])
                    d.op("te", (lambda oc=oc: (lambda e: e.matmul(
                        ps[0:8, 1536:2048], selwq[:, oc, :], sqv[:, oc, :],
                        start=(oc == 0), stop=(oc == 3))))(),
                        inc=f"{pre}_nrm_{oc}")
                else:
                    d.op("te", (lambda oc=oc: (lambda e: e.matmul(
                        ps[0:8, 2048:2560], self.selb[:, oc - 4, :], sqv[:, oc, :],
                        start=(oc == 4), stop=(oc == 7))))(),
                        waits=[f"{pre}_sq_{oc}", P + "mk2"], inc=f"{pre}_nrm_{oc}")
            ftw = [pre + "_ls", P + "setup_dve"]
            if mprev:
                ftw.append(("PE", mprev["pe"]))
            jlist = ([0, 1] if qact else []) + [2, 3]
            for jj, j in enumerate(jlist):
                st = qs_st if j < 2 else ks_st
                d.op("dve", (lambda j=j, st=st, par=par: (lambda e: e.tensor_scalar(
                    tab[:, par, j, :], tab[:, par, j, :], st[:, j % 2, :],
                    None, ALU.mult)))(),
                    waits=ftw if jj == 0 else (),
                    inc=pre + "_ftab" if jj == len(jlist) - 1 else None)
            for i_oc, oc in enumerate(ocs):
                c = oc % 4
                isq = oc < 4
                tj = 0 if isq else 2
                d.op("te", (lambda oc=oc, par=par: (lambda e: e.matmul(
                    ps[:, 2560:3072], self.permt[:, :], rawb[:, par, oc, :],
                    start=True, stop=True)))(),
                    waits=[f"{pre}_qkcp_{oc}"] +
                          ([f"{pre}_rotB_{ocs[i_oc-1]}"] if i_oc >= 1 else []),
                    inc=f"{pre}_perm_{oc}")
                d.op("dve", (lambda oc=oc, tj=tj, par=par: (lambda e: e.tensor_mul(
                    tmpA[:, :], rawb[:, par, oc, :], tab[:, par, tj, :])))(),
                    waits=[f"{pre}_qkcp_{oc}", pre + "_ftab"])
                d.op("dve", (lambda tj=tj, par=par: (lambda e: e.tensor_mul(
                    tmpB[:, :], ps[:, 2560:3072], tab[:, par, tj + 1, :])))(),
                    waits=[f"{pre}_perm_{oc}"], inc=f"{pre}_rotB_{oc}")
                if isq:
                    d.op("dve", (lambda c=c, par=par: (lambda e: e.tensor_add(
                        qq[:, par, c, :], tmpA[:, :], tmpB[:, :])))(),
                        inc=f"{pre}_qq_{c}")
                else:
                    d.op("dve", (lambda c=c, s0=s0: (lambda e: e.tensor_add(
                        kBr[:, c, (s0 % RING):(s0 % RING) + 512],
                        tmpA[:, :], tmpB[:, :])))(),
                        inc=f"{pre}_kw_{c}")
            d.op("act", lambda e: e.sqrt(out=r8[:, 4, :], in_=ps[0:8, 2048:2560]),
                 waits=[f"{pre}_nrm_7"], inc=pre + "_ks1")
            d.op("dve", lambda e: e.tensor_scalar_max(r8[:, 4, :], r8[:, 4, :], 1e-12),
                 waits=[pre + "_ks1"])
            d.op("dve", lambda e: e.reciprocal(r8[:, 4, :], r8[:, 4, :]),
                 inc=pre + "_rk0")
            d.op("act", lambda e: e.mul(out=r8[:, 4, :], in_=r8[:, 4, :], mul=8.0),
                 waits=[pre + "_rk0"], inc=pre + "_rk")
            for sub in range(4):
                rc = ((s0 + sub * 128) // 128) % 12
                d.op("te", (lambda sub=sub: (lambda e: e.transpose(
                    ps[:, 3584:3592], r8[:, 4, sub * 128:(sub + 1) * 128],
                    self.idtf[0:8, 0:8])))(),
                    waits=[pre + "_rk"] +
                          ([f"{pre}_rkT_{sub-1}"] if sub >= 1 else []),
                    inc=f"{pre}_rkTm_{sub}")
                d.op("act", (lambda rc=rc: (lambda e: e.copy(
                    out=rkT[:, rc, :], in_=ps[:, 3584:3592])))(),
                    waits=[f"{pre}_rkTm_{sub}"], inc=f"{pre}_rkT_{sub}")
            if qact:
                d.op("act", lambda e: e.sqrt(out=r8[:, 0, :], in_=ps[0:8, 1024:1536]),
                     waits=[f"{pre}_nrm_3"], inc=pre + "_qs1")
                d.op("act", lambda e: e.sqrt(out=r8[:, 2, :], in_=ps[0:8, 1536:2048]),
                     inc=pre + "_qs2s")
                d.op("dve", lambda e: e.tensor_scalar_max(r8[:, 1, :], r8[:, 0, :], 1e-12),
                     waits=[pre + "_qs1"])
                d.op("dve", lambda e: e.reciprocal(r8[:, 1, :], r8[:, 1, :]),
                     inc=pre + "_rq")
                d.op("dve", lambda e: e.tensor_mul(r8[:, 3, :], r8[:, 2, :], r8[:, 1, :]),
                     waits=[pre + "_qs2s"])
                d.op("dve", lambda e: e.reduce_max(
                    maxqc[:, 0:1], r8[:, 3, :], mybir.AxisListType.X),
                    inc=f"{pre}_mq_3")
                d.op("dve", lambda e: e.tensor_scalar(
                    negc[:, :], maxqc[:, :], maxks8[:, 0:1], -8.0, ALU.mult, ALU.mult),
                    waits=[f"{pre}_mq_3", P + "mk"], inc=pre + "_negc")
                d.op("te", lambda e: e.transpose(
                    ps[0:1, 3592:3600], negc[:, :], self.idtf[0:8, 0:8]),
                    waits=[pre + "_negc", f"{pre}_rkT_3"], inc=pre + "_ngT")
                d.op("act", lambda e: e.copy(out=negT[:, :], in_=ps[0:1, 3592:3600]),
                     waits=[pre + "_ngT"], inc=pre + "_ngTc")
                d.op("dve", lambda e: e.tensor_tensor(
                    negP[:, :],
                    negT.rearrange("o (a b) -> o a b", b=2)[:, :, 0],
                    negT.rearrange("o (a b) -> o a b", b=2)[:, :, 1],
                    ALU.min),
                    waits=[pre + "_ngTc"], inc=pre + "_ngP")
                d.op("te", lambda e: e.matmul(
                    ps[:, 3616:3620], self.one1f[:, :], negP[:, 0:4],
                    start=True, stop=True),
                    waits=[pre + "_ngP"], inc=pre + "_bbm")
                d.op("dve", (lambda par=par: (lambda e: e.tensor_copy(
                    biasB[:, par, :], ps[:, 3616:3620])))(),
                    waits=[pre + "_bbm"], inc=pre + "_bias")
                for c in range(4):
                    d.op("te", (lambda c=c: (lambda e: e.matmul(
                        ps[:, 3072:3584], self.selbr[:, c, :], r8[:, 1, :],
                        start=True, stop=True)))(),
                        waits=[pre + "_rq", f"{pre}_qq_{c}"] +
                              ([f"{pre}_qf_{c-1}"] if c >= 1 else []),
                        inc=f"{pre}_rqb_{c}")
                    d.op("dve", (lambda c=c, par=par: (lambda e: e.tensor_mul(
                        qtb[:, par, c, :], qq[:, par, c, :], ps[:, 3072:3584])))(),
                        waits=[f"{pre}_rqb_{c}"], inc=f"{pre}_qf_{c}")
            for st in range(4):
                rc = ((s0 + st * 128) // 128) % 12
                w = [pre + "_ld", P + "sld"]
                if st >= 2:
                    w.append(f"{pre}_vcp_{st-2}")
                for ic in range(8):
                    ofs = 1 if ic < 4 else 0
                    d.op("te", (lambda st=st, ic=ic, ofs=ofs, par=par:
                                (lambda e: e.matmul(
                                    ps[:, (st % 2) * 512:(st % 2 + 1) * 512],
                                    ibf[:, par, ic, ofs + st * 128: ofs + (st + 1) * 128],
                                    wqkv_sb[:, ic, 1024:1536],
                                    start=(ic == 0), stop=(ic == 7))))(),
                        waits=w if ic == 0 else (),
                        inc=f"{pre}_vmm_{st}" if ic == 7 else None)
                for h in range(8):
                    d.op("act", (lambda st=st, h=h, rc=rc: (lambda e: e.copy(
                        out=VBr[:, rc, h, 0:64],
                        in_=ps[:, (st % 2) * 512 + h * 64:(st % 2) * 512 + (h + 1) * 64])))(),
                        waits=[f"{pre}_vmm_{st}"] if h == 0 else (),
                        inc=f"{pre}_vcp_{st}" if h == 7 else None)
                self.ls((lambda st=st, rc=rc, s0=s0: (lambda e: e.dma_start(
                    out=VBr[:, rc, :, 64:65],
                    in_=self.onesB[s0 + kvofs + st * 128: s0 + kvofs + (st + 1) * 128, :]
                    .rearrange("p (h o) -> p h o", o=1))))(),
                    inc=f"{pre}_vones_{st}")
            d.mark(pre + "_kvrdy_pe", "PE")
            d.mark(pre + "_kvrdy_act", "ACT")
            d.mark(pre + "_kvrdy_dve", "DVE")
            d.mark(pre + "_kvrdy_ls", "LS")
            kvrdy = [(pre + "_kvrdy_pe",), (pre + "_kvrdy_act",),
                     (pre + "_kvrdy_dve",), (pre + "_kvrdy_ls",)]
            kvrdy = [pre + "_kvrdy_pe", pre + "_kvrdy_act",
                     pre + "_kvrdy_dve", pre + "_kvrdy_ls"]
            if attn:
                prevo = marks[ti - 1].get("lasto")
                attend512(f"{pre}_a", s0, par, pre + "_bias", par, kvrdy, prevo, par)
                marks.setdefault(ti, {})["lasto"] = f"{pre}_a"
                wo_res(pre, s0 + srcofs, s0 + uofs, 512, par, False,
                       [f"{pre}_a_o_1_0", f"{pre}_a_o_1_1",
                        f"{pre}_a_o_1_2", f"{pre}_a_o_1_3"], par)
            marks.setdefault(ti, {})
            marks[ti]["snap"] = d.barrier()
            marks[ti]["pe"] = d.cnt["PE"]
            marks[ti]["act"] = d.cnt["ACT"]
            marks[ti]["dve"] = d.cnt["DVE"]

        for ti in range(ntt):
            tt_body(ti)
            if ti == 1:
                pre = f"{P}tS"
                qslot = 1024 - nsp
                t1 = f"{P}t1"
                kvrdy1 = [t1 + "_kvrdy_pe", t1 + "_kvrdy_act",
                          t1 + "_kvrdy_dve", t1 + "_kvrdy_ls"]
                attend_small(pre + "_a", qslot, nsp, 1, qslot - 512,
                             t1 + "_bias", 1, kvrdy1)
                wo_res(pre, qslot + srcofs, qslot + uofs, nsp, None, True,
                       [f"{pre}_a_o_1_0", f"{pre}_a_o_1_1",
                        f"{pre}_a_o_1_2", f"{pre}_a_o_1_3"], 0)
                marks[1]["lasto"] = pre + "_a"
                marks[1]["snap"] = d.barrier()
                marks[1]["pe"] = d.cnt["PE"]
                marks[1]["act"] = d.cnt["ACT"]
                marks[1]["dve"] = d.cnt["DVE"]
                marks[1]["pe"] = d.cnt["PE"]
                marks[1]["act"] = d.cnt["ACT"]
                marks[1]["dve"] = d.cnt["DVE"]

    def _bpass2(self, l):
        d, abf, af = self.d, self.abf, self.af
        ps = self.ps
        W = self.W[l]
        P = f"c{l}"
        SRC2 = self.uA0sp if l == 0 else self.uA1sp
        DST2 = self.u1sp if l == 0 else self.u2sp
        subs = [(1, 1), (2, 1024), (1026, 1024), (2050, 1024)] if l == 0 else \
               [(1026, 1024), (2050, 1024)]

        b1pc = af.alloc([128, 44])
        lgc = af.alloc([128, 8])
        lbc = af.alloc([128, 8])
        b2c = af.alloc([128, 8])
        ubf2 = abf.alloc([128, 8, 1026])
        sq2 = abf.alloc([128, 2, 512])
        hn = abf.alloc([128, 8, 1025])
        h1 = abf.alloc([128, 22, 1025])
        gel = abf.alloc([128, 2, 512])
        w1k = abf.alloc([128, 2, 8, 256])
        w2res = abf.alloc([128, 22, 1024])
        vmb = abf.alloc([128, 512])
        vmrow = abf.alloc([1, 512])
        tmf = af.alloc([128, 2, 512])
        twf = af.alloc([128, 2, 512])
        basef = af.alloc([128, 2, 512])
        row = af.alloc([1, 6, 512])

        for j in range(44):
            off = j * 128 if j < 42 else (5376 if j == 42 else 5418)
            sz = 128 if j < 42 else 42
            self.ls((lambda j=j, off=off, sz=sz, W=W: (lambda e: e.dma_start(
                out=b1pc[0:sz, j:j + 1], in_=W["b1p"].rearrange("(k o) -> k o", o=1)[off:off + sz])))())
        for ic in range(8):
            self.ls((lambda ic=ic, W=W: (lambda e: e.dma_start(
                out=lgc[:, ic:ic + 1],
                in_=W["lg"].rearrange("(k o) -> k o", o=1)[ic * 128:(ic + 1) * 128])))())
            self.ls((lambda ic=ic, W=W: (lambda e: e.dma_start(
                out=lbc[:, ic:ic + 1],
                in_=W["lb"].rearrange("(k o) -> k o", o=1)[ic * 128:(ic + 1) * 128])))())
            self.ls((lambda ic=ic, W=W: (lambda e: e.dma_start(
                out=b2c[:, ic:ic + 1],
                in_=W["b2"].rearrange("(k o) -> k o", o=1)[ic * 128:(ic + 1) * 128])))())
        d.mark(P + "consts", "LS")

        import os
        p2tok = set(os.environ.get("KPARA2", "").split(","))
        for si, (us0, n) in enumerate(subs):
            pre = f"{P}s{si}"
            n2list = [(o, min(512, n - o)) for o in range(0, n, 512)]
            force_para = (n == 1)
            if force_para:
                d.paranoid = True
            for ic in range(8):
                self.ld((lambda ic=ic, us0=us0, n=n: (lambda e: e.dma_start(
                    out=ubf2[:, ic, 0:n + 1],
                    in_=SRC2[ic * 128:(ic + 1) * 128, us0 - 1:us0 + n])))())
            d.mark(pre + "_ld", "LD")
            d.paranoid = force_para or ("ln" in p2tok)
            for ni, (o2, n2) in enumerate(n2list):
                for ic in range(8):
                    ofs = 1 if ic < 4 else 0
                    d.op("act", (lambda ic=ic, ofs=ofs, o2=o2, n2=n2: (lambda e: e.activation(
                        out=sq2[:, ic % 2, 0:n2], in_=ubf2[:, ic, ofs + o2:ofs + o2 + n2],
                        func=AF.Square)))(),
                        waits=[pre + "_ld"] +
                              ([f"{pre}_st_{ni}_{ic-2}"] if ic >= 2 else
                               ([f"{pre}_stz_{ni-1}"] if ni >= 1 else [])),
                        inc=f"{pre}_sq_{ni}_{ic}")
                    d.op("te", (lambda ic=ic, ofs=ofs, o2=o2, n2=n2: (lambda e: e.matmul(
                        ps[0:1, 1024:1024 + n2], self.o128b[:, :],
                        ubf2[:, ic, ofs + o2:ofs + o2 + n2],
                        start=(ic == 0), stop=(ic == 7))))(),
                        waits=([pre + "_ld"] +
                               ([f"{pre}_mu_{ni-1}"] if ni >= 1 else [])) if ic == 0 else ())
                    d.op("te", (lambda ic=ic, n2=n2: (lambda e: e.matmul(
                        ps[0:1, 1536:1536 + n2], self.o128b[:, :], sq2[:, ic % 2, 0:n2],
                        start=(ic == 0), stop=(ic == 7))))(),
                        waits=[f"{pre}_sq_{ni}_{ic}"],
                        inc=f"{pre}_st_{ni}_{ic}" if ic < 7 else f"{pre}_stz_{ni}")
                d.val[f"{pre}_st_{ni}_6x"] = d.val[f"{pre}_stz_{ni}"]
                d.op("act", (lambda n2=n2: (lambda e: e.mul(
                    out=row[:, 0, 0:n2], in_=ps[0:1, 1024:1024 + n2], mul=1.0 / 1024)))(),
                    waits=[f"{pre}_stz_{ni}"] +
                          ([f"{pre}_ab_{ni-1}"] if ni >= 1 else []),
                    inc=f"{pre}_mu_{ni}")
                d.op("act", (lambda n2=n2: (lambda e: e.mul(
                    out=row[:, 1, 0:n2], in_=ps[0:1, 1536:1536 + n2], mul=1.0 / 1024)))())
                d.op("act", (lambda n2=n2: (lambda e: e.activation(
                    out=row[:, 2, 0:n2], in_=row[:, 0, 0:n2], func=AF.Square)))(),
                    inc=f"{pre}_ms_{ni}")
                d.op("dve", (lambda n2=n2: (lambda e: e.tensor_sub(
                    row[:, 3, 0:n2], row[:, 1, 0:n2], row[:, 2, 0:n2])))(),
                    waits=[f"{pre}_ms_{ni}"], inc=f"{pre}_var_{ni}")
                d.op("dve", (lambda n2=n2: (lambda e: e.tensor_scalar_add(
                    row[:, 3, 0:n2], row[:, 3, 0:n2], 1e-5)))(),
                    waits=[f"{pre}_var_{ni}"], inc=f"{pre}_vare_{ni}")
                d.op("act", (lambda n2=n2: (lambda e: e.sqrt(
                    out=row[:, 3, 0:n2], in_=row[:, 3, 0:n2])))(),
                    waits=[f"{pre}_vare_{ni}"], inc=f"{pre}_sd_{ni}")
                d.op("dve", (lambda n2=n2: (lambda e: e.reciprocal(
                    row[:, 4, 0:n2], row[:, 3, 0:n2])))(),
                    waits=[f"{pre}_sd_{ni}"])
                d.op("dve", (lambda n2=n2: (lambda e: e.tensor_mul(
                    row[:, 5, 0:n2], row[:, 0, 0:n2], row[:, 4, 0:n2])))(),
                    inc=f"{pre}_ab_{ni}")
                d.op("te", (lambda n2=n2: (lambda e: e.matmul(
                    ps[:, 2048:2048 + n2], self.one1f[:, :], row[:, 4, 0:n2],
                    start=True, stop=True)))(),
                    waits=[f"{pre}_ab_{ni}"] +
                          ([f"{pre}_hnz_{ni-1}"] if ni >= 1 else []),
                    inc=f"{pre}_bca_{ni}")
                d.op("te", (lambda n2=n2: (lambda e: e.matmul(
                    ps[:, 2560:2560 + n2], self.one1f[:, :], row[:, 5, 0:n2],
                    start=True, stop=True)))(), inc=f"{pre}_bcb_{ni}")
                for ic in range(8):
                    ofs = 1 if ic < 4 else 0
                    w0 = [f"{pre}_bca_{ni}"] if ic == 0 else []
                    if ic >= 2:
                        w0.append(f"{pre}_hn_{ni}_{ic-2}")
                    elif ni >= 1:
                        w0.append(f"{pre}_hn_{ni-1}_{6+ic}")
                    d.op("dve", (lambda ic=ic, ofs=ofs, o2=o2, n2=n2: (lambda e: e.tensor_mul(
                        tmf[:, ic % 2, 0:n2], ubf2[:, ic, ofs + o2:ofs + o2 + n2],
                        ps[:, 2048:2048 + n2])))(),
                        waits=w0)
                    d.op("dve", (lambda ic=ic, n2=n2: (lambda e: e.tensor_sub(
                        tmf[:, ic % 2, 0:n2], tmf[:, ic % 2, 0:n2],
                        ps[:, 2560:2560 + n2])))(),
                        waits=[f"{pre}_bcb_{ni}"] if ic == 0 else (),
                        inc=f"{pre}_t2_{ni}_{ic}")
                    d.op("act", (lambda ic=ic, o2=o2, n2=n2: (lambda e: e.activation(
                        out=hn[:, ic, o2:o2 + n2], in_=tmf[:, ic % 2, 0:n2],
                        func=AF.Identity,
                        bias=lbc[:, ic:ic + 1], scale=lgc[:, ic:ic + 1])))(),
                        waits=[f"{pre}_t2_{ni}_{ic}", P + "consts"],
                        inc=f"{pre}_hn_{ni}_{ic}")
                d.mark(f"{pre}_hnz_{ni}", "ACT")
            d.mark(pre + "_lnz", "ACT")
            d.paranoid = force_para or ("w1" in p2tok)
            d.op("dve", lambda e: e.memset(h1[:, 21, :], 0.0), inc=pre + "_h1z")
            for k in range(22):
                M = 128 if k < 21 else TAIL
                cw = 256 if k < 21 else 84
                self.ls((lambda k=k, cw=cw, W=W: (lambda e: e.dma_start(
                    out=w1k[:, k % 2, :, 0:cw].rearrange("p a b -> p (a b)")
                    .rearrange("p (a b) -> p a b", a=8) if False else
                    w1k[:, k % 2, :, 0:cw],
                    in_=W["w1p"].rearrange("(c p) n -> p c n", p=128)[
                        :, :, k * 256:k * 256 + cw])))(),
                    waits=[f"{pre}_w1mm_{k-2}"] if k >= 2 else (),
                    inc=f"{pre}_w1l_{k}")
                for ni, (o2, n2) in enumerate(n2list):
                    w = [f"{pre}_w1l_{k}", pre + "_lnz"]
                    if k >= 1 or ni >= 1:
                        pk, pn = (k, ni - 1) if ni >= 1 else (k - 1, len(n2list) - 1)
                        w.append(f"{pre}_g1_{pk}_{pn}")
                    for ic in range(8):
                        d.op("te", (lambda k=k, ic=ic, o2=o2, n2=n2, M=M:
                                    (lambda e: e.matmul(
                                        ps[0:M, 0:n2], w1k[:, k % 2, ic, 0:M],
                                        hn[:, ic, o2:o2 + n2],
                                        start=(ic == 0), stop=(ic == 7))))(),
                            waits=w if ic == 0 else ())
                    for ic in range(8):
                        d.op("te", (lambda k=k, ic=ic, o2=o2, n2=n2, M=M, cw=cw:
                                    (lambda e: e.matmul(
                                        ps[0:M, 512:512 + n2],
                                        w1k[:, k % 2, ic, cw // 2:cw // 2 + M],
                                        hn[:, ic, o2:o2 + n2],
                                        start=(ic == 0), stop=(ic == 7))))(),
                            inc=f"{pre}_gmm_{k}_{ni}" if ic == 7 else None)
                    d.val.setdefault(f"{pre}_w1mm_{k}", d.val[f"{pre}_gmm_{k}_{ni}"])
                    d.val[f"{pre}_w1mm_{k}"] = d.val[f"{pre}_gmm_{k}_{ni}"]
                    d.op("act", (lambda k=k, ni=ni, n2=n2, M=M: (lambda e: e.activation(
                        out=gel[0:M, ni % 2, 0:n2], in_=ps[0:M, 512:512 + n2],
                        func=AF.Gelu, bias=b1pc[0:M, 2 * k + 1:2 * k + 2])))(),
                        waits=[f"{pre}_gmm_{k}_{ni}", P + "consts"] +
                              ([f"{pre}_g1_{k}_{ni-1}" if ni >= 1 else
                                (f"{pre}_g1_{k-1}_{len(n2list)-1}" if k >= 1 else
                                 pre + "_h1z")]),
                        inc=f"{pre}_gg_{k}_{ni}")
                    d.op("dve", (lambda k=k, ni=ni, o2=o2, n2=n2, M=M:
                                 (lambda e: e.scalar_tensor_tensor(
                                     h1[0:M, k, o2:o2 + n2], ps[0:M, 0:n2],
                                     b1pc[0:M, 2 * k:2 * k + 1], gel[0:M, ni % 2, 0:n2],
                                     ALU.add, ALU.mult)))(),
                        waits=[f"{pre}_gg_{k}_{ni}", pre + "_h1z"],
                        inc=f"{pre}_g1_{k}_{ni}")
            d.paranoid = force_para or ("w2" in p2tok)
            for k in range(22):
                self.ls((lambda k=k, W=W: (lambda e: e.dma_start(
                    out=w2res[:, k, :],
                    in_=W["w2p"].rearrange("(c p) n -> p c n", p=128)[:, k, :])))(),
                    inc=f"{pre}_w2l_{k}" if k == 21 else None)
            for ni, (o2, n2) in enumerate(n2list):
                nvl = min(512, TB - (us0 + o2))
                self.ls((lambda us0=us0, o2=o2, nvl=nvl: (lambda e: e.dma_start(
                    out=vmrow[:, 0:nvl],
                    in_=self.vmaskB[:, us0 + o2:us0 + o2 + nvl])))(),
                    waits=[f"{pre}_vmm_{ni-1}"] if ni >= 1 else (),
                    inc=f"{pre}_vml_{ni}")
                d.op("te", (lambda n2=n2: (lambda e: e.matmul(
                    ps[:, 3072:3072 + n2], self.one1[:, :],
                    vmrow[:, 0:n2], start=True, stop=True)))(),
                    waits=[f"{pre}_g1_21_{len(n2list)-1}", f"{pre}_vml_{ni}"] +
                          ([f"{pre}_vmc_{ni-1}"] if ni >= 1 else []),
                    inc=f"{pre}_vmm_{ni}")
                d.op("dve", (lambda n2=n2: (lambda e: e.tensor_copy(
                    vmb[:, 0:n2], ps[:, 3072:3072 + n2])))(),
                    waits=[f"{pre}_vmm_{ni}"], inc=f"{pre}_vmc_{ni}")
                for oc in range(8):
                    self.ls((lambda oc=oc, us0=us0, o2=o2, n2=n2: (lambda e: e.dma_start(
                        out=basef[:, oc % 2, 0:n2],
                        in_=SRC2[oc * 128:(oc + 1) * 128,
                                 us0 + o2:us0 + o2 + n2])))(),
                        inc=f"{pre}_bs_{ni}_{oc}")
                    w = [f"{pre}_w2l_21", f"{pre}_vmc_{ni}"]
                    if oc >= 2 or ni >= 1:
                        po, pn = (oc - 2, ni) if oc >= 2 else (6 + oc, ni - 1)
                        w.append(f"{pre}_w2s_{pn}_{po}")
                    for k in range(22):
                        d.op("te", (lambda k=k, oc=oc, o2=o2, n2=n2: (lambda e: e.matmul(
                            ps[:, (oc % 2) * 512:(oc % 2) * 512 + n2],
                            w2res[:, k, oc * 128:(oc + 1) * 128],
                            h1[:, k, o2:o2 + n2],
                            start=(k == 0), stop=(k == 21))))(),
                            waits=w if k == 0 else (),
                            inc=f"{pre}_w2m_{ni}_{oc}" if k == 21 else None)
                    d.op("dve", (lambda oc=oc, ni=ni, n2=n2: (lambda e: e.scalar_tensor_tensor(
                        twf[:, oc % 2, 0:n2], ps[:, (oc % 2) * 512:(oc % 2) * 512 + n2],
                        b2c[:, oc:oc + 1], vmb[:, 0:n2],
                        ALU.add, ALU.mult)))(),
                        waits=[f"{pre}_w2m_{ni}_{oc}"], inc=f"{pre}_w2s_{ni}_{oc}")
                    d.op("dve", (lambda oc=oc, ni=ni, n2=n2: (lambda e: e.tensor_add(
                        basef[:, oc % 2, 0:n2], twf[:, oc % 2, 0:n2],
                        basef[:, oc % 2, 0:n2])))(),
                        waits=[f"{pre}_bs_{ni}_{oc}"], inc=f"{pre}_uadd_{ni}_{oc}")
                    self.sp((lambda oc=oc, ni=ni, us0=us0, o2=o2, n2=n2:
                             (lambda e: e.dma_start(
                                 out=DST2[oc * 128:(oc + 1) * 128,
                                          us0 + o2:us0 + o2 + n2],
                                 in_=basef[:, oc % 2, 0:n2])))(),
                        waits=[f"{pre}_uadd_{ni}_{oc}"])
            d.paranoid = False
            d.phase_barrier()

    def _p7(self):
        d, abf, af = self.d, self.abf, self.af
        ps = self.ps
        yb = af.alloc([128, 8, 512])
        ub = af.alloc([128, 8, 512])
        xabf = abf.alloc([128, 2, 512])
        sqb = abf.alloc([128, 2, 512])
        row = af.alloc([1, 6, 512])
        for t in range(4):
            pre = f"p7t{t}"
            for ic in range(8):
                self.ls((lambda ic=ic, t=t: (lambda e: e.dma_start(
                    out=yb[:, ic, :],
                    in_=self.ysp[ic * 128:(ic + 1) * 128,
                                 HALO + t * 512:HALO + (t + 1) * 512])))(),
                    waits=[f"p7t{t-1}_spz"] if (t >= 1 and ic == 0) else ())
                self.ls((lambda ic=ic, t=t: (lambda e: e.dma_start(
                    out=ub[:, ic, :],
                    in_=self.u2sp[ic * 128:(ic + 1) * 128,
                                  1026 + t * 512:1026 + (t + 1) * 512])))())
            d.mark(pre + "_ld", "LS")
            for ic in range(8):
                d.op("dve", (lambda ic=ic: (lambda e: e.tensor_add(
                    yb[:, ic, :], yb[:, ic, :], ub[:, ic, :])))(),
                    waits=[pre + "_ld"] if ic == 0 else (), inc=f"{pre}_w_{ic}")
                d.op("act", (lambda ic=ic: (lambda e: e.copy(
                    out=xabf[:, ic % 2, :], in_=yb[:, ic, :])))(),
                    waits=[f"{pre}_w_{ic}"] +
                          ([f"{pre}_st_{ic-2}"] if ic >= 2 else
                           ([f"p7t{t-1}_st_{6+ic}"] if t >= 1 else [])),
                    inc=f"{pre}_cb_{ic}")
                d.op("act", (lambda ic=ic: (lambda e: e.activation(
                    out=sqb[:, ic % 2, :], in_=yb[:, ic, :], func=AF.Square)))(),
                    inc=f"{pre}_sb_{ic}")
                d.op("te", (lambda ic=ic, t=t: (lambda e: e.matmul(
                    ps[0:1, 1024:1536], self.o128b[:, :], xabf[:, ic % 2, :],
                    start=(ic == 0), stop=(ic == 7))))(),
                    waits=[f"{pre}_cb_{ic}"] +
                          ([f"p7t{t-1}_ms"] if (t >= 1 and ic == 0) else []))
                d.op("te", (lambda ic=ic: (lambda e: e.matmul(
                    ps[0:1, 1536:2048], self.o128b[:, :], sqb[:, ic % 2, :],
                    start=(ic == 0), stop=(ic == 7))))(),
                    waits=[f"{pre}_sb_{ic}"], inc=f"{pre}_st_{ic}")
            d.op("act", (lambda t=t: (lambda e: e.mul(
                out=row[:, 0, :], in_=ps[0:1, 1024:1536], mul=1.0 / 1024)))(),
                waits=[f"{pre}_st_7"] + ([f"p7t{t-1}_ab"] if t >= 1 else []),
                inc=f"{pre}_mu")
            d.op("act", lambda e: e.mul(out=row[:, 1, :], in_=ps[0:1, 1536:2048],
                                        mul=1.0 / 1024))
            d.op("act", lambda e: e.activation(out=row[:, 2, :], in_=row[:, 0, :],
                                               func=AF.Square), inc=f"{pre}_ms")
            d.op("dve", lambda e: e.tensor_sub(row[:, 3, :], row[:, 1, :], row[:, 2, :]),
                 waits=[f"{pre}_ms"], inc=f"{pre}_var")
            d.op("dve", lambda e: e.tensor_scalar_add(row[:, 3, :], row[:, 3, :], 1e-5),
                 waits=[f"{pre}_var"], inc=f"{pre}_vare")
            d.op("act", lambda e: e.sqrt(out=row[:, 3, :], in_=row[:, 3, :]),
                 waits=[f"{pre}_vare"], inc=f"{pre}_sd")
            d.op("dve", lambda e: e.reciprocal(row[:, 4, :], row[:, 3, :]),
                 waits=[f"{pre}_sd"])
            d.op("dve", lambda e: e.tensor_mul(row[:, 5, :], row[:, 0, :], row[:, 4, :]),
                 inc=f"{pre}_ab")
            d.op("te", (lambda t=t: (lambda e: e.matmul(
                ps[:, 2048:2560], self.one1f[:, :], row[:, 4, :],
                start=True, stop=True)))(),
                waits=[f"{pre}_ab"] + ([f"p7t{t-1}_yz"] if t >= 1 else []),
                inc=f"{pre}_bca")
            d.op("te", (lambda t=t: (lambda e: e.matmul(
                ps[:, 2560:3072], self.one1f[:, :], row[:, 5, :],
                start=True, stop=True)))(), inc=f"{pre}_bcb")
            for ic in range(8):
                d.op("dve", (lambda ic=ic: (lambda e: e.tensor_mul(
                    yb[:, ic, :], yb[:, ic, :], ps[:, 2048:2560])))(),
                    waits=[f"{pre}_bca"] if ic == 0 else ())
                d.op("dve", (lambda ic=ic: (lambda e: e.tensor_sub(
                    yb[:, ic, :], yb[:, ic, :], ps[:, 2560:3072])))(),
                    waits=[f"{pre}_bcb"] if ic == 0 else (),
                    inc=f"{pre}_y_{ic}")
            d.mark(pre + "_yz", "DVE")
            for ic in range(8):
                self.sp((lambda ic=ic, t=t: (lambda e: e.dma_start(
                    out=self.out[ic * 128:(ic + 1) * 128, t * 512:(t + 1) * 512],
                    in_=yb[:, ic, :])))(),
                    waits=[f"{pre}_y_{ic}"])
            d.mark(pre + "_spz", "SP")


def kernel(**inputs):
    maps = host_prepare(inputs)
    b = Builder()
    nc = b.build()
    res = run_bass_kernel_spmd(nc, maps, core_ids=list(range(NCORES)))
    full = np.zeros((2, 8192, 1024), np.float32)
    for c in range(NCORES):
        bb, j = c // 4, c % 4
        full[bb, j * L:(j + 1) * L, :] = np.asarray(res.results[c]["out"]).T
    return full
